# version 15
# baseline (speedup 1.0000x reference)
import numpy as np
import ml_dtypes
from contextlib import ExitStack

import concourse.bass as bass
import concourse.mybir as mybir
from concourse.bass_utils import run_bass_kernel_spmd

F32 = mybir.dt.float32
BF16 = mybir.dt.bfloat16
AF = mybir.ActivationFunctionType
ALU = mybir.AluOpType
AX = mybir.AxisListType

S = 4096
D = 1024
DI = 2816
NT = S // 128
ND = D // 128
NI = DI // 128
G = 4
CRND = float(np.float32(1.5 * 2 ** 23))
UNITS = ([("g", n) for n in range(22)] + [("u", n) for n in range(22)] +
         [("d", dc, ss) for dc in range(8) for ss in range(3)])
NEGBIG = -320000.0
BF = ml_dtypes.bfloat16


class Prog:
    ENGS = ("sp", "pe", "dve", "act", "gp")

    def __init__(self, nc, ctx):
        self.nc = nc
        self.sems = {}
        for ek in self.ENGS:
            self.sems[ek] = ctx.enter_context(nc.semaphore("sem_" + ek))
        for ek in ("sp", "gp"):
            self.sems[ek + ".d"] = ctx.enter_context(
                nc.semaphore("sem_" + ek + "d"))
        self.barsem = ctx.enter_context(nc.semaphore("sem_bar"))
        self.cnt = dict.fromkeys(self.sems, 0)
        self.barcnt = 0
        self.rec = {ek: [] for ek in self.ENGS}

    def op(self, ek, fn, waits=(), dma=False, inc=True, strict=False,
           drain=False):
        key = ek + ".d" if dma else ek
        ws = []
        for t in waits:
            if t is None:
                continue
            if t[0] == ek and not strict:
                continue
            ws.append(t)
        self.rec[ek].append((fn, ws, key, dma, inc, drain))
        if inc:
            self.cnt[key] += 16 if dma else 1
            return (key, self.cnt[key])
        return None

    def barrier(self):
        self.barcnt += len(self.ENGS)
        tgt = self.barcnt
        for ek in self.ENGS:
            self.rec[ek].append(("bar", tgt, None, None, None, None))

    def token(self, key):
        return (key, self.cnt[key])

    def replay(self, ek):
        sems = self.sems

        def go(eng):
            waited = {}
            for entry in self.rec[ek]:
                if entry[0] == "bar":
                    eng.sem_inc(self.barsem, 1)
                    eng.wait_ge(self.barsem, entry[1])
                    continue
                fn, ws, key, dmaf, inc, drn = entry
                for (sk, val) in ws:
                    if waited.get(sk, 0) < val:
                        eng.wait_ge(sems[sk], val)
                        waited[sk] = val
                ins = fn(eng)
                if drn:
                    ins = eng.drain()
                if inc:
                    ins.then_inc(sems[key], 16 if dmaf else 1)
        return go


def build_program(taps=()):
    nc = bass.Bass("TRN2", target_bir_lowering=False)
    taps = set(taps)

    x_d = nc.dram_tensor("x", [S, D], F32, kind="ExternalInput").ap()
    wq_d = nc.dram_tensor("wq", [D, D], F32, kind="ExternalInput").ap()
    wk_d = nc.dram_tensor("wk", [D, D], F32, kind="ExternalInput").ap()
    wv_d = nc.dram_tensor("wv", [D, D], F32, kind="ExternalInput").ap()
    wo_d = nc.dram_tensor("wo", [D, D], F32, kind="ExternalInput").ap()
    bqk_d = nc.dram_tensor("bqk", [128, 16], F32, kind="ExternalInput").ap()
    bv_d = nc.dram_tensor("bv", [1, D], F32, kind="ExternalInput").ap()
    bo_d = nc.dram_tensor("bo", [1, D], F32, kind="ExternalInput").ap()
    wg_d = nc.dram_tensor("wg", [DI, D], F32, kind="ExternalInput").ap()
    wu_d = nc.dram_tensor("wu", [DI, D], F32, kind="ExternalInput").ap()
    wd_d = nc.dram_tensor("wd", [D, DI], F32, kind="ExternalInput").ap()
    out_d = nc.dram_tensor("out", [S, D], F32, kind="ExternalOutput").ap()

    qt_s = nc.dram_tensor("qt_s", [NT, ND, 128, 128], BF16, kind="Internal").ap()
    xqt_s = nc.dram_tensor("xqt_s", [NT, ND, 128, 128], BF16, kind="Internal").ap()
    xnt_s = nc.dram_tensor("xnt_s", [NT, ND, 128, 128], BF16, kind="Internal").ap()
    ao_s = nc.dram_tensor("ao_s", [NT, 128, D], BF16, kind="Internal").ap()
    wqall = nc.dram_tensor("wqall", [68, 8 * 128 * 128], BF16,
                           kind="Internal").ap()

    ident_c = nc.inline_tensor(np.eye(128, dtype=BF), name="identc").ap()
    mrows = np.arange(128)[:, None]
    mcols = np.arange(128)[None, :]
    mask_np = np.where(mcols > mrows, 0.0, 1.0).astype(BF)
    mask_c = nc.inline_tensor(mask_np, name="maskc").ap()
    ones_c = nc.inline_tensor(np.ones((1, 128), dtype=BF), name="onesc").ap()

    tapout = {}

    def tap(name, shape, dt):
        tapout[name] = nc.dram_tensor("tap_" + name, shape, dt,
                                      kind="ExternalOutput").ap()
        return tapout[name]

    g = ExitStack()
    with g:
        P = Prog(nc, g)

        def sb(n, sh, dt):
            return g.enter_context(nc.sbuf_tensor(n, sh, dt))

        ident = sb("ident", [128, 128], BF16)
        mask = sb("mask", [128, 128], BF16)
        ones1 = sb("ones1", [1, 128], BF16)
        bqk = sb("bqk_t", [128, 16], F32)
        bvo = sb("bvo", [1, 2 * D], BF16)
        bvr = bvo[:, 0:D]
        bor = bvo[:, D:2 * D]
        xt = sb("xt", [128, 2 * D], F32)
        QW = 512
        qse = []
        for si in range(2):
            qse.append({
                "a": sb(f"q_a{si}", [128, QW], F32),
                "t1": sb(f"q_t1{si}", [128, QW], F32),
                "r1": sb(f"q_r1{si}", [128, QW], BF16),
                "hi": sb(f"q_hi{si}", [128, QW], BF16),
                "q2": sb(f"q_q2{si}", [128, QW], BF16),
                "sg": sb(f"q_sg{si}", [128, QW], BF16),
                "bmax": sb(f"q_bmax{si}", [128, 32], F32),
                "scl": sb(f"q_scl{si}", [128, 32], F32),
                "rcp": sb(f"q_rcp{si}", [128, 32], F32),
            })
        sm = sb("sm", [128, 16], F32)
        mx = sb("mx", [128, 8], F32)
        negmc = sb("negmc", [128, 8], F32)
        sums = sb("sums", [128, 8], F32)
        fe = sb("fe", [128, 8], F32)
        sums2 = sb("sums2", [128, 8], F32)
        kbias = sb("kbias", [128, 4], F32)

        def xts(sl):
            return xt[:, sl * D:(sl + 1) * D]

        t_ident = P.op("sp", lambda e: e.dma_start(out=ident[:], in_=ident_c),
                       dma=True)
        t_mask = P.op("sp", lambda e: e.dma_start(out=mask[:], in_=mask_c),
                      dma=True)
        t_ones = P.op("sp", lambda e: e.dma_start(out=ones1[:], in_=ones_c),
                      dma=True)
        t_bqk = P.op("sp", lambda e: e.dma_start(out=bqk[:], in_=bqk_d),
                     dma=True)
        t_bv = P.op("gp", lambda e: e.dma_start(out=bvr, in_=bv_d), dma=True)
        for j, val in enumerate((-5.0, -7.0, -10.0)):
            P.op("dve", lambda e, j=j, v=val: e.memset(kbias[:, j:j + 1], v),
                 drain=(j == 2))
        t_kb = P.token("dve")
        t_bo = P.op("gp", lambda e: e.dma_start(out=bor, in_=bo_d), dma=True)
        bqk32 = sb("bqk32", [128, 8], F32)
        t_b32 = P.op("dve", lambda e: e.tensor_scalar(
            bqk32[:], bqk[:, 0:8], 1.0 / 32.0, None, ALU.mult), [t_bqk],
            drain=True)

        last_q = [None, None]
        qctr = [0]

        def emit_quant(src_ap, dst_ap, w, waits, dst_war=()):
            si = qctr[0] % 2
            qctr[0] += 1
            T = qse[si]
            nb = w // 16
            src3 = src_ap.rearrange("p (a b) -> p a b", b=16)
            s2b3 = T["scl"][:, 0:nb].unsqueeze(2).broadcast_to([128, nb, 16])
            rcb3 = T["rcp"][:, 0:nb].unsqueeze(2).broadcast_to([128, nb, 16])
            a2 = T["a"][:, 0:w]
            a3 = a2.rearrange("p (a b) -> p a b", b=16)
            w0 = list(waits) + ([last_q[si]] if last_q[si] else [])
            tA = P.op("act", lambda e: e.activation(a2, src_ap, AF.Abs), w0)
            tSG = P.op("act", lambda e: e.sign(T["sg"][:, 0:w], src_ap), w0)
            tR = P.op("dve", lambda e: e.tensor_reduce(
                T["bmax"][:, 0:nb], src3, AX.X, ALU.max,
                apply_absolute_value=True), w0)
            tsc = P.op("dve", lambda e: e.tensor_scalar(
                T["scl"][:, 0:nb], T["bmax"][:, 0:nb], 1.0 / 12.0, 5e-13,
                ALU.mult, ALU.max), [tR], strict=True)
            trc = P.op("dve", lambda e: e.reciprocal(
                T["rcp"][:, 0:nb], T["scl"][:, 0:nb]), [tsc], strict=True)
            tan = P.op("dve", lambda e: e.tensor_tensor(a3, a3, rcb3, ALU.mult),
                       [tA, trc], strict=True)
            tu1 = P.op("dve", lambda e: e.tensor_scalar(
                T["t1"][:, 0:w], a2, 4.0, CRND, ALU.min, ALU.add), [tan])
            tr1 = P.op("dve", lambda e: e.tensor_scalar(
                T["r1"][:, 0:w], T["t1"][:, 0:w], CRND - 4.0, None,
                ALU.subtract), [tu1])
            th1 = P.op("act", lambda e: e.sign(
                T["hi"][:, 0:w], a2, bias=kbias[:, 0:1]), [tan, t_kb])
            th2 = P.op("act", lambda e: e.sign(
                T["q2"][:, 0:w], a2, bias=kbias[:, 1:2]), [th1])
            tt1 = P.op("dve", lambda e: e.tensor_tensor(
                T["hi"][:, 0:w], T["hi"][:, 0:w], T["q2"][:, 0:w], ALU.add),
                [th1, th2])
            th3 = P.op("act", lambda e: e.sign(
                T["q2"][:, 0:w], a2, bias=kbias[:, 2:3]), [th2, tt1])
            tt2 = P.op("dve", lambda e: e.tensor_tensor(
                T["r1"][:, 0:w], T["r1"][:, 0:w], T["hi"][:, 0:w], ALU.add),
                [tt1, tr1])
            tt3 = P.op("dve", lambda e: e.tensor_scalar(
                T["q2"][:, 0:w], T["q2"][:, 0:w], 2.0, None, ALU.mult), [th3])
            tt4 = P.op("dve", lambda e: e.tensor_tensor(
                T["r1"][:, 0:w], T["r1"][:, 0:w], T["q2"][:, 0:w], ALU.add),
                [tt3, tt2])
            q13 = T["r1"][:, 0:w].rearrange("p (a b) -> p a b", b=16)
            t13 = T["t1"][:, 0:w].rearrange("p (a b) -> p a b", b=16)
            tqs = P.op("dve", lambda e: e.tensor_tensor(
                t13, q13, s2b3, ALU.mult), [tt4, tsc], strict=True)
            tdst = P.op("gp", lambda e: e.tensor_tensor(
                dst_ap, T["t1"][:, 0:w], T["sg"][:, 0:w], ALU.mult),
                [tqs, tSG] + list(dst_war))
            last_q[si] = tdst
            return tdst, [tA, tSG, tR]

        def emit_quant_multi(src_ap, dst_ap, w, waits, dst_war=()):
            toks = []
            sfree = []
            o = 0
            while o < w:
                ww = min(512, w - o)
                t, sf = emit_quant(src_ap[:, o:o + ww], dst_ap[:, o:o + ww],
                                   ww, waits, dst_war)
                toks.append(t)
                sfree.extend(sf)
                o += ww
            return toks, sfree

        s12a = ExitStack()
        kT = s12a.enter_context(nc.sbuf_tensor("kT", [128, ND, S], BF16))

        with ExitStack() as s1:
            wq = s1.enter_context(nc.sbuf_tensor("wq_t", [128, ND, D], BF16))
            wk = s1.enter_context(nc.sbuf_tensor("wk_t", [128, ND, D], BF16))
            xn_f = s1.enter_context(nc.sbuf_tensor("xn_f", [128, D], F32))
            xn_b = s1.enter_context(nc.sbuf_tensor("xn_b", [128, D], BF16))
            xq_b = s1.enter_context(nc.sbuf_tensor("xq_b", [128, D], BF16))
            xnT = s1.enter_context(
                nc.sbuf_tensor("xnT", [128, ND, G * 128], BF16))
            qTb = s1.enter_context(
                nc.sbuf_tensor("qTb", [128, ND, G * 128], BF16))
            xqT = s1.enter_context(nc.sbuf_tensor("xqT", [128, ND * 128], BF16))
            p_tr = s1.enter_context(nc.psum_tensor("p_tr", [128, 2048], F32))
            p_q = s1.enter_context(nc.psum_tensor("p_q", [128, 1024], F32))
            p_k = s1.enter_context(nc.psum_tensor("p_k", [128, 1024], F32))

            t_wq = P.op("gp", lambda e: e.dma_start(
                out=wq[:], in_=wq_d.rearrange("(k p) n -> p k n", p=128)),
                dma=True)
            t_wk = P.op("gp", lambda e: e.dma_start(
                out=wk[:], in_=wk_d.rearrange("(k p) n -> p k n", p=128)),
                dma=True)

            xt_free = [[], []]
            xnT_free = [[] for _ in range(G)]
            xqT_free = []
            qTb_free = []
            kq_psum_free = [[], []]
            tr_free_xn = []
            tr_free_xq = []
            prev_ttr = None
            prev_ttq = None

            for i in range(NT):
                b, gi = divmod(i, G)
                sl = i % 2
                xs = xts(sl)
                tx = P.op("sp", lambda e, i=i, xs=xs: e.dma_start(
                    out=xs, in_=x_d[i * 128:(i + 1) * 128, :]),
                    xt_free[sl], dma=True)
                xt_free[sl] = []
                tsum = P.op("dve", lambda e, xs=xs: e.reduce_sum(
                    sm[:, 0:1], xs, AX.X), [tx])
                tssq = P.op("act", lambda e, xs=xs: e.activation(
                    xn_f[:], xs, AF.Square, accum_out=sm[:, 1:2]),
                    [tx, last_q[0], last_q[1]])
                tmean = P.op("dve", lambda e: e.tensor_scalar(
                    sm[:, 2:3], sm[:, 0:1], 1.0 / D, None, ALU.mult), [tsum],
                    strict=True)
                tm2 = P.op("dve", lambda e: e.tensor_scalar(
                    sm[:, 10:11], sm[:, 2:3], sm[:, 2:3], -1e-5,
                    ALU.mult, ALU.add), [tmean], strict=True)
                tvar = P.op("dve", lambda e: e.scalar_tensor_tensor(
                    sm[:, 3:4], sm[:, 1:2], 1.0 / D, sm[:, 10:11],
                    ALU.mult, ALU.subtract), [tssq, tm2], strict=True)
                tsd = P.op("act", lambda e: e.activation(
                    sm[:, 4:5], sm[:, 3:4], AF.Ln), [tvar])
                trstd = P.op("act", lambda e: e.activation(
                    sm[:, 5:6], sm[:, 4:5], AF.Exp, scale=-0.5), [tsd],
                    strict=True, drain=True)
                txn = P.op("dve", lambda e, xs=xs: e.tensor_scalar(
                    xn_f[:], xs, sm[:, 2:3], sm[:, 5:6],
                    ALU.subtract, ALU.mult), [trstd, last_q[0], last_q[1]])
                xt_free[sl].append(txn)
                txnb = P.op("act", lambda e: e.activation(
                    xn_b[:], xn_f[:], AF.Copy), [txn, prev_ttr])
                txq_l, txq_sf = emit_quant_multi(
                    xn_f[:], xq_b[:], D, [txn, txnb],
                    dst_war=[prev_ttq] if prev_ttq else ())
                txq = txq_l[-1]
                if "xn0" in taps and i == 0:
                    tp = tap("xn0", [128, D], F32)
                    P.op("sp", lambda e, tp=tp: e.dma_start(out=tp, in_=xn_f[:]),
                         [txn, txq], dma=True)
                if "xq0" in taps and i == 0:
                    tp = tap("xq0", [128, D], BF16)
                    P.op("sp", lambda e, tp=tp: e.dma_start(out=tp, in_=xq_b[:]),
                         [txq], dma=True)
                for dc in range(ND):
                    prev_ttr = P.op("pe", lambda e, dc=dc: e.matmul(
                        p_tr[:, dc * 128:(dc + 1) * 128],
                        xn_b[:, dc * 128:(dc + 1) * 128], ident[:],
                        start=True, stop=True),
                        ([txnb, t_ident] + tr_free_xn) if dc == 0 else [],
                        inc=(dc == ND - 1))
                tr_free_xn = []
                for dc in range(ND):
                    prev_ttq = P.op("pe", lambda e, dc=dc: e.matmul(
                        p_tr[:, 1024 + dc * 128:1024 + (dc + 1) * 128],
                        xq_b[:, dc * 128:(dc + 1) * 128], ident[:],
                        start=True, stop=True),
                        (txq_l + tr_free_xq) if dc == 0 else [],
                        inc=(dc == ND - 1))
                tr_free_xq = []
                xnT_dst = xnT[:, :, gi * 128:(gi + 1) * 128]
                tevn = P.op("act", lambda e, d=xnT_dst: e.activation(
                    d, p_tr[:, 0:1024].rearrange("p (c j) -> p c j", j=128),
                    AF.Copy), [prev_ttr] + xnT_free[gi])
                xnT_free[gi] = []
                tr_free_xn.append(tevn)
                tevq = P.op("act", lambda e: e.activation(
                    xqT[:], p_tr[:, 1024:2048], AF.Copy),
                    [prev_ttq] + xqT_free)
                xqT_free = []
                tr_free_xq.append(tevq)
                tso = P.op("sp", lambda e, i=i: e.dma_start(
                    out=xqt_s[i].rearrange("c p j -> p c j"),
                    in_=xqT[:].rearrange("p (c j) -> p c j", j=128)),
                    [tevq], dma=True)
                xqT_free.append(tso)

                if gi == G - 1:
                    for gg in range(G):
                        tsn = P.op("sp", lambda e, b=b, gg=gg: e.dma_start(
                            out=xnt_s[b * G + gg].rearrange("c p j -> p c j"),
                            in_=xnT[:, :, gg * 128:(gg + 1) * 128]),
                            [tevn], dma=True)
                        xnT_free[gg].append(tsn)
                    for proj in range(2):
                        wmat = wq if proj == 0 else wk
                        twm = t_wq if proj == 0 else t_wk
                        pp = p_q if proj == 0 else p_k
                        for do in range(ND):
                            par = do % 2
                            pps = pp[:, par * 512:(par + 1) * 512]
                            tmm = None
                            for di in range(ND):
                                tmm = P.op("pe", lambda e, di=di, do=do,
                                           w=wmat, pp2=pps: e.matmul(
                                    pp2, w[:, di, do * 128:(do + 1) * 128],
                                    xnT[:, di, :],
                                    start=(di == 0), stop=(di == ND - 1)),
                                    ([tevn, twm] + kq_psum_free[par])
                                    if di == 0 else [], inc=(di == ND - 1))
                            kq_psum_free[par] = []
                            if proj == 0:
                                dst = qTb[:, do, :]
                                tev = P.op("act", lambda e, d=dst, pp2=pps,
                                           bc=do: e.activation(
                                    d, pp2, AF.Relu, bias=bqk32[:, bc:bc + 1],
                                    scale=1.0 / 32.0),
                                    [tmm, t_b32] + qTb_free)
                            else:
                                dst = kT[:, do, b * 512:(b + 1) * 512]
                                tev = P.op("act", lambda e, d=dst, pp2=pps,
                                           bc=8 + do: e.activation(
                                    d, pp2, AF.Relu, bias=bqk[:, bc:bc + 1]),
                                    [tmm, t_bqk])
                            if proj == 0:
                                qTb_free = []
                            kq_psum_free[par].append(tev)
                    tact_b = P.token("act")
                    for gg in range(G):
                        tqs_st = P.op("sp", lambda e, b=b, gg=gg: e.dma_start(
                            out=qt_s[b * G + gg].rearrange("c p j -> p c j"),
                            in_=qTb[:, :, gg * 128:(gg + 1) * 128]),
                            [tact_b], dma=True)
                        qTb_free.append(tqs_st)
                    tpe_batch = P.token("pe")
                    for gg in range(G):
                        xnT_free[gg].append(tpe_batch)

            if "kT" in taps:
                tp = tap("kT", [128, ND * S], BF16)
                P.op("sp", lambda e, tp=tp: e.dma_start(
                    out=tp, in_=kT[:].rearrange("p c s -> p (c s)")),
                    [P.token("act")], dma=True)

        P.barrier()

        s12b = ExitStack()
        vS = s12b.enter_context(nc.sbuf_tensor("vS", [128, NT * D], BF16))

        with ExitStack() as s1b:
            wv = s1b.enter_context(nc.sbuf_tensor("wv_t", [128, ND, D], BF16))
            xnT_in = s1b.enter_context(
                nc.sbuf_tensor("xnT_in", [128, 2, ND, 128], BF16))
            p_v = s1b.enter_context(nc.psum_tensor("p_v", [128, 2048], F32))

            t_wv = P.op("gp", lambda e: e.dma_start(
                out=wv[:], in_=wv_d.rearrange("(k p) n -> p k n", p=128)),
                dma=True)
            xnTin_free = [[], []]
            pv_free = [[], []]
            for i in range(NT):
                sl = i % 2
                tld = P.op("sp", lambda e, i=i, sl=sl: e.dma_start(
                    out=xnT_in[:, sl],
                    in_=xnt_s[i].rearrange("c p j -> p c j")),
                    xnTin_free[sl], dma=True)
                xnTin_free[sl] = []
                pvh = p_v[:, sl * 1024:(sl + 1) * 1024]
                tmm = None
                for half in range(2):
                    ph = pvh[:, half * 512:(half + 1) * 512]
                    for di in range(ND):
                        P.op("pe", lambda e, di=di, sl=sl, half=half, ph=ph:
                             e.matmul(
                            ph, xnT_in[:, sl, di, :],
                            wv[:, di, half * 512:(half + 1) * 512],
                            start=(di == 0), stop=False),
                            ([tld, t_wv, t_ones, t_bv] + pv_free[sl])
                            if (di == 0 and half == 0) else [], inc=False)
                    tmm = P.op("pe", lambda e, half=half, ph=ph: e.matmul(
                        ph, ones1[:], bvr[:, half * 512:(half + 1) * 512],
                        start=False, stop=True), [])
                pv_free[sl] = []
                tev = P.op("act", lambda e, i=i, pv2=pvh: e.activation(
                    vS[:, i * D:(i + 1) * D], pv2, AF.Relu), [tmm])
                pv_free[sl].append(tev)
                xnTin_free[sl].append(tev)

            if "V" in taps:
                tp = tap("V", [128, NT * D], BF16)
                P.op("sp", lambda e, tp=tp: e.dma_start(out=tp, in_=vS[:]),
                     [P.token("act")], dma=True)

        P.barrier()

        with ExitStack() as s2:
            wo = s2.enter_context(nc.sbuf_tensor("wo_t", [128, ND, D], BF16))
            qT = s2.enter_context(
                nc.sbuf_tensor("qT_t", [128, 2, ND, 128], BF16))
            Psb = s2.enter_context(nc.sbuf_tensor("Psb", [128, 2 * S], BF16))
            PTsb = s2.enter_context(nc.sbuf_tensor("PTsb", [128, 2, 512], BF16))
            yb = s2.enter_context(nc.sbuf_tensor("yb", [128, D], BF16))
            yT = s2.enter_context(nc.sbuf_tensor("yT", [128, D], BF16))
            aosb = s2.enter_context(nc.sbuf_tensor("aosb", [128, D], BF16))
            wqnat = s2.enter_context(nc.sbuf_tensor("wqnat", [128, 2, D], BF16))
            wbounce = s2.enter_context(nc.sbuf_tensor("wbounce", [128, D], BF16))
            p_s = s2.enter_context(nc.psum_tensor("p_s", [128, 1024], F32))
            p_y = s2.enter_context(nc.psum_tensor("p_y", [128, 1024], F32))
            p_pt = s2.enter_context(nc.psum_tensor("p_pt", [128, 1024], F32))
            p_ao = s2.enter_context(nc.psum_tensor("p_ao", [128, 512], F32))
            p_wp = s2.enter_context(nc.psum_tensor("p_wp", [128, 512], F32))

            t_wo = P.op("gp", lambda e: e.dma_start(
                out=wo[:], in_=wo_d.rearrange("(k p) n -> p k n", p=128)),
                dma=True)

            unit_idx = [0]
            prep_xt_free = [[], []]
            prep_psum_free = []
            prep_bounce_free = []
            prep_wq_war = [[] for _ in range(2)]
            prep_done = []
            prep_q = []

            def emit_prep_quant():
                k = unit_idx[0]
                if k >= len(UNITS):
                    return
                unit_idx[0] += 1
                unit = UNITS[k]
                if unit[0] == "g":
                    src_ap = wg_d[unit[1] * 128:(unit[1] + 1) * 128, :]
                    wsrc = D
                elif unit[0] == "u":
                    src_ap = wu_d[unit[1] * 128:(unit[1] + 1) * 128, :]
                    wsrc = D
                else:
                    _, dcn, ssn = unit
                    wsrc = 768 if ssn == 2 else 1024
                    src_ap = wd_d[dcn * 128:(dcn + 1) * 128,
                                  ssn * 1024:ssn * 1024 + wsrc]
                sl = k % 2
                buf = k % 2
                stg = xts(sl)
                tld = P.op("sp", lambda e, d=stg[:, 0:wsrc], s=src_ap:
                           e.dma_start(out=d, in_=s),
                           prep_xt_free[sl], dma=True)
                prep_xt_free[sl] = []
                tq_l, tq_sf = emit_quant_multi(stg[:, 0:wsrc],
                                               wqnat[:, buf, 0:wsrc], wsrc,
                                               [tld],
                                               dst_war=list(prep_wq_war[buf]))
                prep_wq_war[buf] = []
                prep_xt_free[sl].extend(tq_sf)
                prep_q.append(dict(k=k, buf=buf, tq=tq_l, wsrc=wsrc))

            def emit_prep_store():
                if not prep_q:
                    return
                d = prep_q.pop(0)
                k, buf, wsrc = d["k"], d["buf"], d["wsrc"]
                ntr = wsrc // 128
                tev_last = None
                for grp in range((ntr + 3) // 4):
                    j0 = grp * 4
                    j1 = min(ntr, j0 + 4)
                    tt = None
                    for j in range(j0, j1):
                        tt = P.op("pe", lambda e, j=j, j0=j0, buf=buf:
                                  e.matmul(
                            p_wp[:, (j - j0) * 128:(j - j0 + 1) * 128],
                            wqnat[:, buf, j * 128:(j + 1) * 128], ident[:],
                            start=True, stop=True),
                            (d["tq"] + prep_psum_free) if j == j0 else [],
                            inc=(j == j1 - 1))
                    prep_psum_free.clear()
                    tev_last = P.op("act", lambda e, j0=j0, j1=j1:
                                    e.activation(
                        wbounce[:, j0 * 128:j1 * 128],
                        p_wp[:, 0:(j1 - j0) * 128], AF.Copy),
                        [tt] + prep_bounce_free)
                    prep_bounce_free.clear()
                    prep_psum_free.append(tev_last)
                prep_wq_war[buf].append(P.token("pe"))
                tst = P.op("sp", lambda e, k=k, ntr=ntr: e.dma_start(
                    out=wqall[k].rearrange("(c p j) -> p c j", p=128,
                                           j=128)[:, 0:ntr, :],
                    in_=wbounce[:, 0:wsrc].rearrange("p (c j) -> p c j",
                                                     j=128)),
                    [tev_last], dma=True)
                prep_bounce_free.append(tst)
                prep_done.append(tst)

            qT_free = [[], []]
            ps_free = [[], []]
            psb_war = {}
            PT_free = [[], []]
            ppt_free = [[], []]
            py_free = []
            yb_free = []
            yT_free = []
            ao_free = []
            trs_prev = None
            saved = {}

            for it in range(NT + 1):
                if it < NT:
                    i = it
                    L = 128 * (i + 1)
                    nkc = (L + 511) // 512
                    si = i % 2
                    off = si * S
                    rscol = 9 + 3 * si
                    tqld = P.op("sp", lambda e, i=i, si=si: e.dma_start(
                        out=qT[:, si],
                        in_=qt_s[i].rearrange("c p j -> p c j")),
                        qT_free[si], dma=True)
                    qT_free[si] = []
                    texp = {}
                    tmm = None
                    for kc in range(nkc):
                        w = min(512, L - kc * 512)
                        par = kc % 2
                        psl = p_s[:, par * 512:par * 512 + w]
                        for dc in range(ND):
                            tmm = P.op("pe", lambda e, dc=dc, kc=kc, w=w,
                                       psl=psl, si=si: e.matmul(
                                psl, qT[:, si, dc, :],
                                kT[:, dc, kc * 512:kc * 512 + w],
                                start=(dc == 0), stop=(dc == ND - 1)),
                                ([tqld] + ps_free[par]) if dc == 0 else [],
                                inc=(dc == ND - 1))
                        ps_free[par] = []
                        tmx = P.op("dve", lambda e, kc=kc, psl=psl:
                                   e.reduce_max(
                            negmc[:, kc:kc + 1], psl, AX.X, negate=True),
                            [tmm], drain=True)
                        tex = P.op("act", lambda e, kc=kc, w=w, psl=psl,
                                   off=off: e.activation(
                            Psb[:, off + kc * 512:off + kc * 512 + w], psl,
                            AF.Exp, bias=negmc[:, kc:kc + 1], scale=1.0,
                            accum_out=sums[:, kc:kc + 1]),
                            [tmx, trs_prev] + psb_war.pop((si, kc), []))
                        if kc == nkc - 1:
                            tz = P.op("dve", lambda e, kc=kc, w=w, off=off:
                                      e.tensor_tensor(
                                Psb[:, off + kc * 512 + w - 128:
                                    off + kc * 512 + w],
                                Psb[:, off + kc * 512 + w - 128:
                                    off + kc * 512 + w],
                                mask[:], ALU.mult), [tex, t_mask])
                            tex = P.op("dve", lambda e, kc=kc, w=w, off=off:
                                       e.reduce_sum(
                                sums[:, kc:kc + 1],
                                Psb[:, off + kc * 512:off + kc * 512 + w],
                                AX.X), [tz], strict=True)
                        texp[kc] = tex
                        ps_free[par].append(tex)
                    qT_free[si] = [tmm]
                    tm1 = P.op("dve", lambda e, nkc=nkc: e.tensor_reduce(
                        sm[:, 7:8], negmc[:, 0:nkc], AX.X, ALU.min),
                        [texp[nkc - 1]], strict=True)
                    tfe = P.op("dve", lambda e, nkc=nkc: e.tensor_scalar(
                        fe[:, 0:nkc], negmc[:, 0:nkc], sm[:, 7:8], -1.0,
                        ALU.subtract, ALU.mult), [tm1], strict=True)
                    tfee = P.op("act", lambda e, nkc=nkc: e.activation(
                        fe[:, 0:nkc], fe[:, 0:nkc], AF.Exp), [tfe],
                        drain=True)
                    tsm2 = P.op("dve", lambda e, nkc=nkc: e.tensor_tensor(
                        sums2[:, 0:nkc], sums[:, 0:nkc], fe[:, 0:nkc],
                        ALU.mult), [tfee], strict=True)
                    tss = P.op("dve", lambda e, nkc=nkc: e.reduce_sum(
                        sm[:, 8:9], sums2[:, 0:nkc], AX.X), [tsm2],
                        strict=True)
                    trs = P.op("dve", lambda e, rscol=rscol: e.reciprocal(
                        sm[:, rscol:rscol + 1], sm[:, 8:9]), [tss],
                        strict=True)
                    trs_prev = trs
                    tresc = {}
                    for kc in range(nkc):
                        w = min(512, L - kc * 512)
                        tresc[kc] = P.op("dve", lambda e, kc=kc, w=w, off=off:
                                         e.tensor_scalar(
                            Psb[:, off + kc * 512:off + kc * 512 + w],
                            Psb[:, off + kc * 512:off + kc * 512 + w],
                            fe[:, kc:kc + 1], None, ALU.mult),
                            [tfee, texp[kc]])
                    if "P1" in taps and i == 1:
                        tp = tap("P1", [128, S], BF16)
                        P.op("sp", lambda e, tp=tp, off=off: e.dma_start(
                            out=tp, in_=Psb[:, off:off + S]),
                            [tresc[kc] for kc in range(nkc)], dma=True)
                    saved[i] = dict(nkc=nkc, nj=L // 128, si=si, off=off,
                                    rscol=rscol, trs=trs, tresc=tresc)

                if it >= 1:
                    j = it - 1
                    d = saved.pop(j)
                    sj, offj, nj = d["si"], d["off"], d["nj"]
                    tplast = None
                    for jg in range((nj + 3) // 4):
                        j0 = jg * 4
                        j1 = min(nj, j0 + 4)
                        par = jg % 2
                        tt = None
                        for jj in range(j0, j1):
                            tt = P.op("pe", lambda e, jj=jj, j0=j0, par=par,
                                      offj=offj: e.matmul(
                                p_pt[:, par * 512 + (jj - j0) * 128:
                                     par * 512 + (jj - j0 + 1) * 128],
                                Psb[:, offj + jj * 128:offj + (jj + 1) * 128],
                                ident[:], start=True, stop=True),
                                ([d["tresc"][jg]] + ppt_free[par])
                                if jj == j0 else [], inc=(jj == j1 - 1))
                        ppt_free[par] = []
                        tptev = P.op("act", lambda e, j0=j0, j1=j1, par=par:
                                     e.activation(
                            PTsb[:, par, 0:(j1 - j0) * 128],
                            p_pt[:, par * 512:par * 512 + (j1 - j0) * 128],
                            AF.Copy), [tt] + PT_free[par])
                        PT_free[par] = []
                        ppt_free[par].append(tptev)
                        for jj in range(j0, j1):
                            for half in range(2):
                                tplast = P.op("pe", lambda e, jj=jj, j0=j0,
                                              par=par, half=half, nj=nj:
                                              e.matmul(
                                    p_y[:, half * 512:(half + 1) * 512],
                                    PTsb[:, par,
                                         (jj - j0) * 128:(jj - j0 + 1) * 128],
                                    vS[:, jj * D + half * 512:
                                       jj * D + (half + 1) * 512],
                                    start=(jj == 0), stop=(jj == nj - 1)),
                                    [tptev] + (py_free if (jj == 0 and
                                                           half == 0) else []),
                                    inc=(jj == j1 - 1 and half == 1))
                        if jg == 0:
                            py_free = []
                        PT_free[par].append(tplast)
                    for kc in range(d["nkc"]):
                        psb_war[(sj, kc)] = [tplast]
                    tyb = P.op("dve", lambda e, rc=d["rscol"]: e.tensor_scalar(
                        yb[:], p_y[:], sm[:, rc:rc + 1], None, ALU.mult),
                        [tplast, d["trs"]] + yb_free, strict=True)
                    yb_free = []
                    py_free = [tyb]
                    if "y1" in taps and j == 1:
                        tp = tap("y1", [128, D], BF16)
                        P.op("sp", lambda e, tp=tp: e.dma_start(
                            out=tp, in_=yb[:]), [tyb], dma=True)
                    tytev = []
                    for grp in range(2):
                        par = grp
                        tt = None
                        for jj in range(4):
                            dc = grp * 4 + jj
                            tt = P.op("pe", lambda e, dc=dc, jj=jj, par=par:
                                      e.matmul(
                                p_pt[:, par * 512 + jj * 128:
                                     par * 512 + (jj + 1) * 128],
                                yb[:, dc * 128:(dc + 1) * 128], ident[:],
                                start=True, stop=True),
                                ([tyb] + ppt_free[par]) if jj == 0 else [],
                                inc=(jj == 3))
                        ppt_free[par] = []
                        tev = P.op("act", lambda e, grp=grp, par=par:
                                   e.activation(
                            yT[:, grp * 512:(grp + 1) * 512],
                            p_pt[:, par * 512:(par + 1) * 512], AF.Copy),
                            [tt] + yT_free)
                        yT_free = []
                        ppt_free[par].append(tev)
                        tytev.append(tev)
                    yb_free = [P.token("pe")]
                    tao_ev = []
                    for half in range(2):
                        for dc in range(ND):
                            P.op("pe", lambda e, dc=dc, half=half: e.matmul(
                                p_ao[:], yT[:, dc * 128:(dc + 1) * 128],
                                wo[:, dc, half * 512:(half + 1) * 512],
                                start=(dc == 0), stop=False),
                                ([t_wo, t_bo] + tytev + ao_free)
                                if dc == 0 else [], inc=False)
                        tmm2 = P.op("pe", lambda e, half=half: e.matmul(
                            p_ao[:], ones1[:],
                            bor[:, half * 512:(half + 1) * 512],
                            start=False, stop=True), [])
                        ao_free = []
                        tev = P.op("act", lambda e, half=half: e.activation(
                            aosb[:, half * 512:(half + 1) * 512], p_ao[:],
                            AF.Relu), [tmm2])
                        ao_free.append(tev)
                        tao_ev.append(tev)
                    yT_free = [P.token("pe")]
                    tao_st = P.op("sp", lambda e, j=j: e.dma_start(
                        out=ao_s[j], in_=aosb[:]), tao_ev, dma=True)
                    ao_free.append(tao_st)
                    if "ao1" in taps and j == 1:
                        tp = tap("ao1", [128, D], BF16)
                        P.op("sp", lambda e, tp=tp: e.dma_start(
                            out=tp, in_=aosb[:]), tao_ev, dma=True)
                if it >= 2:
                    emit_prep_store()
                    emit_prep_store()
                if it >= 1:
                    emit_prep_quant()
                    emit_prep_quant()
            while unit_idx[0] < len(UNITS):
                emit_prep_store()
                emit_prep_quant()
            while prep_q:
                emit_prep_store()
            tcoll = P.token("sp.d")

        P.barrier()
        s12b.close()
        s12a.close()
        P.barrier()

        with ExitStack() as s3:
            wgT = s3.enter_context(nc.sbuf_tensor("wgT", [128, ND, DI], BF16))
            wuT = s3.enter_context(nc.sbuf_tensor("wuT", [128, ND, DI], BF16))
            wdT = s3.enter_context(nc.sbuf_tensor("wdT", [128, NI, D], BF16))
            xqT3 = s3.enter_context(
                nc.sbuf_tensor("xqT3", [128, 2, ND, 128], BF16))
            aoin = s3.enter_context(nc.sbuf_tensor("aoin", [128, 2, D], BF16))
            hs2 = s3.enter_context(nc.sbuf_tensor("hs2", [128, 2, 512], F32))
            hqb = s3.enter_context(nc.sbuf_tensor("hqb", [128, 2, DI], BF16))
            hqT = s3.enter_context(nc.sbuf_tensor("hqT", [128, DI], BF16))
            outb = s3.enter_context(nc.sbuf_tensor("outb", [128, D], F32))
            p_g = s3.enter_context(nc.psum_tensor("p_g", [128, 1024], F32))
            p_u = s3.enter_context(nc.psum_tensor("p_u", [128, 1024], F32))
            p_d = s3.enter_context(nc.psum_tensor("p_d", [128, 1024], F32))
            p_ht = s3.enter_context(nc.psum_tensor("p_ht", [128, 1024], F32))

            tload_units = []
            for u, unit in enumerate(UNITS):
                slot = wqall[u].rearrange("(c p j) -> p c j", p=128, j=128)
                if unit[0] == "g":
                    dstu = wgT[:, :, unit[1] * 128:(unit[1] + 1) * 128]
                    ntr = 8
                elif unit[0] == "u":
                    dstu = wuT[:, :, unit[1] * 128:(unit[1] + 1) * 128]
                    ntr = 8
                else:
                    _, dcn, ssn = unit
                    ntr = 6 if ssn == 2 else 8
                    dstu = wdT[:, ssn * 8:ssn * 8 + ntr,
                               dcn * 128:(dcn + 1) * 128]
                tload_units.append(P.op(
                    "sp", lambda e, d=dstu, s2=slot, ntr=ntr: e.dma_start(
                        out=d, in_=s2[:, 0:ntr, :]), [tcoll], dma=True))
            twg = twu = twd = tload_units[-1]

            nchunks = [(jx * 512, min(512, DI - jx * 512))
                       for jx in range((DI + 511) // 512)]
            xq3_free = [[], []]
            xt_free3 = [[], []]
            ao_in_free = [[], []]
            pg_free = [[], []]
            pu_free = [[], []]
            hslot_free = [[], []]
            hqb_war = [[], []]
            hqT_free = []
            pht_free = [[], []]
            pd_free = []
            out_free = []
            saved3 = {}

            for it in range(NT + 1):
                if it < NT:
                    i = it
                    sl = i % 2
                    tqld = P.op("sp", lambda e, i=i, sl=sl: e.dma_start(
                        out=xqT3[:, sl],
                        in_=xqt_s[i].rearrange("c p j -> p c j")),
                        xq3_free[sl], dma=True)
                    xq3_free[sl] = []
                    taold = P.op("sp", lambda e, i=i, sl=sl: e.dma_start(
                        out=aoin[:, sl], in_=ao_s[i]), ao_in_free[sl],
                        dma=True)
                    ao_in_free[sl] = []
                    txld = P.op("sp", lambda e, i=i, sl=sl: e.dma_start(
                        out=xts(sl), in_=x_d[i * 128:(i + 1) * 128, :]),
                        xt_free3[sl], dma=True)
                    xt_free3[sl] = []
                    tqh = []
                    tu = None
                    for ci, (n0, w) in enumerate(nchunks):
                        par = ci % 2
                        hs = hs2[:, par, 0:w]
                        pg = p_g[:, par * 512:par * 512 + w]
                        pu = p_u[:, par * 512:par * 512 + w]
                        tg = None
                        for di in range(ND):
                            tg = P.op("pe", lambda e, di=di, n0=n0, w=w,
                                      sl=sl, pg=pg: e.matmul(
                                pg, xqT3[:, sl, di, :],
                                wgT[:, di, n0:n0 + w],
                                start=(di == 0), stop=(di == ND - 1)),
                                ([tqld, twg] + pg_free[par])
                                if di == 0 else [], inc=(di == ND - 1))
                        pg_free[par] = []
                        for di in range(ND):
                            tu = P.op("pe", lambda e, di=di, n0=n0, w=w,
                                      sl=sl, pu=pu: e.matmul(
                                pu, xqT3[:, sl, di, :],
                                wuT[:, di, n0:n0 + w],
                                start=(di == 0), stop=(di == ND - 1)),
                                ([twu] + pu_free[par]) if di == 0 else [],
                                inc=(di == ND - 1))
                        pu_free[par] = []
                        tsl_ = P.op("act", lambda e, hs=hs, pg=pg:
                                    e.activation(hs, pg, AF.Silu),
                                    [tg] + hslot_free[par])
                        hslot_free[par] = []
                        tmul = P.op("dve", lambda e, hs=hs, pu=pu:
                                    e.tensor_tensor(hs, hs, pu, ALU.mult),
                                    [tsl_, tu])
                        pg_free[par].append(tsl_)
                        pu_free[par].append(tmul)
                        tq, sfree = emit_quant(
                            hs, hqb[:, sl, n0:n0 + w], w, [tmul],
                            dst_war=list(hqb_war[sl]))
                        hslot_free[par] = sfree
                        tqh.append(tq)
                    hqb_war[sl] = []
                    xq3_free[sl].append(tu)
                    if "hq0" in taps and i == 0:
                        tp = tap("hq0", [128, DI], BF16)
                        P.op("sp", lambda e, tp=tp, sl=sl: e.dma_start(
                            out=tp, in_=hqb[:, sl]), tqh, dma=True)
                    saved3[i] = dict(tqh=tqh, taold=taold, txld=txld, sl=sl)

                if it >= 1:
                    j = it - 1
                    d = saved3.pop(j)
                    sj = d["sl"]
                    for jg in range((NI + 3) // 4):
                        j0 = jg * 4
                        j1 = min(NI, j0 + 4)
                        par = jg % 2
                        tt = None
                        for jj in range(j0, j1):
                            tt = P.op("pe", lambda e, jj=jj, j0=j0, par=par,
                                      sj=sj: e.matmul(
                                p_ht[:, par * 512 + (jj - j0) * 128:
                                     par * 512 + (jj - j0 + 1) * 128],
                                hqb[:, sj, jj * 128:(jj + 1) * 128],
                                ident[:], start=True, stop=True),
                                (d["tqh"] + pht_free[par]) if jj == j0 else [],
                                inc=(jj == j1 - 1))
                        pht_free[par] = []
                        tev = P.op("act", lambda e, j0=j0, j1=j1, par=par:
                                   e.activation(
                            hqT[:, j0 * 128:j1 * 128],
                            p_ht[:, par * 512:par * 512 + (j1 - j0) * 128],
                            AF.Copy), [tt] + hqT_free)
                        hqT_free = []
                        pht_free[par].append(tev)
                    hqb_war[sj] = [P.token("pe")]
                    thqT = P.token("act")
                    tdm = None
                    for half in range(2):
                        for ic in range(NI):
                            tdm = P.op("pe", lambda e, ic=ic, half=half:
                                       e.matmul(
                                p_d[:, half * 512:(half + 1) * 512],
                                hqT[:, ic * 128:(ic + 1) * 128],
                                wdT[:, ic, half * 512:(half + 1) * 512],
                                start=(ic == 0), stop=(ic == NI - 1)),
                                ([thqT, twd] + pd_free) if ic == 0 else [],
                                inc=(ic == NI - 1))
                        pd_free = []
                    hqT_free.append(tdm)
                    tc1 = P.op("dve", lambda e, sj=sj: e.scalar_tensor_tensor(
                        outb[:], p_d[:], 1.0, aoin[:, sj], ALU.mult, ALU.add),
                        [tdm, d["taold"]] + out_free)
                    out_free = []
                    tc2 = P.op("dve", lambda e, sj=sj: e.tensor_tensor(
                        outb[:], outb[:], xts(sj), ALU.add),
                        [d["txld"], tc1])
                    pd_free.append(tc1)
                    ao_in_free[sj].append(tc1)
                    xt_free3[sj].append(tc2)
                    tst = P.op("sp", lambda e, j=j: e.dma_start(
                        out=out_d[j * 128:(j + 1) * 128, :], in_=outb[:]),
                        [tc2], dma=True)
                    out_free.append(tst)

            tfinal = P.token("sp.d")
            P.op("sp", lambda e: e.nop(),
                 [("pe", P.cnt["pe"]), ("dve", P.cnt["dve"]), tfinal],
                 inc=False)

        with nc.Block() as block:
            block.sync(P.replay("sp"))
            block.tensor(P.replay("pe"))
            block.vector(P.replay("dve"))
            block.scalar(P.replay("act"))
            block.gpsimd(P.replay("gp"))

    return nc, tapout


def _layout_inputs(L0_attn_x, L0_attn_Wq, L0_attn_bq, L0_attn_Wk, L0_attn_bk,
                   L0_attn_Wv, L0_attn_bv, L0_attn_Wo, L0_attn_bo,
                   L0_ffn_w_gate, L0_ffn_w_up, L0_ffn_w_down):
    x = np.asarray(L0_attn_x, dtype=np.float32)
    bq = np.asarray(L0_attn_bq, np.float32).reshape(8, 128).T
    bk = np.asarray(L0_attn_bk, np.float32).reshape(8, 128).T
    bqk = np.ascontiguousarray(np.concatenate([bq, bk], axis=1))
    common = {
        "wq": np.ascontiguousarray(np.asarray(L0_attn_Wq, np.float32)),
        "wk": np.ascontiguousarray(np.asarray(L0_attn_Wk, np.float32)),
        "wv": np.ascontiguousarray(np.asarray(L0_attn_Wv, np.float32)),
        "wo": np.ascontiguousarray(np.asarray(L0_attn_Wo, np.float32)),
        "bqk": bqk,
        "bv": np.ascontiguousarray(
            np.asarray(L0_attn_bv, np.float32).reshape(1, D)),
        "bo": np.ascontiguousarray(
            np.asarray(L0_attn_bo, np.float32).reshape(1, D)),
        "wg": np.ascontiguousarray(np.asarray(L0_ffn_w_gate, np.float32)),
        "wu": np.ascontiguousarray(np.asarray(L0_ffn_w_up, np.float32)),
        "wd": np.ascontiguousarray(np.asarray(L0_ffn_w_down, np.float32)),
    }
    in_maps = []
    for c in range(8):
        m = dict(common)
        m["x"] = np.ascontiguousarray(x[c])
        in_maps.append(m)
    return in_maps


def kernel(L0_attn_x, L0_attn_Wq, L0_attn_bq, L0_attn_Wk, L0_attn_bk,
           L0_attn_Wv, L0_attn_bv, L0_attn_Wo, L0_attn_bo,
           L0_ffn_w_gate, L0_ffn_w_up, L0_ffn_w_down):
    in_maps = _layout_inputs(L0_attn_x, L0_attn_Wq, L0_attn_bq, L0_attn_Wk,
                             L0_attn_bk, L0_attn_Wv, L0_attn_bv, L0_attn_Wo,
                             L0_attn_bo, L0_ffn_w_gate, L0_ffn_w_up,
                             L0_ffn_w_down)
    nc, _ = build_program()
    res = run_bass_kernel_spmd(nc, in_maps, core_ids=list(range(8)))
    return np.stack([np.asarray(r["out"], dtype=np.float32)
                     for r in res.results], axis=0)


# version 16
# speedup vs baseline: 1.0271x; 1.0271x over previous
import numpy as np
import ml_dtypes
from contextlib import ExitStack

import concourse.bass as bass
import concourse.mybir as mybir
from concourse.bass_utils import run_bass_kernel_spmd

F32 = mybir.dt.float32
BF16 = mybir.dt.bfloat16
AF = mybir.ActivationFunctionType
ALU = mybir.AluOpType
AX = mybir.AxisListType

S = 4096
D = 1024
DI = 2816
NT = S // 128
ND = D // 128
NI = DI // 128
G = 4
CRND = float(np.float32(1.5 * 2 ** 23))
UNITS = ([("g", n) for n in range(22)] + [("u", n) for n in range(22)] +
         [("d", dc, ss) for dc in range(8) for ss in range(3)])
NEGBIG = -320000.0
BF = ml_dtypes.bfloat16


class Prog:
    ENGS = ("sp", "pe", "dve", "act", "gp")

    def __init__(self, nc, ctx):
        self.nc = nc
        self.sems = {}
        for ek in self.ENGS:
            self.sems[ek] = ctx.enter_context(nc.semaphore("sem_" + ek))
        for ek in ("sp", "gp"):
            self.sems[ek + ".d"] = ctx.enter_context(
                nc.semaphore("sem_" + ek + "d"))
        self.barsem = ctx.enter_context(nc.semaphore("sem_bar"))
        self.cnt = dict.fromkeys(self.sems, 0)
        self.barcnt = 0
        self.rec = {ek: [] for ek in self.ENGS}

    def op(self, ek, fn, waits=(), dma=False, inc=True, strict=False,
           drain=False):
        key = ek + ".d" if dma else ek
        ws = []
        for t in waits:
            if t is None:
                continue
            if t[0] == ek and not strict:
                continue
            ws.append(t)
        self.rec[ek].append((fn, ws, key, dma, inc, drain))
        if inc:
            self.cnt[key] += 16 if dma else 1
            return (key, self.cnt[key])
        return None

    def barrier(self):
        self.barcnt += len(self.ENGS)
        tgt = self.barcnt
        for ek in self.ENGS:
            self.rec[ek].append(("bar", tgt, None, None, None, None))

    def token(self, key):
        return (key, self.cnt[key])

    def replay(self, ek):
        sems = self.sems

        def go(eng):
            waited = {}
            for entry in self.rec[ek]:
                if entry[0] == "bar":
                    eng.sem_inc(self.barsem, 1)
                    eng.wait_ge(self.barsem, entry[1])
                    continue
                fn, ws, key, dmaf, inc, drn = entry
                for (sk, val) in ws:
                    if waited.get(sk, 0) < val:
                        eng.wait_ge(sems[sk], val)
                        waited[sk] = val
                ins = fn(eng)
                if drn:
                    ins = eng.drain()
                if inc:
                    ins.then_inc(sems[key], 16 if dmaf else 1)
        return go


def build_program(taps=()):
    nc = bass.Bass("TRN2", target_bir_lowering=False)
    taps = set(taps)

    x_d = nc.dram_tensor("x", [S, D], F32, kind="ExternalInput").ap()
    wq_d = nc.dram_tensor("wq", [D, D], F32, kind="ExternalInput").ap()
    wk_d = nc.dram_tensor("wk", [D, D], F32, kind="ExternalInput").ap()
    wv_d = nc.dram_tensor("wv", [D, D], F32, kind="ExternalInput").ap()
    wo_d = nc.dram_tensor("wo", [D, D], F32, kind="ExternalInput").ap()
    bqk_d = nc.dram_tensor("bqk", [128, 16], F32, kind="ExternalInput").ap()
    bv_d = nc.dram_tensor("bv", [1, D], F32, kind="ExternalInput").ap()
    bo_d = nc.dram_tensor("bo", [1, D], F32, kind="ExternalInput").ap()
    wg_d = nc.dram_tensor("wg", [DI, D], F32, kind="ExternalInput").ap()
    wu_d = nc.dram_tensor("wu", [DI, D], F32, kind="ExternalInput").ap()
    wd_d = nc.dram_tensor("wd", [D, DI], F32, kind="ExternalInput").ap()
    out_d = nc.dram_tensor("out", [S, D], F32, kind="ExternalOutput").ap()

    qt_s = nc.dram_tensor("qt_s", [NT, ND, 128, 128], BF16, kind="Internal").ap()
    xqt_s = nc.dram_tensor("xqt_s", [NT, ND, 128, 128], BF16, kind="Internal").ap()
    xnt_s = nc.dram_tensor("xnt_s", [NT, ND, 128, 128], BF16, kind="Internal").ap()
    ao_s = nc.dram_tensor("ao_s", [NT, 128, D], BF16, kind="Internal").ap()
    wqall = nc.dram_tensor("wqall", [68, 8 * 128 * 128], BF16,
                           kind="Internal").ap()

    ident_c = nc.inline_tensor(np.eye(128, dtype=BF), name="identc").ap()
    mrows = np.arange(128)[:, None]
    mcols = np.arange(128)[None, :]
    mask_np = np.where(mcols > mrows, 0.0, 1.0).astype(BF)
    mask_c = nc.inline_tensor(mask_np, name="maskc").ap()
    ones_c = nc.inline_tensor(np.ones((1, 128), dtype=BF), name="onesc").ap()

    tapout = {}

    def tap(name, shape, dt):
        tapout[name] = nc.dram_tensor("tap_" + name, shape, dt,
                                      kind="ExternalOutput").ap()
        return tapout[name]

    g = ExitStack()
    with g:
        P = Prog(nc, g)

        def sb(n, sh, dt):
            return g.enter_context(nc.sbuf_tensor(n, sh, dt))

        ident = sb("ident", [128, 128], BF16)
        mask = sb("mask", [128, 128], BF16)
        ones1 = sb("ones1", [1, 128], BF16)
        bqk = sb("bqk_t", [128, 16], F32)
        bvo = sb("bvo", [1, 2 * D], BF16)
        bvr = bvo[:, 0:D]
        bor = bvo[:, D:2 * D]
        xt = sb("xt", [128, 2 * D], F32)
        QW = 512
        qse = []
        for si in range(2):
            qse.append({
                "a": sb(f"q_a{si}", [128, QW], F32),
                "t1": sb(f"q_t1{si}", [128, QW], F32),
                "r1": sb(f"q_r1{si}", [128, QW], BF16),
                "hi": sb(f"q_hi{si}", [128, QW], BF16),
                "q2": sb(f"q_q2{si}", [128, QW], BF16),
                "sg": sb(f"q_sg{si}", [128, QW], BF16),
                "bmax": sb(f"q_bmax{si}", [128, 32], F32),
                "scl": sb(f"q_scl{si}", [128, 32], F32),
                "rcp": sb(f"q_rcp{si}", [128, 32], F32),
            })
        sm = sb("sm", [128, 16], F32)
        mx = sb("mx", [128, 8], F32)
        negmc = sb("negmc", [128, 8], F32)
        sums = sb("sums", [128, 8], F32)
        fe = sb("fe", [128, 8], F32)
        sums2 = sb("sums2", [128, 8], F32)
        kbias = sb("kbias", [128, 4], F32)

        def xts(sl):
            return xt[:, sl * D:(sl + 1) * D]

        t_ident = P.op("sp", lambda e: e.dma_start(out=ident[:], in_=ident_c),
                       dma=True)
        t_mask = P.op("sp", lambda e: e.dma_start(out=mask[:], in_=mask_c),
                      dma=True)
        t_ones = P.op("sp", lambda e: e.dma_start(out=ones1[:], in_=ones_c),
                      dma=True)
        t_bqk = P.op("sp", lambda e: e.dma_start(out=bqk[:], in_=bqk_d),
                     dma=True)
        t_bv = P.op("gp", lambda e: e.dma_start(out=bvr, in_=bv_d), dma=True)
        for j, val in enumerate((-5.0, -7.0, -10.0)):
            P.op("dve", lambda e, j=j, v=val: e.memset(kbias[:, j:j + 1], v),
                 drain=(j == 2))
        t_kb = P.token("dve")
        t_bo = P.op("gp", lambda e: e.dma_start(out=bor, in_=bo_d), dma=True)
        bqk32 = sb("bqk32", [128, 8], F32)
        t_b32 = P.op("dve", lambda e: e.tensor_scalar(
            bqk32[:], bqk[:, 0:8], 1.0 / 32.0, None, ALU.mult), [t_bqk],
            drain=True)

        last_q = [None, None]
        qctr = [0]

        def emit_quant(src_ap, dst_ap, w, waits, dst_war=()):
            si = qctr[0] % 2
            qctr[0] += 1
            T = qse[si]
            nb = w // 16
            src3 = src_ap.rearrange("p (a b) -> p a b", b=16)
            s2b3 = T["scl"][:, 0:nb].unsqueeze(2).broadcast_to([128, nb, 16])
            rcb3 = T["rcp"][:, 0:nb].unsqueeze(2).broadcast_to([128, nb, 16])
            a2 = T["a"][:, 0:w]
            a3 = a2.rearrange("p (a b) -> p a b", b=16)
            w0 = list(waits) + ([last_q[si]] if last_q[si] else [])
            tA = P.op("act", lambda e: e.activation(a2, src_ap, AF.Abs), w0)
            tSG = P.op("act", lambda e: e.sign(T["sg"][:, 0:w], src_ap), w0)
            tR = P.op("dve", lambda e: e.tensor_reduce(
                T["bmax"][:, 0:nb], src3, AX.X, ALU.max,
                apply_absolute_value=True), w0)
            tsc = P.op("dve", lambda e: e.tensor_scalar(
                T["scl"][:, 0:nb], T["bmax"][:, 0:nb], 1.0 / 12.0, 5e-13,
                ALU.mult, ALU.max), [tR], strict=True)
            trc = P.op("dve", lambda e: e.reciprocal(
                T["rcp"][:, 0:nb], T["scl"][:, 0:nb]), [tsc], strict=True)
            tan = P.op("dve", lambda e: e.tensor_tensor(a3, a3, rcb3, ALU.mult),
                       [tA, trc], strict=True)
            tu1 = P.op("dve", lambda e: e.tensor_scalar(
                T["t1"][:, 0:w], a2, 4.0, CRND, ALU.min, ALU.add), [tan])
            tr1 = P.op("dve", lambda e: e.tensor_scalar(
                T["r1"][:, 0:w], T["t1"][:, 0:w], CRND - 4.0, None,
                ALU.subtract), [tu1])
            th1 = P.op("act", lambda e: e.sign(
                T["hi"][:, 0:w], a2, bias=kbias[:, 0:1]), [tan, t_kb])
            th2 = P.op("act", lambda e: e.sign(
                T["q2"][:, 0:w], a2, bias=kbias[:, 1:2]), [th1])
            tt1 = P.op("dve", lambda e: e.tensor_tensor(
                T["hi"][:, 0:w], T["hi"][:, 0:w], T["q2"][:, 0:w], ALU.add),
                [th1, th2])
            th3 = P.op("act", lambda e: e.sign(
                T["q2"][:, 0:w], a2, bias=kbias[:, 2:3]), [th2, tt1])
            tt2 = P.op("dve", lambda e: e.tensor_tensor(
                T["r1"][:, 0:w], T["r1"][:, 0:w], T["hi"][:, 0:w], ALU.add),
                [tt1, tr1])
            tt3 = P.op("dve", lambda e: e.tensor_scalar(
                T["q2"][:, 0:w], T["q2"][:, 0:w], 2.0, None, ALU.mult), [th3])
            tt4 = P.op("dve", lambda e: e.tensor_tensor(
                T["r1"][:, 0:w], T["r1"][:, 0:w], T["q2"][:, 0:w], ALU.add),
                [tt3, tt2])
            q13 = T["r1"][:, 0:w].rearrange("p (a b) -> p a b", b=16)
            t13 = T["t1"][:, 0:w].rearrange("p (a b) -> p a b", b=16)
            tqs = P.op("dve", lambda e: e.tensor_tensor(
                t13, q13, s2b3, ALU.mult), [tt4, tsc], strict=True)
            tdst = P.op("gp", lambda e: e.tensor_tensor(
                dst_ap, T["t1"][:, 0:w], T["sg"][:, 0:w], ALU.mult),
                [tqs, tSG] + list(dst_war))
            last_q[si] = tdst
            return tdst, [tA, tSG, tR]

        def emit_quant_multi(src_ap, dst_ap, w, waits, dst_war=()):
            toks = []
            sfree = []
            o = 0
            while o < w:
                ww = min(512, w - o)
                t, sf = emit_quant(src_ap[:, o:o + ww], dst_ap[:, o:o + ww],
                                   ww, waits, dst_war)
                toks.append(t)
                sfree.extend(sf)
                o += ww
            return toks, sfree

        s12a = ExitStack()
        kT = s12a.enter_context(nc.sbuf_tensor("kT", [128, ND, S], BF16))

        with ExitStack() as s1:
            wq = s1.enter_context(nc.sbuf_tensor("wq_t", [128, ND, D], BF16))
            wk = s1.enter_context(nc.sbuf_tensor("wk_t", [128, ND, D], BF16))
            xn_f = s1.enter_context(nc.sbuf_tensor("xn_f", [128, 2, D], F32))
            xn_b = s1.enter_context(nc.sbuf_tensor("xn_b", [128, D], BF16))
            xq_b = s1.enter_context(nc.sbuf_tensor("xq_b", [128, D], BF16))
            xnT = s1.enter_context(
                nc.sbuf_tensor("xnT", [128, ND, G * 128], BF16))
            qTb = s1.enter_context(
                nc.sbuf_tensor("qTb", [128, ND, G * 128], BF16))
            xqT = s1.enter_context(nc.sbuf_tensor("xqT", [128, ND * 128], BF16))
            p_tr = s1.enter_context(nc.psum_tensor("p_tr", [128, 2048], F32))
            p_q = s1.enter_context(nc.psum_tensor("p_q", [128, 1024], F32))
            p_k = s1.enter_context(nc.psum_tensor("p_k", [128, 1024], F32))

            t_wq = P.op("gp", lambda e: e.dma_start(
                out=wq[:], in_=wq_d.rearrange("(k p) n -> p k n", p=128)),
                dma=True)
            t_wk = P.op("gp", lambda e: e.dma_start(
                out=wk[:], in_=wk_d.rearrange("(k p) n -> p k n", p=128)),
                dma=True)

            xt_free = [[], []]
            xnT_free = [[] for _ in range(G)]
            xqT_free = []
            qTb_free = []
            kq_psum_free = [[], []]
            tr_free_xn = []
            tr_free_xq = []
            prev_ttr = None
            prev_ttq = None
            xnf_free = [[], []]
            tvar_prev = None
            tx_tok = {}

            def p1_load(i):
                sl = i % 2
                t = P.op("sp", lambda e, i=i, sl=sl: e.dma_start(
                    out=xts(sl), in_=x_d[i * 128:(i + 1) * 128, :]),
                    xt_free[sl], dma=True)
                xt_free[sl] = []
                tx_tok[i] = t

            p1_load(0)
            for i in range(NT):
                b, gi = divmod(i, G)
                sl = i % 2
                xs = xts(sl)
                xnf = xn_f[:, sl]
                tx = tx_tok.pop(i)
                tsum = P.op("dve", lambda e, xs=xs: e.reduce_sum(
                    sm[:, 0:1], xs, AX.X), [tx])
                tssq = P.op("act", lambda e, xs=xs, xnf=xnf: e.activation(
                    xnf, xs, AF.Square, accum_out=sm[:, 1:2]),
                    [tx, tvar_prev] + xnf_free[sl])
                xnf_free[sl] = []
                tmean = P.op("dve", lambda e: e.tensor_scalar(
                    sm[:, 2:3], sm[:, 0:1], 1.0 / D, None, ALU.mult), [tsum],
                    strict=True)
                tm2 = P.op("dve", lambda e: e.tensor_scalar(
                    sm[:, 10:11], sm[:, 2:3], sm[:, 2:3], -1e-5,
                    ALU.mult, ALU.add), [tmean], strict=True)
                tvar = P.op("dve", lambda e: e.scalar_tensor_tensor(
                    sm[:, 3:4], sm[:, 1:2], 1.0 / D, sm[:, 10:11],
                    ALU.mult, ALU.subtract), [tssq, tm2], strict=True)
                tvar_prev = tvar
                tsd = P.op("act", lambda e: e.activation(
                    sm[:, 4:5], sm[:, 3:4], AF.Ln), [tvar])
                trstd = P.op("act", lambda e: e.activation(
                    sm[:, 5:6], sm[:, 4:5], AF.Exp, scale=-0.5), [tsd],
                    strict=True, drain=True)
                txn = P.op("dve", lambda e, xs=xs, xnf=xnf: e.tensor_scalar(
                    xnf, xs, sm[:, 2:3], sm[:, 5:6],
                    ALU.subtract, ALU.mult), [trstd])
                xt_free[sl].append(tssq)
                xt_free[sl].append(txn)
                if i + 1 < NT:
                    p1_load(i + 1)
                txnb = P.op("act", lambda e, xnf=xnf: e.activation(
                    xn_b[:], xnf, AF.Copy), [txn, prev_ttr])
                txq_l, txq_sf = emit_quant_multi(
                    xnf, xq_b[:], D, [txn, txnb],
                    dst_war=[prev_ttq] if prev_ttq else ())
                xnf_free[sl] = list(txq_sf) + [txnb]
                txq = txq_l[-1]
                if "xn0" in taps and i == 0:
                    tp = tap("xn0", [128, D], F32)
                    P.op("sp", lambda e, tp=tp, xnf=xnf: e.dma_start(
                        out=tp, in_=xnf), [txn, txq], dma=True)
                if "xq0" in taps and i == 0:
                    tp = tap("xq0", [128, D], BF16)
                    P.op("sp", lambda e, tp=tp: e.dma_start(out=tp, in_=xq_b[:]),
                         [txq], dma=True)
                for dc in range(ND):
                    prev_ttr = P.op("pe", lambda e, dc=dc: e.matmul(
                        p_tr[:, dc * 128:(dc + 1) * 128],
                        xn_b[:, dc * 128:(dc + 1) * 128], ident[:],
                        start=True, stop=True),
                        ([txnb, t_ident] + tr_free_xn) if dc == 0 else [],
                        inc=(dc == ND - 1))
                tr_free_xn = []
                for dc in range(ND):
                    prev_ttq = P.op("pe", lambda e, dc=dc: e.matmul(
                        p_tr[:, 1024 + dc * 128:1024 + (dc + 1) * 128],
                        xq_b[:, dc * 128:(dc + 1) * 128], ident[:],
                        start=True, stop=True),
                        (txq_l + tr_free_xq) if dc == 0 else [],
                        inc=(dc == ND - 1))
                tr_free_xq = []
                xnT_dst = xnT[:, :, gi * 128:(gi + 1) * 128]
                tevn = P.op("act", lambda e, d=xnT_dst: e.activation(
                    d, p_tr[:, 0:1024].rearrange("p (c j) -> p c j", j=128),
                    AF.Copy), [prev_ttr] + xnT_free[gi])
                xnT_free[gi] = []
                tr_free_xn.append(tevn)
                tevq = P.op("act", lambda e: e.activation(
                    xqT[:], p_tr[:, 1024:2048], AF.Copy),
                    [prev_ttq] + xqT_free)
                xqT_free = []
                tr_free_xq.append(tevq)
                tso = P.op("sp", lambda e, i=i: e.dma_start(
                    out=xqt_s[i].rearrange("c p j -> p c j"),
                    in_=xqT[:].rearrange("p (c j) -> p c j", j=128)),
                    [tevq], dma=True)
                xqT_free.append(tso)

                if gi == G - 1:
                    for gg in range(G):
                        tsn = P.op("sp", lambda e, b=b, gg=gg: e.dma_start(
                            out=xnt_s[b * G + gg].rearrange("c p j -> p c j"),
                            in_=xnT[:, :, gg * 128:(gg + 1) * 128]),
                            [tevn], dma=True)
                        xnT_free[gg].append(tsn)
                    for proj in range(2):
                        wmat = wq if proj == 0 else wk
                        twm = t_wq if proj == 0 else t_wk
                        pp = p_q if proj == 0 else p_k
                        for do in range(ND):
                            par = do % 2
                            pps = pp[:, par * 512:(par + 1) * 512]
                            tmm = None
                            for di in range(ND):
                                tmm = P.op("pe", lambda e, di=di, do=do,
                                           w=wmat, pp2=pps: e.matmul(
                                    pp2, w[:, di, do * 128:(do + 1) * 128],
                                    xnT[:, di, :],
                                    start=(di == 0), stop=(di == ND - 1)),
                                    ([tevn, twm] + kq_psum_free[par])
                                    if di == 0 else [], inc=(di == ND - 1))
                            kq_psum_free[par] = []
                            if proj == 0:
                                dst = qTb[:, do, :]
                                tev = P.op("act", lambda e, d=dst, pp2=pps,
                                           bc=do: e.activation(
                                    d, pp2, AF.Relu, bias=bqk32[:, bc:bc + 1],
                                    scale=1.0 / 32.0),
                                    [tmm, t_b32] + qTb_free)
                            else:
                                dst = kT[:, do, b * 512:(b + 1) * 512]
                                tev = P.op("act", lambda e, d=dst, pp2=pps,
                                           bc=8 + do: e.activation(
                                    d, pp2, AF.Relu, bias=bqk[:, bc:bc + 1]),
                                    [tmm, t_bqk])
                            if proj == 0:
                                qTb_free = []
                            kq_psum_free[par].append(tev)
                    tact_b = P.token("act")
                    for gg in range(G):
                        tqs_st = P.op("sp", lambda e, b=b, gg=gg: e.dma_start(
                            out=qt_s[b * G + gg].rearrange("c p j -> p c j"),
                            in_=qTb[:, :, gg * 128:(gg + 1) * 128]),
                            [tact_b], dma=True)
                        qTb_free.append(tqs_st)
                    tpe_batch = P.token("pe")
                    for gg in range(G):
                        xnT_free[gg].append(tpe_batch)

            if "kT" in taps:
                tp = tap("kT", [128, ND * S], BF16)
                P.op("sp", lambda e, tp=tp: e.dma_start(
                    out=tp, in_=kT[:].rearrange("p c s -> p (c s)")),
                    [P.token("act")], dma=True)

        P.barrier()

        s12b = ExitStack()
        vS = s12b.enter_context(nc.sbuf_tensor("vS", [128, NT * D], BF16))

        with ExitStack() as s1b:
            wv = s1b.enter_context(nc.sbuf_tensor("wv_t", [128, ND, D], BF16))
            xnT_in = s1b.enter_context(
                nc.sbuf_tensor("xnT_in", [128, 2, ND, 128], BF16))
            p_v = s1b.enter_context(nc.psum_tensor("p_v", [128, 2048], F32))

            t_wv = P.op("gp", lambda e: e.dma_start(
                out=wv[:], in_=wv_d.rearrange("(k p) n -> p k n", p=128)),
                dma=True)
            xnTin_free = [[], []]
            pv_free = [[], []]
            xnb_tok = {}

            def p1b_load(i):
                sl = i % 2
                t = P.op("sp", lambda e, i=i, sl=sl: e.dma_start(
                    out=xnT_in[:, sl],
                    in_=xnt_s[i].rearrange("c p j -> p c j")),
                    xnTin_free[sl], dma=True)
                xnTin_free[sl] = []
                xnb_tok[i] = t

            p1b_load(0)
            for i in range(NT):
                sl = i % 2
                tld = xnb_tok.pop(i)
                if i + 1 < NT:
                    p1b_load(i + 1)
                pvh = p_v[:, sl * 1024:(sl + 1) * 1024]
                tmm = None
                for half in range(2):
                    ph = pvh[:, half * 512:(half + 1) * 512]
                    for di in range(ND):
                        P.op("pe", lambda e, di=di, sl=sl, half=half, ph=ph:
                             e.matmul(
                            ph, xnT_in[:, sl, di, :],
                            wv[:, di, half * 512:(half + 1) * 512],
                            start=(di == 0), stop=False),
                            ([tld, t_wv, t_ones, t_bv] + pv_free[sl])
                            if (di == 0 and half == 0) else [], inc=False)
                    tmm = P.op("pe", lambda e, half=half, ph=ph: e.matmul(
                        ph, ones1[:], bvr[:, half * 512:(half + 1) * 512],
                        start=False, stop=True), [])
                pv_free[sl] = []
                tev = P.op("act", lambda e, i=i, pv2=pvh: e.activation(
                    vS[:, i * D:(i + 1) * D], pv2, AF.Relu), [tmm])
                pv_free[sl].append(tev)
                xnTin_free[sl].append(tev)

            if "V" in taps:
                tp = tap("V", [128, NT * D], BF16)
                P.op("sp", lambda e, tp=tp: e.dma_start(out=tp, in_=vS[:]),
                     [P.token("act")], dma=True)

        P.barrier()

        with ExitStack() as s2:
            wo = s2.enter_context(nc.sbuf_tensor("wo_t", [128, ND, D], BF16))
            qT = s2.enter_context(
                nc.sbuf_tensor("qT_t", [128, 2, ND, 128], BF16))
            Psb = s2.enter_context(nc.sbuf_tensor("Psb", [128, 2 * S], BF16))
            PTsb = s2.enter_context(nc.sbuf_tensor("PTsb", [128, 2, 512], BF16))
            yb = s2.enter_context(nc.sbuf_tensor("yb", [128, D], BF16))
            yT = s2.enter_context(nc.sbuf_tensor("yT", [128, D], BF16))
            aosb = s2.enter_context(nc.sbuf_tensor("aosb", [128, D], BF16))
            wqnat = s2.enter_context(nc.sbuf_tensor("wqnat", [128, 2, D], BF16))
            wbounce = s2.enter_context(nc.sbuf_tensor("wbounce", [128, D], BF16))
            p_s = s2.enter_context(nc.psum_tensor("p_s", [128, 1024], F32))
            p_y = s2.enter_context(nc.psum_tensor("p_y", [128, 1024], F32))
            p_pt = s2.enter_context(nc.psum_tensor("p_pt", [128, 1024], F32))
            p_ao = s2.enter_context(nc.psum_tensor("p_ao", [128, 512], F32))
            p_wp = s2.enter_context(nc.psum_tensor("p_wp", [128, 512], F32))

            t_wo = P.op("gp", lambda e: e.dma_start(
                out=wo[:], in_=wo_d.rearrange("(k p) n -> p k n", p=128)),
                dma=True)

            unit_idx = [0]
            prep_xt_free = [[], []]
            prep_psum_free = []
            prep_bounce_free = []
            prep_wq_war = [[] for _ in range(2)]
            prep_done = []
            prep_q = []

            def emit_prep_quant():
                k = unit_idx[0]
                if k >= len(UNITS):
                    return
                unit_idx[0] += 1
                unit = UNITS[k]
                if unit[0] == "g":
                    src_ap = wg_d[unit[1] * 128:(unit[1] + 1) * 128, :]
                    wsrc = D
                elif unit[0] == "u":
                    src_ap = wu_d[unit[1] * 128:(unit[1] + 1) * 128, :]
                    wsrc = D
                else:
                    _, dcn, ssn = unit
                    wsrc = 768 if ssn == 2 else 1024
                    src_ap = wd_d[dcn * 128:(dcn + 1) * 128,
                                  ssn * 1024:ssn * 1024 + wsrc]
                sl = k % 2
                buf = k % 2
                stg = xts(sl)
                tld = P.op("sp", lambda e, d=stg[:, 0:wsrc], s=src_ap:
                           e.dma_start(out=d, in_=s),
                           prep_xt_free[sl], dma=True)
                prep_xt_free[sl] = []
                tq_l, tq_sf = emit_quant_multi(stg[:, 0:wsrc],
                                               wqnat[:, buf, 0:wsrc], wsrc,
                                               [tld],
                                               dst_war=list(prep_wq_war[buf]))
                prep_wq_war[buf] = []
                prep_xt_free[sl].extend(tq_sf)
                prep_q.append(dict(k=k, buf=buf, tq=tq_l, wsrc=wsrc))

            def emit_prep_store():
                if not prep_q:
                    return
                d = prep_q.pop(0)
                k, buf, wsrc = d["k"], d["buf"], d["wsrc"]
                ntr = wsrc // 128
                tev_last = None
                for grp in range((ntr + 3) // 4):
                    j0 = grp * 4
                    j1 = min(ntr, j0 + 4)
                    tt = None
                    for j in range(j0, j1):
                        tt = P.op("pe", lambda e, j=j, j0=j0, buf=buf:
                                  e.matmul(
                            p_wp[:, (j - j0) * 128:(j - j0 + 1) * 128],
                            wqnat[:, buf, j * 128:(j + 1) * 128], ident[:],
                            start=True, stop=True),
                            (d["tq"] + prep_psum_free) if j == j0 else [],
                            inc=(j == j1 - 1))
                    prep_psum_free.clear()
                    tev_last = P.op("act", lambda e, j0=j0, j1=j1:
                                    e.activation(
                        wbounce[:, j0 * 128:j1 * 128],
                        p_wp[:, 0:(j1 - j0) * 128], AF.Copy),
                        [tt] + prep_bounce_free)
                    prep_bounce_free.clear()
                    prep_psum_free.append(tev_last)
                prep_wq_war[buf].append(P.token("pe"))
                tst = P.op("sp", lambda e, k=k, ntr=ntr: e.dma_start(
                    out=wqall[k].rearrange("(c p j) -> p c j", p=128,
                                           j=128)[:, 0:ntr, :],
                    in_=wbounce[:, 0:wsrc].rearrange("p (c j) -> p c j",
                                                     j=128)),
                    [tev_last], dma=True)
                prep_bounce_free.append(tst)
                prep_done.append(tst)

            qT_free = [[], []]
            ps_free = [[], []]
            psb_war = {}
            PT_free = [[], []]
            ppt_free = [[], []]
            py_free = []
            yb_free = []
            yT_free = []
            ao_free = []
            trs_prev = None
            saved = {}
            qld_tok = {}

            def p2_qload(i):
                si = i % 2
                t = P.op("sp", lambda e, i=i, si=si: e.dma_start(
                    out=qT[:, si],
                    in_=qt_s[i].rearrange("c p j -> p c j")),
                    qT_free[si], dma=True)
                qT_free[si] = []
                qld_tok[i] = t

            p2_qload(0)
            for it in range(NT + 1):
                if it < NT:
                    i = it
                    L = 128 * (i + 1)
                    nkc = (L + 511) // 512
                    si = i % 2
                    off = si * S
                    rscol = 9 + 3 * si
                    tqld = qld_tok.pop(i)
                    if i + 1 < NT:
                        p2_qload(i + 1)
                    texp = {}
                    tmm = None
                    for kc in range(nkc):
                        w = min(512, L - kc * 512)
                        par = kc % 2
                        psl = p_s[:, par * 512:par * 512 + w]
                        for dc in range(ND):
                            tmm = P.op("pe", lambda e, dc=dc, kc=kc, w=w,
                                       psl=psl, si=si: e.matmul(
                                psl, qT[:, si, dc, :],
                                kT[:, dc, kc * 512:kc * 512 + w],
                                start=(dc == 0), stop=(dc == ND - 1)),
                                ([tqld] + ps_free[par]) if dc == 0 else [],
                                inc=(dc == ND - 1))
                        ps_free[par] = []
                        tmx = P.op("dve", lambda e, kc=kc, psl=psl:
                                   e.reduce_max(
                            negmc[:, kc:kc + 1], psl, AX.X, negate=True),
                            [tmm], drain=True)
                        tex = P.op("act", lambda e, kc=kc, w=w, psl=psl,
                                   off=off: e.activation(
                            Psb[:, off + kc * 512:off + kc * 512 + w], psl,
                            AF.Exp, bias=negmc[:, kc:kc + 1], scale=1.0,
                            accum_out=sums[:, kc:kc + 1]),
                            [tmx, trs_prev] + psb_war.pop((si, kc), []))
                        if kc == nkc - 1:
                            tz = P.op("dve", lambda e, kc=kc, w=w, off=off:
                                      e.tensor_tensor(
                                Psb[:, off + kc * 512 + w - 128:
                                    off + kc * 512 + w],
                                Psb[:, off + kc * 512 + w - 128:
                                    off + kc * 512 + w],
                                mask[:], ALU.mult), [tex, t_mask])
                            tex = P.op("dve", lambda e, kc=kc, w=w, off=off:
                                       e.reduce_sum(
                                sums[:, kc:kc + 1],
                                Psb[:, off + kc * 512:off + kc * 512 + w],
                                AX.X), [tz], strict=True)
                        texp[kc] = tex
                        ps_free[par].append(tex)
                    qT_free[si] = [tmm]
                    tm1 = P.op("dve", lambda e, nkc=nkc: e.tensor_reduce(
                        sm[:, 7:8], negmc[:, 0:nkc], AX.X, ALU.min),
                        [texp[nkc - 1]], strict=True)
                    tfe = P.op("dve", lambda e, nkc=nkc: e.tensor_scalar(
                        fe[:, 0:nkc], negmc[:, 0:nkc], sm[:, 7:8], -1.0,
                        ALU.subtract, ALU.mult), [tm1], strict=True)
                    tfee = P.op("act", lambda e, nkc=nkc: e.activation(
                        fe[:, 0:nkc], fe[:, 0:nkc], AF.Exp), [tfe],
                        drain=True)
                    tsm2 = P.op("dve", lambda e, nkc=nkc: e.tensor_tensor(
                        sums2[:, 0:nkc], sums[:, 0:nkc], fe[:, 0:nkc],
                        ALU.mult), [tfee], strict=True)
                    tss = P.op("dve", lambda e, nkc=nkc: e.reduce_sum(
                        sm[:, 8:9], sums2[:, 0:nkc], AX.X), [tsm2],
                        strict=True)
                    trs = P.op("dve", lambda e, rscol=rscol: e.reciprocal(
                        sm[:, rscol:rscol + 1], sm[:, 8:9]), [tss],
                        strict=True)
                    trs_prev = trs
                    tresc = {}
                    for kc in range(nkc):
                        w = min(512, L - kc * 512)
                        tresc[kc] = P.op("dve", lambda e, kc=kc, w=w, off=off:
                                         e.tensor_scalar(
                            Psb[:, off + kc * 512:off + kc * 512 + w],
                            Psb[:, off + kc * 512:off + kc * 512 + w],
                            fe[:, kc:kc + 1], None, ALU.mult),
                            [tfee, texp[kc]])
                    if "P1" in taps and i == 1:
                        tp = tap("P1", [128, S], BF16)
                        P.op("sp", lambda e, tp=tp, off=off: e.dma_start(
                            out=tp, in_=Psb[:, off:off + S]),
                            [tresc[kc] for kc in range(nkc)], dma=True)
                    saved[i] = dict(nkc=nkc, nj=L // 128, si=si, off=off,
                                    rscol=rscol, trs=trs, tresc=tresc)

                if it >= 1:
                    j = it - 1
                    d = saved.pop(j)
                    sj, offj, nj = d["si"], d["off"], d["nj"]
                    tplast = None
                    for jg in range((nj + 3) // 4):
                        j0 = jg * 4
                        j1 = min(nj, j0 + 4)
                        par = jg % 2
                        tt = None
                        for jj in range(j0, j1):
                            tt = P.op("pe", lambda e, jj=jj, j0=j0, par=par,
                                      offj=offj: e.matmul(
                                p_pt[:, par * 512 + (jj - j0) * 128:
                                     par * 512 + (jj - j0 + 1) * 128],
                                Psb[:, offj + jj * 128:offj + (jj + 1) * 128],
                                ident[:], start=True, stop=True),
                                ([d["tresc"][jg]] + ppt_free[par])
                                if jj == j0 else [], inc=(jj == j1 - 1))
                        ppt_free[par] = []
                        tptev = P.op("act", lambda e, j0=j0, j1=j1, par=par:
                                     e.activation(
                            PTsb[:, par, 0:(j1 - j0) * 128],
                            p_pt[:, par * 512:par * 512 + (j1 - j0) * 128],
                            AF.Copy), [tt] + PT_free[par])
                        PT_free[par] = []
                        ppt_free[par].append(tptev)
                        for jj in range(j0, j1):
                            for half in range(2):
                                tplast = P.op("pe", lambda e, jj=jj, j0=j0,
                                              par=par, half=half, nj=nj:
                                              e.matmul(
                                    p_y[:, half * 512:(half + 1) * 512],
                                    PTsb[:, par,
                                         (jj - j0) * 128:(jj - j0 + 1) * 128],
                                    vS[:, jj * D + half * 512:
                                       jj * D + (half + 1) * 512],
                                    start=(jj == 0), stop=(jj == nj - 1)),
                                    [tptev] + (py_free if (jj == 0 and
                                                           half == 0) else []),
                                    inc=(jj == j1 - 1 and half == 1))
                        if jg == 0:
                            py_free = []
                        PT_free[par].append(tplast)
                    for kc in range(d["nkc"]):
                        psb_war[(sj, kc)] = [tplast]
                    tyb = P.op("dve", lambda e, rc=d["rscol"]: e.tensor_scalar(
                        yb[:], p_y[:], sm[:, rc:rc + 1], None, ALU.mult),
                        [tplast, d["trs"]] + yb_free, strict=True)
                    yb_free = []
                    py_free = [tyb]
                    if "y1" in taps and j == 1:
                        tp = tap("y1", [128, D], BF16)
                        P.op("sp", lambda e, tp=tp: e.dma_start(
                            out=tp, in_=yb[:]), [tyb], dma=True)
                    tytev = []
                    for grp in range(2):
                        par = grp
                        tt = None
                        for jj in range(4):
                            dc = grp * 4 + jj
                            tt = P.op("pe", lambda e, dc=dc, jj=jj, par=par:
                                      e.matmul(
                                p_pt[:, par * 512 + jj * 128:
                                     par * 512 + (jj + 1) * 128],
                                yb[:, dc * 128:(dc + 1) * 128], ident[:],
                                start=True, stop=True),
                                ([tyb] + ppt_free[par]) if jj == 0 else [],
                                inc=(jj == 3))
                        ppt_free[par] = []
                        tev = P.op("act", lambda e, grp=grp, par=par:
                                   e.activation(
                            yT[:, grp * 512:(grp + 1) * 512],
                            p_pt[:, par * 512:(par + 1) * 512], AF.Copy),
                            [tt] + yT_free)
                        yT_free = []
                        ppt_free[par].append(tev)
                        tytev.append(tev)
                    yb_free = [P.token("pe")]
                    tao_ev = []
                    for half in range(2):
                        for dc in range(ND):
                            P.op("pe", lambda e, dc=dc, half=half: e.matmul(
                                p_ao[:], yT[:, dc * 128:(dc + 1) * 128],
                                wo[:, dc, half * 512:(half + 1) * 512],
                                start=(dc == 0), stop=False),
                                ([t_wo, t_bo] + tytev + ao_free)
                                if dc == 0 else [], inc=False)
                        tmm2 = P.op("pe", lambda e, half=half: e.matmul(
                            p_ao[:], ones1[:],
                            bor[:, half * 512:(half + 1) * 512],
                            start=False, stop=True), [])
                        ao_free = []
                        tev = P.op("act", lambda e, half=half: e.activation(
                            aosb[:, half * 512:(half + 1) * 512], p_ao[:],
                            AF.Relu), [tmm2])
                        ao_free.append(tev)
                        tao_ev.append(tev)
                    yT_free = [P.token("pe")]
                    tao_st = P.op("sp", lambda e, j=j: e.dma_start(
                        out=ao_s[j], in_=aosb[:]), tao_ev, dma=True)
                    ao_free.append(tao_st)
                    if "ao1" in taps and j == 1:
                        tp = tap("ao1", [128, D], BF16)
                        P.op("sp", lambda e, tp=tp: e.dma_start(
                            out=tp, in_=aosb[:]), tao_ev, dma=True)
                if it >= 2:
                    emit_prep_store()
                    emit_prep_store()
                if it >= 1:
                    emit_prep_quant()
                    emit_prep_quant()
            while unit_idx[0] < len(UNITS):
                emit_prep_store()
                emit_prep_quant()
            while prep_q:
                emit_prep_store()
            tcoll = P.token("sp.d")

        P.barrier()
        s12b.close()
        s12a.close()
        P.barrier()

        with ExitStack() as s3:
            wgT = s3.enter_context(nc.sbuf_tensor("wgT", [128, ND, DI], BF16))
            wuT = s3.enter_context(nc.sbuf_tensor("wuT", [128, ND, DI], BF16))
            wdT = s3.enter_context(nc.sbuf_tensor("wdT", [128, NI, D], BF16))
            xqT3 = s3.enter_context(
                nc.sbuf_tensor("xqT3", [128, 2, ND, 128], BF16))
            aoin = s3.enter_context(nc.sbuf_tensor("aoin", [128, 2, D], BF16))
            hs2 = s3.enter_context(nc.sbuf_tensor("hs2", [128, 2, 512], F32))
            hqb = s3.enter_context(nc.sbuf_tensor("hqb", [128, 2, DI], BF16))
            hqT = s3.enter_context(nc.sbuf_tensor("hqT", [128, DI], BF16))
            outb = s3.enter_context(nc.sbuf_tensor("outb", [128, D], F32))
            p_g = s3.enter_context(nc.psum_tensor("p_g", [128, 1024], F32))
            p_u = s3.enter_context(nc.psum_tensor("p_u", [128, 1024], F32))
            p_d = s3.enter_context(nc.psum_tensor("p_d", [128, 1024], F32))
            p_ht = s3.enter_context(nc.psum_tensor("p_ht", [128, 1024], F32))

            tload_units = []
            for u, unit in enumerate(UNITS):
                slot = wqall[u].rearrange("(c p j) -> p c j", p=128, j=128)
                if unit[0] == "g":
                    dstu = wgT[:, :, unit[1] * 128:(unit[1] + 1) * 128]
                    ntr = 8
                elif unit[0] == "u":
                    dstu = wuT[:, :, unit[1] * 128:(unit[1] + 1) * 128]
                    ntr = 8
                else:
                    _, dcn, ssn = unit
                    ntr = 6 if ssn == 2 else 8
                    dstu = wdT[:, ssn * 8:ssn * 8 + ntr,
                               dcn * 128:(dcn + 1) * 128]
                tload_units.append(P.op(
                    "sp", lambda e, d=dstu, s2=slot, ntr=ntr: e.dma_start(
                        out=d, in_=s2[:, 0:ntr, :]), [tcoll], dma=True))
            twg = twu = twd = tload_units[-1]

            nchunks = [(jx * 512, min(512, DI - jx * 512))
                       for jx in range((DI + 511) // 512)]
            xq3_free = [[], []]
            xt_free3 = [[], []]
            ao_in_free = [[], []]
            pg_free = [[], []]
            pu_free = [[], []]
            hslot_free = [[], []]
            hqb_war = [[], []]
            hqT_free = []
            pht_free = [[], []]
            pd_free = []
            out_free = []
            saved3 = {}

            ld3 = {}

            def emit_p3_loads(i):
                sl = i % 2
                t1_ = P.op("sp", lambda e, i=i, sl=sl: e.dma_start(
                    out=xqT3[:, sl],
                    in_=xqt_s[i].rearrange("c p j -> p c j")),
                    xq3_free[sl], dma=True)
                xq3_free[sl] = []
                t2_ = P.op("sp", lambda e, i=i, sl=sl: e.dma_start(
                    out=aoin[:, sl], in_=ao_s[i]), ao_in_free[sl],
                    dma=True)
                ao_in_free[sl] = []
                t3_ = P.op("sp", lambda e, i=i, sl=sl: e.dma_start(
                    out=xts(sl), in_=x_d[i * 128:(i + 1) * 128, :]),
                    xt_free3[sl], dma=True)
                xt_free3[sl] = []
                ld3[i] = (t1_, t2_, t3_)

            emit_p3_loads(0)
            for it in range(NT + 1):
                if it < NT:
                    i = it
                    sl = i % 2
                    tqld, taold, txld = ld3.pop(i)
                    tqh = []
                    tu = None
                    for ci, (n0, w) in enumerate(nchunks):
                        par = ci % 2
                        hs = hs2[:, par, 0:w]
                        pg = p_g[:, par * 512:par * 512 + w]
                        pu = p_u[:, par * 512:par * 512 + w]
                        tg = None
                        for di in range(ND):
                            tg = P.op("pe", lambda e, di=di, n0=n0, w=w,
                                      sl=sl, pg=pg: e.matmul(
                                pg, xqT3[:, sl, di, :],
                                wgT[:, di, n0:n0 + w],
                                start=(di == 0), stop=(di == ND - 1)),
                                ([tqld, twg] + pg_free[par])
                                if di == 0 else [], inc=(di == ND - 1))
                        pg_free[par] = []
                        for di in range(ND):
                            tu = P.op("pe", lambda e, di=di, n0=n0, w=w,
                                      sl=sl, pu=pu: e.matmul(
                                pu, xqT3[:, sl, di, :],
                                wuT[:, di, n0:n0 + w],
                                start=(di == 0), stop=(di == ND - 1)),
                                ([twu] + pu_free[par]) if di == 0 else [],
                                inc=(di == ND - 1))
                        pu_free[par] = []
                        tsl_ = P.op("act", lambda e, hs=hs, pg=pg:
                                    e.activation(hs, pg, AF.Silu),
                                    [tg] + hslot_free[par])
                        hslot_free[par] = []
                        tmul = P.op("dve", lambda e, hs=hs, pu=pu:
                                    e.tensor_tensor(hs, hs, pu, ALU.mult),
                                    [tsl_, tu])
                        pg_free[par].append(tsl_)
                        pu_free[par].append(tmul)
                        tq, sfree = emit_quant(
                            hs, hqb[:, sl, n0:n0 + w], w, [tmul],
                            dst_war=list(hqb_war[sl]))
                        hslot_free[par] = sfree
                        tqh.append(tq)
                    hqb_war[sl] = []
                    xq3_free[sl].append(tu)
                    if "hq0" in taps and i == 0:
                        tp = tap("hq0", [128, DI], BF16)
                        P.op("sp", lambda e, tp=tp, sl=sl: e.dma_start(
                            out=tp, in_=hqb[:, sl]), tqh, dma=True)
                    saved3[i] = dict(tqh=tqh, taold=taold, txld=txld, sl=sl)

                if it >= 1:
                    j = it - 1
                    d = saved3.pop(j)
                    sj = d["sl"]
                    for jg in range((NI + 3) // 4):
                        j0 = jg * 4
                        j1 = min(NI, j0 + 4)
                        par = jg % 2
                        tt = None
                        for jj in range(j0, j1):
                            tt = P.op("pe", lambda e, jj=jj, j0=j0, par=par,
                                      sj=sj: e.matmul(
                                p_ht[:, par * 512 + (jj - j0) * 128:
                                     par * 512 + (jj - j0 + 1) * 128],
                                hqb[:, sj, jj * 128:(jj + 1) * 128],
                                ident[:], start=True, stop=True),
                                (d["tqh"] + pht_free[par]) if jj == j0 else [],
                                inc=(jj == j1 - 1))
                        pht_free[par] = []
                        tev = P.op("act", lambda e, j0=j0, j1=j1, par=par:
                                   e.activation(
                            hqT[:, j0 * 128:j1 * 128],
                            p_ht[:, par * 512:par * 512 + (j1 - j0) * 128],
                            AF.Copy), [tt] + hqT_free)
                        hqT_free = []
                        pht_free[par].append(tev)
                    hqb_war[sj] = [P.token("pe")]
                    thqT = P.token("act")
                    tdm = None
                    for half in range(2):
                        for ic in range(NI):
                            tdm = P.op("pe", lambda e, ic=ic, half=half:
                                       e.matmul(
                                p_d[:, half * 512:(half + 1) * 512],
                                hqT[:, ic * 128:(ic + 1) * 128],
                                wdT[:, ic, half * 512:(half + 1) * 512],
                                start=(ic == 0), stop=(ic == NI - 1)),
                                ([thqT, twd] + pd_free) if ic == 0 else [],
                                inc=(ic == NI - 1))
                        pd_free = []
                    hqT_free.append(tdm)
                    tc1 = P.op("dve", lambda e, sj=sj: e.scalar_tensor_tensor(
                        outb[:], p_d[:], 1.0, aoin[:, sj], ALU.mult, ALU.add),
                        [tdm, d["taold"]] + out_free)
                    out_free = []
                    tc2 = P.op("dve", lambda e, sj=sj: e.tensor_tensor(
                        outb[:], outb[:], xts(sj), ALU.add),
                        [d["txld"], tc1])
                    pd_free.append(tc1)
                    ao_in_free[sj].append(tc1)
                    xt_free3[sj].append(tc2)
                    tst = P.op("sp", lambda e, j=j: e.dma_start(
                        out=out_d[j * 128:(j + 1) * 128, :], in_=outb[:]),
                        [tc2], dma=True)
                    out_free.append(tst)
                if it + 1 < NT:
                    emit_p3_loads(it + 1)

            tfinal = P.token("sp.d")
            P.op("sp", lambda e: e.nop(),
                 [("pe", P.cnt["pe"]), ("dve", P.cnt["dve"]), tfinal],
                 inc=False)

        with nc.Block() as block:
            block.sync(P.replay("sp"))
            block.tensor(P.replay("pe"))
            block.vector(P.replay("dve"))
            block.scalar(P.replay("act"))
            block.gpsimd(P.replay("gp"))

    return nc, tapout


def _layout_inputs(L0_attn_x, L0_attn_Wq, L0_attn_bq, L0_attn_Wk, L0_attn_bk,
                   L0_attn_Wv, L0_attn_bv, L0_attn_Wo, L0_attn_bo,
                   L0_ffn_w_gate, L0_ffn_w_up, L0_ffn_w_down):
    x = np.asarray(L0_attn_x, dtype=np.float32)
    bq = np.asarray(L0_attn_bq, np.float32).reshape(8, 128).T
    bk = np.asarray(L0_attn_bk, np.float32).reshape(8, 128).T
    bqk = np.ascontiguousarray(np.concatenate([bq, bk], axis=1))
    common = {
        "wq": np.ascontiguousarray(np.asarray(L0_attn_Wq, np.float32)),
        "wk": np.ascontiguousarray(np.asarray(L0_attn_Wk, np.float32)),
        "wv": np.ascontiguousarray(np.asarray(L0_attn_Wv, np.float32)),
        "wo": np.ascontiguousarray(np.asarray(L0_attn_Wo, np.float32)),
        "bqk": bqk,
        "bv": np.ascontiguousarray(
            np.asarray(L0_attn_bv, np.float32).reshape(1, D)),
        "bo": np.ascontiguousarray(
            np.asarray(L0_attn_bo, np.float32).reshape(1, D)),
        "wg": np.ascontiguousarray(np.asarray(L0_ffn_w_gate, np.float32)),
        "wu": np.ascontiguousarray(np.asarray(L0_ffn_w_up, np.float32)),
        "wd": np.ascontiguousarray(np.asarray(L0_ffn_w_down, np.float32)),
    }
    in_maps = []
    for c in range(8):
        m = dict(common)
        m["x"] = np.ascontiguousarray(x[c])
        in_maps.append(m)
    return in_maps


def kernel(L0_attn_x, L0_attn_Wq, L0_attn_bq, L0_attn_Wk, L0_attn_bk,
           L0_attn_Wv, L0_attn_bv, L0_attn_Wo, L0_attn_bo,
           L0_ffn_w_gate, L0_ffn_w_up, L0_ffn_w_down):
    in_maps = _layout_inputs(L0_attn_x, L0_attn_Wq, L0_attn_bq, L0_attn_Wk,
                             L0_attn_bk, L0_attn_Wv, L0_attn_bv, L0_attn_Wo,
                             L0_attn_bo, L0_ffn_w_gate, L0_ffn_w_up,
                             L0_ffn_w_down)
    nc, _ = build_program()
    res = run_bass_kernel_spmd(nc, in_maps, core_ids=list(range(8)))
    return np.stack([np.asarray(r["out"], dtype=np.float32)
                     for r in res.results], axis=0)


# version 19
# speedup vs baseline: 1.2340x; 1.2014x over previous
import numpy as np
import ml_dtypes
from contextlib import ExitStack

import concourse.bass as bass
import concourse.mybir as mybir
from concourse.bass_utils import run_bass_kernel_spmd

F32 = mybir.dt.float32
BF16 = mybir.dt.bfloat16
AF = mybir.ActivationFunctionType
ALU = mybir.AluOpType
AX = mybir.AxisListType

S = 4096
D = 1024
DI = 2816
NT = S // 128
ND = D // 128
NI = DI // 128
G = 4
CRND = float(np.float32(1.5 * 2 ** 23))
UNITS = ([("g", n) for n in range(22)] + [("u", n) for n in range(22)] +
         [("d", dc, ss) for dc in range(8) for ss in range(3)])
NEGBIG = -320000.0
BF = ml_dtypes.bfloat16


class Prog:
    ENGS = ("sp", "pe", "dve", "act", "gp")

    def __init__(self, nc, ctx):
        self.nc = nc
        self.sems = {}
        for ek in self.ENGS:
            self.sems[ek] = ctx.enter_context(nc.semaphore("sem_" + ek))
        for ek in ("sp", "gp"):
            self.sems[ek + ".d"] = ctx.enter_context(
                nc.semaphore("sem_" + ek + "d"))
        self.barsem = ctx.enter_context(nc.semaphore("sem_bar"))
        self.cnt = dict.fromkeys(self.sems, 0)
        self.barcnt = 0
        self.rec = {ek: [] for ek in self.ENGS}

    def op(self, ek, fn, waits=(), dma=False, inc=True, strict=False,
           drain=False):
        key = ek + ".d" if dma else ek
        ws = []
        for t in waits:
            if t is None:
                continue
            if t[0] == ek and not strict:
                continue
            ws.append(t)
        self.rec[ek].append((fn, ws, key, dma, inc, drain))
        if inc:
            self.cnt[key] += 16 if dma else 1
            return (key, self.cnt[key])
        return None

    def barrier(self):
        self.barcnt += len(self.ENGS)
        tgt = self.barcnt
        for ek in self.ENGS:
            self.rec[ek].append(("bar", tgt, None, None, None, None))

    def token(self, key):
        return (key, self.cnt[key])

    def replay(self, ek):
        sems = self.sems

        def go(eng):
            waited = {}
            for entry in self.rec[ek]:
                if entry[0] == "bar":
                    eng.sem_inc(self.barsem, 1)
                    eng.wait_ge(self.barsem, entry[1])
                    continue
                fn, ws, key, dmaf, inc, drn = entry
                for (sk, val) in ws:
                    if waited.get(sk, 0) < val:
                        eng.wait_ge(sems[sk], val)
                        waited[sk] = val
                ins = fn(eng)
                if drn:
                    ins = eng.drain()
                if inc:
                    ins.then_inc(sems[key], 16 if dmaf else 1)
        return go


def build_program(taps=()):
    nc = bass.Bass("TRN2", target_bir_lowering=False)
    taps = set(taps)

    x_d = nc.dram_tensor("x", [S, D], F32, kind="ExternalInput").ap()
    wq_d = nc.dram_tensor("wq", [D, D], F32, kind="ExternalInput").ap()
    wk_d = nc.dram_tensor("wk", [D, D], F32, kind="ExternalInput").ap()
    wv_d = nc.dram_tensor("wv", [D, D], F32, kind="ExternalInput").ap()
    wo_d = nc.dram_tensor("wo", [D, D], F32, kind="ExternalInput").ap()
    bqk_d = nc.dram_tensor("bqk", [128, 16], F32, kind="ExternalInput").ap()
    bv_d = nc.dram_tensor("bv", [1, D], F32, kind="ExternalInput").ap()
    bo_d = nc.dram_tensor("bo", [1, D], F32, kind="ExternalInput").ap()
    wg_d = nc.dram_tensor("wg", [DI, D], F32, kind="ExternalInput").ap()
    wu_d = nc.dram_tensor("wu", [DI, D], F32, kind="ExternalInput").ap()
    wd_d = nc.dram_tensor("wd", [D, DI], F32, kind="ExternalInput").ap()
    out_d = nc.dram_tensor("out", [S, D], F32, kind="ExternalOutput").ap()

    qt_s = nc.dram_tensor("qt_s", [NT, ND, 128, 128], BF16, kind="Internal").ap()
    xqt_s = nc.dram_tensor("xqt_s", [NT, ND, 128, 128], BF16, kind="Internal").ap()
    xnt_s = nc.dram_tensor("xnt_s", [NT, ND, 128, 128], BF16, kind="Internal").ap()
    ao_s = nc.dram_tensor("ao_s", [NT, 128, D], BF16, kind="Internal").ap()
    wqall = nc.dram_tensor("wqall", [68, 8 * 128 * 128], BF16,
                           kind="Internal").ap()

    ident_c = nc.inline_tensor(np.eye(128, dtype=BF), name="identc").ap()
    mrows = np.arange(128)[:, None]
    mcols = np.arange(128)[None, :]
    mask_np = np.where(mcols > mrows, 0.0, 1.0).astype(BF)
    mask_c = nc.inline_tensor(mask_np, name="maskc").ap()
    ones_c = nc.inline_tensor(np.ones((1, 128), dtype=BF), name="onesc").ap()

    tapout = {}

    def tap(name, shape, dt):
        tapout[name] = nc.dram_tensor("tap_" + name, shape, dt,
                                      kind="ExternalOutput").ap()
        return tapout[name]

    g = ExitStack()
    with g:
        P = Prog(nc, g)

        def sb(n, sh, dt):
            return g.enter_context(nc.sbuf_tensor(n, sh, dt))

        ident = sb("ident", [128, 128], BF16)
        mask = sb("mask", [128, 128], BF16)
        ones1 = sb("ones1", [1, 128], BF16)
        bqk = sb("bqk_t", [128, 16], F32)
        bvo = sb("bvo", [1, 2 * D], BF16)
        bvr = bvo[:, 0:D]
        bor = bvo[:, D:2 * D]
        xt = sb("xt", [128, 2 * D], F32)
        QW = 512
        qse = []
        for si in range(2):
            qse.append({
                "a": sb(f"q_a{si}", [128, QW], F32),
                "t1": sb(f"q_t1{si}", [128, QW], F32),
                "r1": sb(f"q_r1{si}", [128, QW], BF16),
                "hi": sb(f"q_hi{si}", [128, QW], BF16),
                "q2": sb(f"q_q2{si}", [128, QW], BF16),
                "sg": sb(f"q_sg{si}", [128, QW], BF16),
                "bmax": sb(f"q_bmax{si}", [128, 32], F32),
                "scl": sb(f"q_scl{si}", [128, 32], F32),
                "rcp": sb(f"q_rcp{si}", [128, 32], F32),
            })
        sm = sb("sm", [128, 16], F32)
        mx = sb("mx", [128, 8], F32)
        negmc = sb("negmc", [128, 8], F32)
        sums = sb("sums", [128, 8], F32)
        fe = sb("fe", [128, 8], F32)
        sums2 = sb("sums2", [128, 8], F32)
        kbias = sb("kbias", [128, 4], F32)

        def xts(sl):
            return xt[:, sl * D:(sl + 1) * D]

        t_ident = P.op("sp", lambda e: e.dma_start(out=ident[:], in_=ident_c),
                       dma=True)
        t_mask = P.op("sp", lambda e: e.dma_start(out=mask[:], in_=mask_c),
                      dma=True)
        t_ones = P.op("sp", lambda e: e.dma_start(out=ones1[:], in_=ones_c),
                      dma=True)
        t_bqk = P.op("sp", lambda e: e.dma_start(out=bqk[:], in_=bqk_d),
                     dma=True)
        t_bv = P.op("gp", lambda e: e.dma_start(out=bvr, in_=bv_d), dma=True)
        for j, val in enumerate((-5.0, -7.0, -10.0)):
            P.op("dve", lambda e, j=j, v=val: e.memset(kbias[:, j:j + 1], v),
                 drain=(j == 2))
        t_kb = P.token("dve")
        t_bo = P.op("gp", lambda e: e.dma_start(out=bor, in_=bo_d), dma=True)
        bqk32 = sb("bqk32", [128, 8], F32)
        t_b32 = P.op("dve", lambda e: e.tensor_scalar(
            bqk32[:], bqk[:, 0:8], 1.0 / 32.0, None, ALU.mult), [t_bqk],
            drain=True)

        last_q = [None, None]
        qctr = [0]

        def emit_quant_gen(src_ap, dst_ap, w, waits, dst_war, res, si):
            T = qse[si]
            nb = w // 16
            src3 = src_ap.rearrange("p (a b) -> p a b", b=16)
            s2b3 = T["scl"][:, 0:nb].unsqueeze(2).broadcast_to([128, nb, 16])
            rcb3 = T["rcp"][:, 0:nb].unsqueeze(2).broadcast_to([128, nb, 16])
            a2 = T["a"][:, 0:w]
            a3 = a2.rearrange("p (a b) -> p a b", b=16)
            w0 = list(waits) + ([last_q[si]] if last_q[si] else [])
            tA = P.op("act", lambda e: e.activation(a2, src_ap, AF.Abs), w0)
            yield
            tSG = P.op("act", lambda e: e.sign(T["sg"][:, 0:w], src_ap), w0)
            yield
            tR = P.op("dve", lambda e: e.tensor_reduce(
                T["bmax"][:, 0:nb], src3, AX.X, ALU.max,
                apply_absolute_value=True), w0)
            yield
            tsc = P.op("dve", lambda e: e.tensor_scalar(
                T["scl"][:, 0:nb], T["bmax"][:, 0:nb], 1.0 / 12.0, 5e-13,
                ALU.mult, ALU.max), [tR], strict=True)
            yield
            trc = P.op("dve", lambda e: e.reciprocal(
                T["rcp"][:, 0:nb], T["scl"][:, 0:nb]), [tsc], strict=True)
            yield
            tan = P.op("dve", lambda e: e.tensor_tensor(a3, a3, rcb3, ALU.mult),
                       [tA, trc], strict=True)
            yield
            tu1 = P.op("dve", lambda e: e.tensor_scalar(
                T["t1"][:, 0:w], a2, 4.0, CRND, ALU.min, ALU.add), [tan])
            yield
            tr1 = P.op("dve", lambda e: e.tensor_scalar(
                T["r1"][:, 0:w], T["t1"][:, 0:w], CRND - 4.0, None,
                ALU.subtract), [tu1])
            yield
            th1 = P.op("act", lambda e: e.sign(
                T["hi"][:, 0:w], a2, bias=kbias[:, 0:1]), [tan, t_kb])
            yield
            th2 = P.op("act", lambda e: e.sign(
                T["q2"][:, 0:w], a2, bias=kbias[:, 1:2]), [th1])
            yield
            tt1 = P.op("dve", lambda e: e.tensor_tensor(
                T["hi"][:, 0:w], T["hi"][:, 0:w], T["q2"][:, 0:w], ALU.add),
                [th1, th2])
            yield
            th3 = P.op("act", lambda e: e.sign(
                T["q2"][:, 0:w], a2, bias=kbias[:, 2:3]), [th2, tt1])
            yield
            tt2 = P.op("dve", lambda e: e.tensor_tensor(
                T["r1"][:, 0:w], T["r1"][:, 0:w], T["hi"][:, 0:w], ALU.add),
                [tt1, tr1])
            yield
            tt3 = P.op("dve", lambda e: e.tensor_scalar(
                T["q2"][:, 0:w], T["q2"][:, 0:w], 2.0, None, ALU.mult), [th3])
            yield
            tt4 = P.op("dve", lambda e: e.tensor_tensor(
                T["r1"][:, 0:w], T["r1"][:, 0:w], T["q2"][:, 0:w], ALU.add),
                [tt3, tt2])
            yield
            q13 = T["r1"][:, 0:w].rearrange("p (a b) -> p a b", b=16)
            t13 = T["t1"][:, 0:w].rearrange("p (a b) -> p a b", b=16)
            tqs = P.op("dve", lambda e: e.tensor_tensor(
                t13, q13, s2b3, ALU.mult), [tt4, tsc], strict=True)
            yield
            tdst = P.op("gp", lambda e: e.tensor_tensor(
                dst_ap, T["t1"][:, 0:w], T["sg"][:, 0:w], ALU.mult),
                [tqs, tSG] + list(dst_war))
            last_q[si] = tdst
            res["tdst"] = tdst
            res["sf"] = [tA, tSG, tR]

        def emit_quant_multi(src_ap, dst_ap, w, waits, dst_war=()):
            gens = []
            outs = []
            o = 0
            while o < w:
                ww = min(512, w - o)
                si = qctr[0] % 2
                qctr[0] += 1
                res = {}
                gens.append(emit_quant_gen(
                    src_ap[:, o:o + ww], dst_ap[:, o:o + ww], ww, waits,
                    dst_war, res, si))
                outs.append(res)
                o += ww
            for gn in gens:
                for _ in gn:
                    pass
            toks = [r["tdst"] for r in outs]
            sfree = [t for r in outs for t in r["sf"]]
            return toks, sfree

        def emit_quant(src_ap, dst_ap, w, waits, dst_war=()):
            toks, sfree = emit_quant_multi(src_ap, dst_ap, w, waits, dst_war)
            return toks[-1], sfree

        s12a = ExitStack()
        kT = s12a.enter_context(nc.sbuf_tensor("kT", [128, ND, S], BF16))

        with ExitStack() as s1:
            wq = s1.enter_context(nc.sbuf_tensor("wq_t", [128, ND, D], BF16))
            wk = s1.enter_context(nc.sbuf_tensor("wk_t", [128, ND, D], BF16))
            xn_f = s1.enter_context(nc.sbuf_tensor("xn_f", [128, 2, D], F32))
            xn_b = s1.enter_context(nc.sbuf_tensor("xn_b", [128, D], BF16))
            xq_b = s1.enter_context(nc.sbuf_tensor("xq_b", [128, D], BF16))
            xnT = s1.enter_context(
                nc.sbuf_tensor("xnT", [128, ND, G * 128], BF16))
            qTb = s1.enter_context(
                nc.sbuf_tensor("qTb", [128, ND, G * 128], BF16))
            xqT = s1.enter_context(nc.sbuf_tensor("xqT", [128, ND * 128], BF16))
            p_tr = s1.enter_context(nc.psum_tensor("p_tr", [128, 2048], F32))
            p_q = s1.enter_context(nc.psum_tensor("p_q", [128, 1024], F32))
            p_k = s1.enter_context(nc.psum_tensor("p_k", [128, 1024], F32))

            t_wq = P.op("gp", lambda e: e.dma_start(
                out=wq[:], in_=wq_d.rearrange("(k p) n -> p k n", p=128)),
                dma=True)
            t_wk = P.op("gp", lambda e: e.dma_start(
                out=wk[:], in_=wk_d.rearrange("(k p) n -> p k n", p=128)),
                dma=True)

            xt_free = [[], []]
            xnT_free = [[] for _ in range(G)]
            xqT_free = []
            qTb_free = []
            kq_psum_free = [[], []]
            tr_free_xn = []
            tr_free_xq = []
            prev_ttr = None
            prev_ttq = None
            xnf_free = [[], []]
            tvar_prev = None
            tx_tok = {}

            def p1_load(i):
                sl = i % 2
                t = P.op("sp", lambda e, i=i, sl=sl: e.dma_start(
                    out=xts(sl), in_=x_d[i * 128:(i + 1) * 128, :]),
                    xt_free[sl], dma=True)
                xt_free[sl] = []
                tx_tok[i] = t

            p1_load(0)
            for i in range(NT):
                b, gi = divmod(i, G)
                sl = i % 2
                xs = xts(sl)
                xnf = xn_f[:, sl]
                tx = tx_tok.pop(i)
                tsum = P.op("dve", lambda e, xs=xs: e.reduce_sum(
                    sm[:, 0:1], xs, AX.X), [tx])
                tssq = P.op("act", lambda e, xs=xs, xnf=xnf: e.activation(
                    xnf, xs, AF.Square, accum_out=sm[:, 1:2]),
                    [tx, tvar_prev] + xnf_free[sl])
                xnf_free[sl] = []
                tmean = P.op("dve", lambda e: e.tensor_scalar(
                    sm[:, 2:3], sm[:, 0:1], 1.0 / D, None, ALU.mult), [tsum],
                    strict=True)
                tm2 = P.op("dve", lambda e: e.tensor_scalar(
                    sm[:, 10:11], sm[:, 2:3], sm[:, 2:3], -1e-5,
                    ALU.mult, ALU.add), [tmean], strict=True)
                tvar = P.op("dve", lambda e: e.scalar_tensor_tensor(
                    sm[:, 3:4], sm[:, 1:2], 1.0 / D, sm[:, 10:11],
                    ALU.mult, ALU.subtract), [tssq, tm2], strict=True)
                tvar_prev = tvar
                tsd = P.op("act", lambda e: e.activation(
                    sm[:, 4:5], sm[:, 3:4], AF.Ln), [tvar])
                trstd = P.op("act", lambda e: e.activation(
                    sm[:, 5:6], sm[:, 4:5], AF.Exp, scale=-0.5), [tsd],
                    strict=True, drain=True)
                txn = P.op("dve", lambda e, xs=xs, xnf=xnf: e.tensor_scalar(
                    xnf, xs, sm[:, 2:3], sm[:, 5:6],
                    ALU.subtract, ALU.mult), [trstd])
                xt_free[sl].append(tssq)
                xt_free[sl].append(txn)
                if i + 1 < NT:
                    p1_load(i + 1)
                txnb = P.op("act", lambda e, xnf=xnf: e.activation(
                    xn_b[:], xnf, AF.Copy), [txn, prev_ttr])
                txq_l, txq_sf = emit_quant_multi(
                    xnf, xq_b[:], D, [txn, txnb],
                    dst_war=[prev_ttq] if prev_ttq else ())
                xnf_free[sl] = list(txq_sf) + [txnb]
                txq = txq_l[-1]
                if "xn0" in taps and i == 0:
                    tp = tap("xn0", [128, D], F32)
                    P.op("sp", lambda e, tp=tp, xnf=xnf: e.dma_start(
                        out=tp, in_=xnf), [txn, txq], dma=True)
                if "xq0" in taps and i == 0:
                    tp = tap("xq0", [128, D], BF16)
                    P.op("sp", lambda e, tp=tp: e.dma_start(out=tp, in_=xq_b[:]),
                         [txq], dma=True)
                for dc in range(ND):
                    prev_ttr = P.op("pe", lambda e, dc=dc: e.matmul(
                        p_tr[:, dc * 128:(dc + 1) * 128],
                        xn_b[:, dc * 128:(dc + 1) * 128], ident[:],
                        start=True, stop=True),
                        ([txnb, t_ident] + tr_free_xn) if dc == 0 else [],
                        inc=(dc == ND - 1))
                tr_free_xn = []
                for dc in range(ND):
                    prev_ttq = P.op("pe", lambda e, dc=dc: e.matmul(
                        p_tr[:, 1024 + dc * 128:1024 + (dc + 1) * 128],
                        xq_b[:, dc * 128:(dc + 1) * 128], ident[:],
                        start=True, stop=True),
                        (txq_l + tr_free_xq) if dc == 0 else [],
                        inc=(dc == ND - 1))
                tr_free_xq = []
                xnT_dst = xnT[:, :, gi * 128:(gi + 1) * 128]
                tevn = P.op("act", lambda e, d=xnT_dst: e.activation(
                    d, p_tr[:, 0:1024].rearrange("p (c j) -> p c j", j=128),
                    AF.Copy), [prev_ttr] + xnT_free[gi])
                xnT_free[gi] = []
                tr_free_xn.append(tevn)
                tevq = P.op("act", lambda e: e.activation(
                    xqT[:], p_tr[:, 1024:2048], AF.Copy),
                    [prev_ttq] + xqT_free)
                xqT_free = []
                tr_free_xq.append(tevq)
                tso = P.op("sp", lambda e, i=i: e.dma_start(
                    out=xqt_s[i].rearrange("c p j -> p c j"),
                    in_=xqT[:].rearrange("p (c j) -> p c j", j=128)),
                    [tevq], dma=True)
                xqT_free.append(tso)

                if gi == G - 1:
                    for gg in range(G):
                        tsn = P.op("sp", lambda e, b=b, gg=gg: e.dma_start(
                            out=xnt_s[b * G + gg].rearrange("c p j -> p c j"),
                            in_=xnT[:, :, gg * 128:(gg + 1) * 128]),
                            [tevn], dma=True)
                        xnT_free[gg].append(tsn)
                    for proj in range(2):
                        wmat = wq if proj == 0 else wk
                        twm = t_wq if proj == 0 else t_wk
                        pp = p_q if proj == 0 else p_k
                        for do in range(ND):
                            par = do % 2
                            pps = pp[:, par * 512:(par + 1) * 512]
                            tmm = None
                            for di in range(ND):
                                tmm = P.op("pe", lambda e, di=di, do=do,
                                           w=wmat, pp2=pps: e.matmul(
                                    pp2, w[:, di, do * 128:(do + 1) * 128],
                                    xnT[:, di, :],
                                    start=(di == 0), stop=(di == ND - 1)),
                                    ([tevn, twm] + kq_psum_free[par])
                                    if di == 0 else [], inc=(di == ND - 1))
                            kq_psum_free[par] = []
                            if proj == 0:
                                dst = qTb[:, do, :]
                                tev = P.op("act", lambda e, d=dst, pp2=pps,
                                           bc=do: e.activation(
                                    d, pp2, AF.Relu, bias=bqk32[:, bc:bc + 1],
                                    scale=1.0 / 32.0),
                                    [tmm, t_b32] + qTb_free)
                            else:
                                dst = kT[:, do, b * 512:(b + 1) * 512]
                                tev = P.op("act", lambda e, d=dst, pp2=pps,
                                           bc=8 + do: e.activation(
                                    d, pp2, AF.Relu, bias=bqk[:, bc:bc + 1]),
                                    [tmm, t_bqk])
                            if proj == 0:
                                qTb_free = []
                            kq_psum_free[par].append(tev)
                    tact_b = P.token("act")
                    for gg in range(G):
                        tqs_st = P.op("sp", lambda e, b=b, gg=gg: e.dma_start(
                            out=qt_s[b * G + gg].rearrange("c p j -> p c j"),
                            in_=qTb[:, :, gg * 128:(gg + 1) * 128]),
                            [tact_b], dma=True)
                        qTb_free.append(tqs_st)
                    tpe_batch = P.token("pe")
                    for gg in range(G):
                        xnT_free[gg].append(tpe_batch)

            if "kT" in taps:
                tp = tap("kT", [128, ND * S], BF16)
                P.op("sp", lambda e, tp=tp: e.dma_start(
                    out=tp, in_=kT[:].rearrange("p c s -> p (c s)")),
                    [P.token("act")], dma=True)

        P.barrier()

        s12b = ExitStack()
        vS = s12b.enter_context(nc.sbuf_tensor("vS", [128, NT * D], BF16))

        with ExitStack() as s1b:
            wv = s1b.enter_context(nc.sbuf_tensor("wv_t", [128, ND, D], BF16))
            xnT_in = s1b.enter_context(
                nc.sbuf_tensor("xnT_in", [128, 2, ND, 128], BF16))
            p_v = s1b.enter_context(nc.psum_tensor("p_v", [128, 2048], F32))

            t_wv = P.op("gp", lambda e: e.dma_start(
                out=wv[:], in_=wv_d.rearrange("(k p) n -> p k n", p=128)),
                dma=True)
            xnTin_free = [[], []]
            pv_free = [[], []]
            xnb_tok = {}

            def p1b_load(i):
                sl = i % 2
                t = P.op("sp", lambda e, i=i, sl=sl: e.dma_start(
                    out=xnT_in[:, sl],
                    in_=xnt_s[i].rearrange("c p j -> p c j")),
                    xnTin_free[sl], dma=True)
                xnTin_free[sl] = []
                xnb_tok[i] = t

            p1b_load(0)
            for i in range(NT):
                sl = i % 2
                tld = xnb_tok.pop(i)
                if i + 1 < NT:
                    p1b_load(i + 1)
                pvh = p_v[:, sl * 1024:(sl + 1) * 1024]
                tmm = None
                for half in range(2):
                    ph = pvh[:, half * 512:(half + 1) * 512]
                    for di in range(ND):
                        P.op("pe", lambda e, di=di, sl=sl, half=half, ph=ph:
                             e.matmul(
                            ph, xnT_in[:, sl, di, :],
                            wv[:, di, half * 512:(half + 1) * 512],
                            start=(di == 0), stop=False),
                            ([tld, t_wv, t_ones, t_bv] + pv_free[sl])
                            if (di == 0 and half == 0) else [], inc=False)
                    tmm = P.op("pe", lambda e, half=half, ph=ph: e.matmul(
                        ph, ones1[:], bvr[:, half * 512:(half + 1) * 512],
                        start=False, stop=True), [])
                pv_free[sl] = []
                tev = P.op("act", lambda e, i=i, pv2=pvh: e.activation(
                    vS[:, i * D:(i + 1) * D], pv2, AF.Relu), [tmm])
                pv_free[sl].append(tev)
                xnTin_free[sl].append(tev)

            if "V" in taps:
                tp = tap("V", [128, NT * D], BF16)
                P.op("sp", lambda e, tp=tp: e.dma_start(out=tp, in_=vS[:]),
                     [P.token("act")], dma=True)

        P.barrier()

        with ExitStack() as s2:
            wo = s2.enter_context(nc.sbuf_tensor("wo_t", [128, ND, D], BF16))
            qT = s2.enter_context(
                nc.sbuf_tensor("qT_t", [128, 2, ND, 128], BF16))
            Psb = s2.enter_context(nc.sbuf_tensor("Psb", [128, 2 * S], BF16))
            PTsb = s2.enter_context(nc.sbuf_tensor("PTsb", [128, 2, 512], BF16))
            yb = s2.enter_context(nc.sbuf_tensor("yb", [128, D], BF16))
            yT = s2.enter_context(nc.sbuf_tensor("yT", [128, D], BF16))
            aosb = s2.enter_context(nc.sbuf_tensor("aosb", [128, D], BF16))
            wqnat = s2.enter_context(nc.sbuf_tensor("wqnat", [128, 2, D], BF16))
            wbounce = s2.enter_context(nc.sbuf_tensor("wbounce", [128, D], BF16))
            p_s = s2.enter_context(nc.psum_tensor("p_s", [128, 1024], F32))
            p_y = s2.enter_context(nc.psum_tensor("p_y", [128, 1024], F32))
            p_pt = s2.enter_context(nc.psum_tensor("p_pt", [128, 1024], F32))
            p_ao = s2.enter_context(nc.psum_tensor("p_ao", [128, 512], F32))
            p_wp = s2.enter_context(nc.psum_tensor("p_wp", [128, 512], F32))

            t_wo = P.op("gp", lambda e: e.dma_start(
                out=wo[:], in_=wo_d.rearrange("(k p) n -> p k n", p=128)),
                dma=True)

            unit_idx = [0]
            prep_xt_free = [[], []]
            prep_psum_free = []
            prep_bounce_free = []
            prep_wq_war = [[] for _ in range(2)]
            prep_done = []
            prep_q = []

            def emit_prep_quant():
                k = unit_idx[0]
                if k >= len(UNITS):
                    return
                unit_idx[0] += 1
                unit = UNITS[k]
                if unit[0] == "g":
                    src_ap = wg_d[unit[1] * 128:(unit[1] + 1) * 128, :]
                    wsrc = D
                elif unit[0] == "u":
                    src_ap = wu_d[unit[1] * 128:(unit[1] + 1) * 128, :]
                    wsrc = D
                else:
                    _, dcn, ssn = unit
                    wsrc = 768 if ssn == 2 else 1024
                    src_ap = wd_d[dcn * 128:(dcn + 1) * 128,
                                  ssn * 1024:ssn * 1024 + wsrc]
                sl = k % 2
                buf = k % 2
                stg = xts(sl)
                tld = P.op("sp", lambda e, d=stg[:, 0:wsrc], s=src_ap:
                           e.dma_start(out=d, in_=s),
                           prep_xt_free[sl], dma=True)
                prep_xt_free[sl] = []
                tq_l, tq_sf = emit_quant_multi(stg[:, 0:wsrc],
                                               wqnat[:, buf, 0:wsrc], wsrc,
                                               [tld],
                                               dst_war=list(prep_wq_war[buf]))
                prep_wq_war[buf] = []
                prep_xt_free[sl].extend(tq_sf)
                prep_q.append(dict(k=k, buf=buf, tq=tq_l, wsrc=wsrc))

            def emit_prep_store():
                if not prep_q:
                    return
                d = prep_q.pop(0)
                k, buf, wsrc = d["k"], d["buf"], d["wsrc"]
                ntr = wsrc // 128
                tev_last = None
                for grp in range((ntr + 3) // 4):
                    j0 = grp * 4
                    j1 = min(ntr, j0 + 4)
                    tt = None
                    for j in range(j0, j1):
                        tt = P.op("pe", lambda e, j=j, j0=j0, buf=buf:
                                  e.matmul(
                            p_wp[:, (j - j0) * 128:(j - j0 + 1) * 128],
                            wqnat[:, buf, j * 128:(j + 1) * 128], ident[:],
                            start=True, stop=True),
                            (d["tq"] + prep_psum_free) if j == j0 else [],
                            inc=(j == j1 - 1))
                    prep_psum_free.clear()
                    tev_last = P.op("act", lambda e, j0=j0, j1=j1:
                                    e.activation(
                        wbounce[:, j0 * 128:j1 * 128],
                        p_wp[:, 0:(j1 - j0) * 128], AF.Copy),
                        [tt] + prep_bounce_free)
                    prep_bounce_free.clear()
                    prep_psum_free.append(tev_last)
                prep_wq_war[buf].append(P.token("pe"))
                tst = P.op("sp", lambda e, k=k, ntr=ntr: e.dma_start(
                    out=wqall[k].rearrange("(c p j) -> p c j", p=128,
                                           j=128)[:, 0:ntr, :],
                    in_=wbounce[:, 0:wsrc].rearrange("p (c j) -> p c j",
                                                     j=128)),
                    [tev_last], dma=True)
                prep_bounce_free.append(tst)
                prep_done.append(tst)

            qT_free = [[], []]
            ps_free = [[], []]
            psb_war = {}
            PT_free = [[], []]
            ppt_free = [[], []]
            py_free = []
            yb_free = []
            yT_free = []
            ao_free = []
            trs_prev = None
            saved = {}
            qld_tok = {}

            def p2_qload(i):
                si = i % 2
                t = P.op("sp", lambda e, i=i, si=si: e.dma_start(
                    out=qT[:, si],
                    in_=qt_s[i].rearrange("c p j -> p c j")),
                    qT_free[si], dma=True)
                qT_free[si] = []
                qld_tok[i] = t

            p2_qload(0)
            for it in range(NT + 1):
                if it < NT:
                    i = it
                    L = 128 * (i + 1)
                    nkc = (L + 511) // 512
                    si = i % 2
                    off = si * S
                    rscol = 9 + 3 * si
                    tqld = qld_tok.pop(i)
                    if i + 1 < NT:
                        p2_qload(i + 1)
                    texp = {}
                    tmm = None
                    for kc in range(nkc):
                        w = min(512, L - kc * 512)
                        par = kc % 2
                        psl = p_s[:, par * 512:par * 512 + w]
                        for dc in range(ND):
                            tmm = P.op("pe", lambda e, dc=dc, kc=kc, w=w,
                                       psl=psl, si=si: e.matmul(
                                psl, qT[:, si, dc, :],
                                kT[:, dc, kc * 512:kc * 512 + w],
                                start=(dc == 0), stop=(dc == ND - 1)),
                                ([tqld] + ps_free[par]) if dc == 0 else [],
                                inc=(dc == ND - 1))
                        ps_free[par] = []
                        tmx = P.op("dve", lambda e, kc=kc, psl=psl:
                                   e.reduce_max(
                            negmc[:, kc:kc + 1], psl, AX.X, negate=True),
                            [tmm], drain=True)
                        tex = P.op("act", lambda e, kc=kc, w=w, psl=psl,
                                   off=off: e.activation(
                            Psb[:, off + kc * 512:off + kc * 512 + w], psl,
                            AF.Exp, bias=negmc[:, kc:kc + 1], scale=1.0,
                            accum_out=sums[:, kc:kc + 1]),
                            [tmx, trs_prev] + psb_war.pop((si, kc), []))
                        if kc == nkc - 1:
                            tz = P.op("dve", lambda e, kc=kc, w=w, off=off:
                                      e.tensor_tensor(
                                Psb[:, off + kc * 512 + w - 128:
                                    off + kc * 512 + w],
                                Psb[:, off + kc * 512 + w - 128:
                                    off + kc * 512 + w],
                                mask[:], ALU.mult), [tex, t_mask])
                            tex = P.op("dve", lambda e, kc=kc, w=w, off=off:
                                       e.reduce_sum(
                                sums[:, kc:kc + 1],
                                Psb[:, off + kc * 512:off + kc * 512 + w],
                                AX.X), [tz], strict=True)
                        texp[kc] = tex
                        ps_free[par].append(tex)
                    qT_free[si] = [tmm]
                    tm1 = P.op("dve", lambda e, nkc=nkc: e.tensor_reduce(
                        sm[:, 7:8], negmc[:, 0:nkc], AX.X, ALU.min),
                        [texp[nkc - 1]], strict=True)
                    tfe = P.op("dve", lambda e, nkc=nkc: e.tensor_scalar(
                        fe[:, 0:nkc], negmc[:, 0:nkc], sm[:, 7:8], -1.0,
                        ALU.subtract, ALU.mult), [tm1], strict=True)
                    tfee = P.op("act", lambda e, nkc=nkc: e.activation(
                        fe[:, 0:nkc], fe[:, 0:nkc], AF.Exp), [tfe],
                        drain=True)
                    tsm2 = P.op("dve", lambda e, nkc=nkc: e.tensor_tensor(
                        sums2[:, 0:nkc], sums[:, 0:nkc], fe[:, 0:nkc],
                        ALU.mult), [tfee], strict=True)
                    tss = P.op("dve", lambda e, nkc=nkc: e.reduce_sum(
                        sm[:, 8:9], sums2[:, 0:nkc], AX.X), [tsm2],
                        strict=True)
                    trs = P.op("dve", lambda e, rscol=rscol: e.reciprocal(
                        sm[:, rscol:rscol + 1], sm[:, 8:9]), [tss],
                        strict=True)
                    trs_prev = trs
                    tresc = {}
                    for kc in range(nkc):
                        w = min(512, L - kc * 512)
                        tresc[kc] = P.op("dve", lambda e, kc=kc, w=w, off=off:
                                         e.tensor_scalar(
                            Psb[:, off + kc * 512:off + kc * 512 + w],
                            Psb[:, off + kc * 512:off + kc * 512 + w],
                            fe[:, kc:kc + 1], None, ALU.mult),
                            [tfee, texp[kc]])
                    if "P1" in taps and i == 1:
                        tp = tap("P1", [128, S], BF16)
                        P.op("sp", lambda e, tp=tp, off=off: e.dma_start(
                            out=tp, in_=Psb[:, off:off + S]),
                            [tresc[kc] for kc in range(nkc)], dma=True)
                    saved[i] = dict(nkc=nkc, nj=L // 128, si=si, off=off,
                                    rscol=rscol, trs=trs, tresc=tresc)

                if it >= 1:
                    j = it - 1
                    d = saved.pop(j)
                    sj, offj, nj = d["si"], d["off"], d["nj"]
                    ngr = (nj + 3) // 4
                    ptev = {}
                    pv_last = [None]

                    def emit_T2(jg):
                        j0 = jg * 4
                        j1 = min(nj, j0 + 4)
                        par = jg % 2
                        tt = None
                        for jj in range(j0, j1):
                            tt = P.op("pe", lambda e, jj=jj, j0=j0, par=par,
                                      offj=offj: e.matmul(
                                p_pt[:, par * 512 + (jj - j0) * 128:
                                     par * 512 + (jj - j0 + 1) * 128],
                                Psb[:, offj + jj * 128:offj + (jj + 1) * 128],
                                ident[:], start=True, stop=True),
                                ([d["tresc"][jg]] + ppt_free[par])
                                if jj == j0 else [], inc=(jj == j1 - 1))
                        ppt_free[par] = []
                        if jg % 2 == 0:
                            tptev = P.op("dve", lambda e, j0=j0, j1=j1,
                                         par=par: e.tensor_copy(
                                PTsb[:, par, 0:(j1 - j0) * 128],
                                p_pt[:, par * 512:
                                     par * 512 + (j1 - j0) * 128]),
                                [tt] + PT_free[par])
                        else:
                            tptev = P.op("act", lambda e, j0=j0, j1=j1,
                                         par=par: e.activation(
                                PTsb[:, par, 0:(j1 - j0) * 128],
                                p_pt[:, par * 512:
                                     par * 512 + (j1 - j0) * 128],
                                AF.Copy), [tt] + PT_free[par])
                        PT_free[par] = []
                        ppt_free[par].append(tptev)
                        ptev[jg] = tptev

                    def emit_PV2(jg):
                        j0 = jg * 4
                        j1 = min(nj, j0 + 4)
                        par = jg % 2
                        tptev = ptev.pop(jg)
                        tpl = None
                        for jj in range(j0, j1):
                            for half in range(2):
                                tpl = P.op("pe", lambda e, jj=jj, j0=j0,
                                           par=par, half=half, nj=nj:
                                           e.matmul(
                                    p_y[:, half * 512:(half + 1) * 512],
                                    PTsb[:, par,
                                         (jj - j0) * 128:(jj - j0 + 1) * 128],
                                    vS[:, jj * D + half * 512:
                                       jj * D + (half + 1) * 512],
                                    start=(jj == 0), stop=(jj == nj - 1)),
                                    [tptev] + (py_free if (jj == 0 and
                                                           half == 0) else []),
                                    inc=(jj == j1 - 1 and half == 1))
                        if jg == 0:
                            py_free.clear()
                        PT_free[par].append(tpl)
                        pv_last[0] = tpl

                    emit_T2(0)
                    for jg in range(1, ngr):
                        emit_T2(jg)
                        emit_PV2(jg - 1)
                    emit_PV2(ngr - 1)
                    tplast = pv_last[0]
                    for kc in range(d["nkc"]):
                        psb_war[(sj, kc)] = [tplast]
                    tyb = P.op("dve", lambda e, rc=d["rscol"]: e.tensor_scalar(
                        yb[:], p_y[:], sm[:, rc:rc + 1], None, ALU.mult),
                        [tplast, d["trs"]] + yb_free, strict=True)
                    yb_free = []
                    py_free = [tyb]
                    if "y1" in taps and j == 1:
                        tp = tap("y1", [128, D], BF16)
                        P.op("sp", lambda e, tp=tp: e.dma_start(
                            out=tp, in_=yb[:]), [tyb], dma=True)
                    tytev = []
                    for grp in range(2):
                        par = grp
                        tt = None
                        for jj in range(4):
                            dc = grp * 4 + jj
                            tt = P.op("pe", lambda e, dc=dc, jj=jj, par=par:
                                      e.matmul(
                                p_pt[:, par * 512 + jj * 128:
                                     par * 512 + (jj + 1) * 128],
                                yb[:, dc * 128:(dc + 1) * 128], ident[:],
                                start=True, stop=True),
                                ([tyb] + ppt_free[par]) if jj == 0 else [],
                                inc=(jj == 3))
                        ppt_free[par] = []
                        tev = P.op("act", lambda e, grp=grp, par=par:
                                   e.activation(
                            yT[:, grp * 512:(grp + 1) * 512],
                            p_pt[:, par * 512:(par + 1) * 512], AF.Copy),
                            [tt] + yT_free)
                        yT_free = []
                        ppt_free[par].append(tev)
                        tytev.append(tev)
                    yb_free = [P.token("pe")]
                    tao_ev = []
                    for half in range(2):
                        for dc in range(ND):
                            P.op("pe", lambda e, dc=dc, half=half: e.matmul(
                                p_ao[:], yT[:, dc * 128:(dc + 1) * 128],
                                wo[:, dc, half * 512:(half + 1) * 512],
                                start=(dc == 0), stop=False),
                                ([t_wo, t_bo] + tytev + ao_free)
                                if dc == 0 else [], inc=False)
                        tmm2 = P.op("pe", lambda e, half=half: e.matmul(
                            p_ao[:], ones1[:],
                            bor[:, half * 512:(half + 1) * 512],
                            start=False, stop=True), [])
                        ao_free = []
                        tev = P.op("act", lambda e, half=half: e.activation(
                            aosb[:, half * 512:(half + 1) * 512], p_ao[:],
                            AF.Relu), [tmm2])
                        ao_free.append(tev)
                        tao_ev.append(tev)
                    yT_free = [P.token("pe")]
                    tao_st = P.op("sp", lambda e, j=j: e.dma_start(
                        out=ao_s[j], in_=aosb[:]), tao_ev, dma=True)
                    ao_free.append(tao_st)
                    if "ao1" in taps and j == 1:
                        tp = tap("ao1", [128, D], BF16)
                        P.op("sp", lambda e, tp=tp: e.dma_start(
                            out=tp, in_=aosb[:]), tao_ev, dma=True)
                if it >= 2:
                    emit_prep_store()
                    emit_prep_store()
                if it >= 1:
                    emit_prep_quant()
                    emit_prep_quant()
            while unit_idx[0] < len(UNITS):
                emit_prep_store()
                emit_prep_quant()
            while prep_q:
                emit_prep_store()
            tcoll = P.token("sp.d")

        P.barrier()
        s12b.close()
        s12a.close()
        P.barrier()

        with ExitStack() as s3:
            wgT = s3.enter_context(nc.sbuf_tensor("wgT", [128, ND, DI], BF16))
            wuT = s3.enter_context(nc.sbuf_tensor("wuT", [128, ND, DI], BF16))
            wdT = s3.enter_context(nc.sbuf_tensor("wdT", [128, NI, D], BF16))
            xqT3 = s3.enter_context(
                nc.sbuf_tensor("xqT3", [128, 2, ND, 128], BF16))
            aoin = s3.enter_context(nc.sbuf_tensor("aoin", [128, 2, D], BF16))
            hs2 = s3.enter_context(nc.sbuf_tensor("hs2", [128, 2, 512], F32))
            hqb = s3.enter_context(nc.sbuf_tensor("hqb", [128, 2, DI], BF16))
            hqT = s3.enter_context(nc.sbuf_tensor("hqT", [128, DI], BF16))
            outb = s3.enter_context(nc.sbuf_tensor("outb", [128, D], F32))
            p_g = s3.enter_context(nc.psum_tensor("p_g", [128, 1024], F32))
            p_u = s3.enter_context(nc.psum_tensor("p_u", [128, 1024], F32))
            p_d = s3.enter_context(nc.psum_tensor("p_d", [128, 1024], F32))
            p_ht = s3.enter_context(nc.psum_tensor("p_ht", [128, 1024], F32))

            tload_units = []
            for u, unit in enumerate(UNITS):
                slot = wqall[u].rearrange("(c p j) -> p c j", p=128, j=128)
                if unit[0] == "g":
                    dstu = wgT[:, :, unit[1] * 128:(unit[1] + 1) * 128]
                    ntr = 8
                elif unit[0] == "u":
                    dstu = wuT[:, :, unit[1] * 128:(unit[1] + 1) * 128]
                    ntr = 8
                else:
                    _, dcn, ssn = unit
                    ntr = 6 if ssn == 2 else 8
                    dstu = wdT[:, ssn * 8:ssn * 8 + ntr,
                               dcn * 128:(dcn + 1) * 128]
                tload_units.append(P.op(
                    "sp" if u % 2 == 0 else "gp",
                    lambda e, d=dstu, s2=slot, ntr=ntr: e.dma_start(
                        out=d, in_=s2[:, 0:ntr, :]), [prep_done[u]],
                    dma=True))
            twg_l = [tload_units[20], tload_units[21]]
            twu_l = [tload_units[42], tload_units[43]]
            twd_l = [tload_units[66], tload_units[67]]

            nchunks = [(jx * 512, min(512, DI - jx * 512))
                       for jx in range((DI + 511) // 512)]
            xq3_free = [[], []]
            xt_free3 = [[], []]
            ao_in_free = [[], []]
            pg_free = [[], []]
            pu_free = [[], []]
            hslot_free = [[], []]
            hqb_war = [[], []]
            hqT_free = []
            pht_free = [[], []]
            pd_free = []
            out_free = []
            saved3 = {}

            ld3 = {}

            def emit_p3_loads(i):
                sl = i % 2
                t1_ = P.op("sp", lambda e, i=i, sl=sl: e.dma_start(
                    out=xqT3[:, sl],
                    in_=xqt_s[i].rearrange("c p j -> p c j")),
                    xq3_free[sl], dma=True)
                xq3_free[sl] = []
                t2_ = P.op("sp", lambda e, i=i, sl=sl: e.dma_start(
                    out=aoin[:, sl], in_=ao_s[i]), ao_in_free[sl],
                    dma=True)
                ao_in_free[sl] = []
                t3_ = P.op("sp", lambda e, i=i, sl=sl: e.dma_start(
                    out=xts(sl), in_=x_d[i * 128:(i + 1) * 128, :]),
                    xt_free3[sl], dma=True)
                xt_free3[sl] = []
                ld3[i] = (t1_, t2_, t3_)

            emit_p3_loads(0)
            for it in range(NT + 1):
                if it < NT:
                    i = it
                    sl = i % 2
                    tqld, taold, txld = ld3.pop(i)
                    tqh = []
                    tu = None
                    for ci, (n0, w) in enumerate(nchunks):
                        par = ci % 2
                        hs = hs2[:, par, 0:w]
                        pg = p_g[:, par * 512:par * 512 + w]
                        pu = p_u[:, par * 512:par * 512 + w]
                        tg = None
                        for di in range(ND):
                            tg = P.op("pe", lambda e, di=di, n0=n0, w=w,
                                      sl=sl, pg=pg: e.matmul(
                                pg, xqT3[:, sl, di, :],
                                wgT[:, di, n0:n0 + w],
                                start=(di == 0), stop=(di == ND - 1)),
                                ([tqld] + twg_l + pg_free[par])
                                if di == 0 else [], inc=(di == ND - 1))
                        pg_free[par] = []
                        for di in range(ND):
                            tu = P.op("pe", lambda e, di=di, n0=n0, w=w,
                                      sl=sl, pu=pu: e.matmul(
                                pu, xqT3[:, sl, di, :],
                                wuT[:, di, n0:n0 + w],
                                start=(di == 0), stop=(di == ND - 1)),
                                (twu_l + pu_free[par]) if di == 0 else [],
                                inc=(di == ND - 1))
                        pu_free[par] = []
                        tsl_ = P.op("act", lambda e, hs=hs, pg=pg:
                                    e.activation(hs, pg, AF.Silu),
                                    [tg] + hslot_free[par])
                        hslot_free[par] = []
                        tmul = P.op("dve", lambda e, hs=hs, pu=pu:
                                    e.tensor_tensor(hs, hs, pu, ALU.mult),
                                    [tsl_, tu])
                        pg_free[par].append(tsl_)
                        pu_free[par].append(tmul)
                        tq, sfree = emit_quant(
                            hs, hqb[:, sl, n0:n0 + w], w, [tmul],
                            dst_war=list(hqb_war[sl]))
                        hslot_free[par] = sfree
                        tqh.append(tq)
                    hqb_war[sl] = []
                    xq3_free[sl].append(tu)
                    if "hq0" in taps and i == 0:
                        tp = tap("hq0", [128, DI], BF16)
                        P.op("sp", lambda e, tp=tp, sl=sl: e.dma_start(
                            out=tp, in_=hqb[:, sl]), tqh, dma=True)
                    saved3[i] = dict(tqh=tqh, taold=taold, txld=txld, sl=sl)

                if it >= 1:
                    j = it - 1
                    d = saved3.pop(j)
                    sj = d["sl"]
                    ngr3 = (NI + 3) // 4
                    htev = {}
                    dn_last = [None]

                    def emit_HT(jg):
                        j0 = jg * 4
                        j1 = min(NI, j0 + 4)
                        par = jg % 2
                        tt = None
                        for jj in range(j0, j1):
                            tt = P.op("pe", lambda e, jj=jj, j0=j0, par=par,
                                      sj=sj: e.matmul(
                                p_ht[:, par * 512 + (jj - j0) * 128:
                                     par * 512 + (jj - j0 + 1) * 128],
                                hqb[:, sj, jj * 128:(jj + 1) * 128],
                                ident[:], start=True, stop=True),
                                ([d["tqh"][jg]] + pht_free[par])
                                if jj == j0 else [], inc=(jj == j1 - 1))
                        pht_free[par] = []
                        if jg % 2 == 0:
                            tev = P.op("dve", lambda e, j0=j0, j1=j1, par=par:
                                       e.tensor_copy(
                                hqT[:, j0 * 128:j1 * 128],
                                p_ht[:, par * 512:
                                     par * 512 + (j1 - j0) * 128]),
                                [tt] + hqT_free)
                        else:
                            tev = P.op("act", lambda e, j0=j0, j1=j1, par=par:
                                       e.activation(
                                hqT[:, j0 * 128:j1 * 128],
                                p_ht[:, par * 512:
                                     par * 512 + (j1 - j0) * 128],
                                AF.Copy), [tt] + hqT_free)
                        pht_free[par].append(tev)
                        htev[jg] = tev

                    def emit_DN(jg):
                        j0 = jg * 4
                        j1 = min(NI, j0 + 4)
                        tev = htev.pop(jg)
                        tl = None
                        for half in range(2):
                            for ic in range(j0, j1):
                                tl = P.op("pe", lambda e, ic=ic, half=half:
                                          e.matmul(
                                    p_d[:, half * 512:(half + 1) * 512],
                                    hqT[:, ic * 128:(ic + 1) * 128],
                                    wdT[:, ic, half * 512:(half + 1) * 512],
                                    start=(ic == 0), stop=(ic == NI - 1)),
                                    ([tev] + twd_l + pd_free)
                                    if (ic == j0 and half == 0) else [],
                                    inc=(ic == j1 - 1 and half == 1))
                        if jg == 0:
                            pd_free.clear()
                        dn_last[0] = tl

                    emit_HT(0)
                    for jg in range(1, ngr3):
                        emit_HT(jg)
                        emit_DN(jg - 1)
                    emit_DN(ngr3 - 1)
                    tdm = dn_last[0]
                    hqb_war[sj] = [P.token("pe")]
                    hqT_free[:] = [tdm]
                    tc1 = P.op("dve", lambda e, sj=sj: e.scalar_tensor_tensor(
                        outb[:], p_d[:], 1.0, aoin[:, sj], ALU.mult, ALU.add),
                        [tdm, d["taold"]] + out_free)
                    out_free = []
                    tc2 = P.op("dve", lambda e, sj=sj: e.tensor_tensor(
                        outb[:], outb[:], xts(sj), ALU.add),
                        [d["txld"], tc1])
                    pd_free.append(tc1)
                    ao_in_free[sj].append(tc1)
                    xt_free3[sj].append(tc2)
                    tst = P.op("sp", lambda e, j=j: e.dma_start(
                        out=out_d[j * 128:(j + 1) * 128, :], in_=outb[:]),
                        [tc2], dma=True)
                    out_free.append(tst)
                if it + 1 < NT:
                    emit_p3_loads(it + 1)

            tfinal = P.token("sp.d")
            P.op("sp", lambda e: e.nop(),
                 [("pe", P.cnt["pe"]), ("dve", P.cnt["dve"]), tfinal],
                 inc=False)

        with nc.Block() as block:
            block.sync(P.replay("sp"))
            block.tensor(P.replay("pe"))
            block.vector(P.replay("dve"))
            block.scalar(P.replay("act"))
            block.gpsimd(P.replay("gp"))

    return nc, tapout


def _layout_inputs(L0_attn_x, L0_attn_Wq, L0_attn_bq, L0_attn_Wk, L0_attn_bk,
                   L0_attn_Wv, L0_attn_bv, L0_attn_Wo, L0_attn_bo,
                   L0_ffn_w_gate, L0_ffn_w_up, L0_ffn_w_down):
    x = np.asarray(L0_attn_x, dtype=np.float32)
    bq = np.asarray(L0_attn_bq, np.float32).reshape(8, 128).T
    bk = np.asarray(L0_attn_bk, np.float32).reshape(8, 128).T
    bqk = np.ascontiguousarray(np.concatenate([bq, bk], axis=1))
    common = {
        "wq": np.ascontiguousarray(np.asarray(L0_attn_Wq, np.float32)),
        "wk": np.ascontiguousarray(np.asarray(L0_attn_Wk, np.float32)),
        "wv": np.ascontiguousarray(np.asarray(L0_attn_Wv, np.float32)),
        "wo": np.ascontiguousarray(np.asarray(L0_attn_Wo, np.float32)),
        "bqk": bqk,
        "bv": np.ascontiguousarray(
            np.asarray(L0_attn_bv, np.float32).reshape(1, D)),
        "bo": np.ascontiguousarray(
            np.asarray(L0_attn_bo, np.float32).reshape(1, D)),
        "wg": np.ascontiguousarray(np.asarray(L0_ffn_w_gate, np.float32)),
        "wu": np.ascontiguousarray(np.asarray(L0_ffn_w_up, np.float32)),
        "wd": np.ascontiguousarray(np.asarray(L0_ffn_w_down, np.float32)),
    }
    in_maps = []
    for c in range(8):
        m = dict(common)
        m["x"] = np.ascontiguousarray(x[c])
        in_maps.append(m)
    return in_maps


def kernel(L0_attn_x, L0_attn_Wq, L0_attn_bq, L0_attn_Wk, L0_attn_bk,
           L0_attn_Wv, L0_attn_bv, L0_attn_Wo, L0_attn_bo,
           L0_ffn_w_gate, L0_ffn_w_up, L0_ffn_w_down):
    in_maps = _layout_inputs(L0_attn_x, L0_attn_Wq, L0_attn_bq, L0_attn_Wk,
                             L0_attn_bk, L0_attn_Wv, L0_attn_bv, L0_attn_Wo,
                             L0_attn_bo, L0_ffn_w_gate, L0_ffn_w_up,
                             L0_ffn_w_down)
    nc, _ = build_program()
    res = run_bass_kernel_spmd(nc, in_maps, core_ids=list(range(8)))
    return np.stack([np.asarray(r["out"], dtype=np.float32)
                     for r in res.results], axis=0)


# version 22 (final)
# speedup vs baseline: 1.2419x; 1.0064x over previous
import numpy as np
import ml_dtypes
from contextlib import ExitStack

import concourse.bass as bass
import concourse.mybir as mybir
from concourse.bass_utils import run_bass_kernel_spmd

F32 = mybir.dt.float32
BF16 = mybir.dt.bfloat16
AF = mybir.ActivationFunctionType
ALU = mybir.AluOpType
AX = mybir.AxisListType

S = 4096
D = 1024
DI = 2816
NT = S // 128
ND = D // 128
NI = DI // 128
G = 4
CRND = float(np.float32(1.5 * 2 ** 23))
UNITS = ([("g", n) for n in range(22)] + [("u", n) for n in range(22)] +
         [("d", dc, ss) for dc in range(8) for ss in range(3)])
NEGBIG = -320000.0
BF = ml_dtypes.bfloat16


class Prog:
    ENGS = ("sp", "pe", "dve", "act", "gp")

    def __init__(self, nc, ctx):
        self.nc = nc
        self.sems = {}
        for ek in self.ENGS:
            self.sems[ek] = ctx.enter_context(nc.semaphore("sem_" + ek))
        for ek in ("sp", "gp", "act", "dve"):
            self.sems[ek + ".d"] = ctx.enter_context(
                nc.semaphore("sem_" + ek + "d"))
        self.barsem = ctx.enter_context(nc.semaphore("sem_bar"))
        self.cnt = dict.fromkeys(self.sems, 0)
        self.barcnt = 0
        self.rec = {ek: [] for ek in self.ENGS}

    def op(self, ek, fn, waits=(), dma=False, inc=True, strict=False,
           drain=False):
        key = ek + ".d" if dma else ek
        ws = []
        for t in waits:
            if t is None:
                continue
            if t[0] == ek and not strict:
                continue
            ws.append(t)
        self.rec[ek].append((fn, ws, key, dma, inc, drain))
        if inc:
            self.cnt[key] += 16 if dma else 1
            return (key, self.cnt[key])
        return None

    def barrier(self):
        self.barcnt += len(self.ENGS)
        tgt = self.barcnt
        for ek in self.ENGS:
            self.rec[ek].append(("bar", tgt, None, None, None, None))

    def token(self, key):
        return (key, self.cnt[key])

    def replay(self, ek):
        sems = self.sems

        def go(eng):
            waited = {}
            for entry in self.rec[ek]:
                if entry[0] == "bar":
                    eng.sem_inc(self.barsem, 1)
                    eng.wait_ge(self.barsem, entry[1])
                    continue
                fn, ws, key, dmaf, inc, drn = entry
                for (sk, val) in ws:
                    if waited.get(sk, 0) < val:
                        eng.wait_ge(sems[sk], val)
                        waited[sk] = val
                ins = fn(eng)
                if drn:
                    ins = eng.drain()
                if inc:
                    ins.then_inc(sems[key], 16 if dmaf else 1)
        return go


def build_program(taps=()):
    nc = bass.Bass("TRN2", target_bir_lowering=False)
    taps = set(taps)

    x_d = nc.dram_tensor("x", [S, D], F32, kind="ExternalInput").ap()
    wq_d = nc.dram_tensor("wq", [D, D], F32, kind="ExternalInput").ap()
    wk_d = nc.dram_tensor("wk", [D, D], F32, kind="ExternalInput").ap()
    wv_d = nc.dram_tensor("wv", [D, D], F32, kind="ExternalInput").ap()
    wo_d = nc.dram_tensor("wo", [D, D], F32, kind="ExternalInput").ap()
    bqk_d = nc.dram_tensor("bqk", [128, 16], F32, kind="ExternalInput").ap()
    bv_d = nc.dram_tensor("bv", [1, D], F32, kind="ExternalInput").ap()
    bo_d = nc.dram_tensor("bo", [1, D], F32, kind="ExternalInput").ap()
    wg_d = nc.dram_tensor("wg", [DI, D], F32, kind="ExternalInput").ap()
    wu_d = nc.dram_tensor("wu", [DI, D], F32, kind="ExternalInput").ap()
    wd_d = nc.dram_tensor("wd", [D, DI], F32, kind="ExternalInput").ap()
    out_d = nc.dram_tensor("out", [S, D], F32, kind="ExternalOutput").ap()

    qt_s = nc.dram_tensor("qt_s", [NT, ND, 128, 128], BF16, kind="Internal").ap()
    xqt_s = nc.dram_tensor("xqt_s", [NT, ND, 128, 128], BF16, kind="Internal").ap()
    xnt_s = nc.dram_tensor("xnt_s", [NT, ND, 128, 128], BF16, kind="Internal").ap()
    ao_s = nc.dram_tensor("ao_s", [NT, 128, D], BF16, kind="Internal").ap()
    wqall = nc.dram_tensor("wqall", [68, 8 * 128 * 128], BF16,
                           kind="Internal").ap()

    ident_c = nc.inline_tensor(np.eye(128, dtype=BF), name="identc").ap()
    mrows = np.arange(128)[:, None]
    mcols = np.arange(128)[None, :]
    mask_np = np.where(mcols > mrows, 0.0, 1.0).astype(BF)
    mask_c = nc.inline_tensor(mask_np, name="maskc").ap()
    ones_c = nc.inline_tensor(np.ones((1, 128), dtype=BF), name="onesc").ap()

    tapout = {}

    def tap(name, shape, dt):
        tapout[name] = nc.dram_tensor("tap_" + name, shape, dt,
                                      kind="ExternalOutput").ap()
        return tapout[name]

    g = ExitStack()
    with g:
        P = Prog(nc, g)

        def sb(n, sh, dt):
            return g.enter_context(nc.sbuf_tensor(n, sh, dt))

        ident = sb("ident", [128, 128], BF16)
        mask = sb("mask", [128, 128], BF16)
        ones1 = sb("ones1", [1, 128], BF16)
        bqk = sb("bqk_t", [128, 16], F32)
        bvo = sb("bvo", [1, 2 * D], BF16)
        bvr = bvo[:, 0:D]
        bor = bvo[:, D:2 * D]
        xt = sb("xt", [128, 2 * D], F32)
        QW = 512
        qse = []
        for si in range(2):
            qse.append({
                "a": sb(f"q_a{si}", [128, QW], F32),
                "t1": sb(f"q_t1{si}", [128, QW], F32),
                "r1": sb(f"q_r1{si}", [128, QW], BF16),
                "hi": sb(f"q_hi{si}", [128, QW], BF16),
                "q2": sb(f"q_q2{si}", [128, QW], BF16),
                "sg": sb(f"q_sg{si}", [128, QW], BF16),
                "bmax": sb(f"q_bmax{si}", [128, 32], F32),
                "scl": sb(f"q_scl{si}", [128, 32], F32),
                "rcp": sb(f"q_rcp{si}", [128, 32], F32),
            })
        sm = sb("sm", [128, 16], F32)
        mx = sb("mx", [128, 8], F32)
        negmc = sb("negmc", [128, 8], F32)
        sums = sb("sums", [128, 8], F32)
        fe = sb("fe", [128, 8], F32)
        sums2 = sb("sums2", [128, 8], F32)
        kbias = sb("kbias", [128, 4], F32)

        def xts(sl):
            return xt[:, sl * D:(sl + 1) * D]

        t_ident = P.op("sp", lambda e: e.dma_start(out=ident[:], in_=ident_c),
                       dma=True)
        t_mask = P.op("sp", lambda e: e.dma_start(out=mask[:], in_=mask_c),
                      dma=True)
        t_ones = P.op("sp", lambda e: e.dma_start(out=ones1[:], in_=ones_c),
                      dma=True)
        t_bqk = P.op("sp", lambda e: e.dma_start(out=bqk[:], in_=bqk_d),
                     dma=True)
        t_bv = P.op("gp", lambda e: e.dma_start(out=bvr, in_=bv_d), dma=True)
        for j, val in enumerate((-5.0, -7.0, -10.0)):
            P.op("dve", lambda e, j=j, v=val: e.memset(kbias[:, j:j + 1], v),
                 drain=(j == 2))
        t_kb = P.token("dve")
        t_bo = P.op("gp", lambda e: e.dma_start(out=bor, in_=bo_d), dma=True)
        bqk32 = sb("bqk32", [128, 8], F32)
        t_b32 = P.op("dve", lambda e: e.tensor_scalar(
            bqk32[:], bqk[:, 0:8], 1.0 / 32.0, None, ALU.mult), [t_bqk],
            drain=True)

        last_q = [None, None]
        qctr = [0]

        def emit_quant_gen(src_ap, dst_ap, w, waits, dst_war, res, si):
            T = qse[si]
            nb = w // 16
            src3 = src_ap.rearrange("p (a b) -> p a b", b=16)
            s2b3 = T["scl"][:, 0:nb].unsqueeze(2).broadcast_to([128, nb, 16])
            rcb3 = T["rcp"][:, 0:nb].unsqueeze(2).broadcast_to([128, nb, 16])
            a2 = T["a"][:, 0:w]
            a3 = a2.rearrange("p (a b) -> p a b", b=16)
            w0 = list(waits) + ([last_q[si]] if last_q[si] else [])
            tA = P.op("act", lambda e: e.activation(a2, src_ap, AF.Abs), w0)
            yield
            tSG = P.op("act", lambda e: e.sign(T["sg"][:, 0:w], src_ap), w0)
            yield
            tR = P.op("dve", lambda e: e.tensor_reduce(
                T["bmax"][:, 0:nb], src3, AX.X, ALU.max,
                apply_absolute_value=True), w0)
            yield
            tsc = P.op("dve", lambda e: e.tensor_scalar(
                T["scl"][:, 0:nb], T["bmax"][:, 0:nb], 1.0 / 12.0, 5e-13,
                ALU.mult, ALU.max), [tR], strict=True)
            yield
            trc = P.op("dve", lambda e: e.reciprocal(
                T["rcp"][:, 0:nb], T["scl"][:, 0:nb]), [tsc], strict=True)
            yield
            tan = P.op("dve", lambda e: e.tensor_tensor(a3, a3, rcb3, ALU.mult),
                       [tA, trc], strict=True)
            yield
            tu1 = P.op("dve", lambda e: e.tensor_scalar(
                T["t1"][:, 0:w], a2, 4.0, CRND, ALU.min, ALU.add), [tan])
            yield
            tr1 = P.op("dve", lambda e: e.tensor_scalar(
                T["r1"][:, 0:w], T["t1"][:, 0:w], CRND - 4.0, None,
                ALU.subtract), [tu1])
            yield
            th1 = P.op("act", lambda e: e.sign(
                T["hi"][:, 0:w], a2, bias=kbias[:, 0:1]), [tan, t_kb])
            yield
            th2 = P.op("act", lambda e: e.sign(
                T["q2"][:, 0:w], a2, bias=kbias[:, 1:2]), [th1])
            yield
            tt1 = P.op("dve", lambda e: e.tensor_tensor(
                T["hi"][:, 0:w], T["hi"][:, 0:w], T["q2"][:, 0:w], ALU.add),
                [th1, th2])
            yield
            th3 = P.op("act", lambda e: e.sign(
                T["q2"][:, 0:w], a2, bias=kbias[:, 2:3]), [th2, tt1])
            yield
            tt2 = P.op("dve", lambda e: e.tensor_tensor(
                T["r1"][:, 0:w], T["r1"][:, 0:w], T["hi"][:, 0:w], ALU.add),
                [tt1, tr1])
            yield
            tt3 = P.op("dve", lambda e: e.tensor_scalar(
                T["q2"][:, 0:w], T["q2"][:, 0:w], 2.0, None, ALU.mult), [th3])
            yield
            tt4 = P.op("dve", lambda e: e.tensor_tensor(
                T["r1"][:, 0:w], T["r1"][:, 0:w], T["q2"][:, 0:w], ALU.add),
                [tt3, tt2])
            yield
            q13 = T["r1"][:, 0:w].rearrange("p (a b) -> p a b", b=16)
            t13 = T["t1"][:, 0:w].rearrange("p (a b) -> p a b", b=16)
            tqs = P.op("dve", lambda e: e.tensor_tensor(
                t13, q13, s2b3, ALU.mult), [tt4, tsc], strict=True)
            yield
            tdst = P.op("gp", lambda e: e.tensor_tensor(
                dst_ap, T["t1"][:, 0:w], T["sg"][:, 0:w], ALU.mult),
                [tqs, tSG] + list(dst_war))
            last_q[si] = tdst
            res["tdst"] = tdst
            res["sf"] = [tA, tSG, tR]

        def emit_quant_multi(src_ap, dst_ap, w, waits, dst_war=()):
            gens = []
            outs = []
            o = 0
            while o < w:
                ww = min(512, w - o)
                si = qctr[0] % 2
                qctr[0] += 1
                res = {}
                gens.append(emit_quant_gen(
                    src_ap[:, o:o + ww], dst_ap[:, o:o + ww], ww, waits,
                    dst_war, res, si))
                outs.append(res)
                o += ww
            for gn in gens:
                for _ in gn:
                    pass
            toks = [r["tdst"] for r in outs]
            sfree = [t for r in outs for t in r["sf"]]
            return toks, sfree

        def emit_quant(src_ap, dst_ap, w, waits, dst_war=()):
            toks, sfree = emit_quant_multi(src_ap, dst_ap, w, waits, dst_war)
            return toks[-1], sfree

        s12a = ExitStack()
        kT = s12a.enter_context(nc.sbuf_tensor("kT", [128, ND, S], BF16))

        with ExitStack() as s1:
            wq = s1.enter_context(nc.sbuf_tensor("wq_t", [128, ND, D], BF16))
            wk = s1.enter_context(nc.sbuf_tensor("wk_t", [128, ND, D], BF16))
            xn_f = s1.enter_context(nc.sbuf_tensor("xn_f", [128, 2, D], F32))
            xn_b = s1.enter_context(nc.sbuf_tensor("xn_b", [128, D], BF16))
            xq_b = s1.enter_context(nc.sbuf_tensor("xq_b", [128, D], BF16))
            xnT = s1.enter_context(
                nc.sbuf_tensor("xnT", [128, ND, G * 128], BF16))
            qTb = s1.enter_context(
                nc.sbuf_tensor("qTb", [128, ND, G * 128], BF16))
            xqT = s1.enter_context(nc.sbuf_tensor("xqT", [128, ND * 128], BF16))
            p_tr = s1.enter_context(nc.psum_tensor("p_tr", [128, 2048], F32))
            p_q = s1.enter_context(nc.psum_tensor("p_q", [128, 1024], F32))
            p_k = s1.enter_context(nc.psum_tensor("p_k", [128, 1024], F32))

            t_wq = P.op("gp", lambda e: e.dma_start(
                out=wq[:], in_=wq_d.rearrange("(k p) n -> p k n", p=128)),
                dma=True)
            t_wk = P.op("gp", lambda e: e.dma_start(
                out=wk[:], in_=wk_d.rearrange("(k p) n -> p k n", p=128)),
                dma=True)

            xt_free = [[], []]
            xnT_free = [[] for _ in range(G)]
            xqT_free = []
            qTb_free = []
            kq_psum_free = [[], []]
            tr_free_xn = []
            tr_free_xq = []
            prev_ttr = None
            prev_ttq = None
            xnf_free = [[], []]
            tvar_prev = None
            tx_tok = {}

            def p1_load(i):
                sl = i % 2
                t = P.op("sp", lambda e, i=i, sl=sl: e.dma_start(
                    out=xts(sl), in_=x_d[i * 128:(i + 1) * 128, :]),
                    xt_free[sl], dma=True)
                xt_free[sl] = []
                tx_tok[i] = t

            p1_load(0)
            for i in range(NT):
                b, gi = divmod(i, G)
                sl = i % 2
                xs = xts(sl)
                xnf = xn_f[:, sl]
                tx = tx_tok.pop(i)
                tsum = P.op("dve", lambda e, xs=xs: e.reduce_sum(
                    sm[:, 0:1], xs, AX.X), [tx])
                tssq = P.op("act", lambda e, xs=xs, xnf=xnf: e.activation(
                    xnf, xs, AF.Square, accum_out=sm[:, 1:2]),
                    [tx, tvar_prev] + xnf_free[sl])
                xnf_free[sl] = []
                tmean = P.op("dve", lambda e: e.tensor_scalar(
                    sm[:, 2:3], sm[:, 0:1], 1.0 / D, None, ALU.mult), [tsum],
                    strict=True)
                tm2 = P.op("dve", lambda e: e.tensor_scalar(
                    sm[:, 10:11], sm[:, 2:3], sm[:, 2:3], -1e-5,
                    ALU.mult, ALU.add), [tmean], strict=True)
                tvar = P.op("dve", lambda e: e.scalar_tensor_tensor(
                    sm[:, 3:4], sm[:, 1:2], 1.0 / D, sm[:, 10:11],
                    ALU.mult, ALU.subtract), [tssq, tm2], strict=True)
                tvar_prev = tvar
                tsd = P.op("act", lambda e: e.activation(
                    sm[:, 4:5], sm[:, 3:4], AF.Ln), [tvar])
                trstd = P.op("act", lambda e: e.activation(
                    sm[:, 5:6], sm[:, 4:5], AF.Exp, scale=-0.5), [tsd],
                    strict=True, drain=True)
                txn = P.op("dve", lambda e, xs=xs, xnf=xnf: e.tensor_scalar(
                    xnf, xs, sm[:, 2:3], sm[:, 5:6],
                    ALU.subtract, ALU.mult), [trstd])
                xt_free[sl].append(tssq)
                xt_free[sl].append(txn)
                if i + 1 < NT:
                    p1_load(i + 1)
                txnb = P.op("act", lambda e, xnf=xnf: e.activation(
                    xn_b[:], xnf, AF.Copy), [txn, prev_ttr])
                txq_l, txq_sf = emit_quant_multi(
                    xnf, xq_b[:], D, [txn, txnb],
                    dst_war=[prev_ttq] if prev_ttq else ())
                xnf_free[sl] = list(txq_sf) + [txnb]
                txq = txq_l[-1]
                if "xn0" in taps and i == 0:
                    tp = tap("xn0", [128, D], F32)
                    P.op("sp", lambda e, tp=tp, xnf=xnf: e.dma_start(
                        out=tp, in_=xnf), [txn, txq], dma=True)
                if "xq0" in taps and i == 0:
                    tp = tap("xq0", [128, D], BF16)
                    P.op("sp", lambda e, tp=tp: e.dma_start(out=tp, in_=xq_b[:]),
                         [txq], dma=True)
                for dc in range(ND):
                    prev_ttr = P.op("pe", lambda e, dc=dc: e.matmul(
                        p_tr[:, dc * 128:(dc + 1) * 128],
                        xn_b[:, dc * 128:(dc + 1) * 128], ident[:],
                        start=True, stop=True),
                        ([txnb, t_ident] + tr_free_xn) if dc == 0 else [],
                        inc=(dc == ND - 1))
                tr_free_xn = []
                for dc in range(ND):
                    if dc == 0:
                        ws = [txq_l[0]] + tr_free_xq
                    elif dc == 4:
                        ws = [txq_l[1]]
                    else:
                        ws = []
                    prev_ttq = P.op("pe", lambda e, dc=dc: e.matmul(
                        p_tr[:, 1024 + dc * 128:1024 + (dc + 1) * 128],
                        xq_b[:, dc * 128:(dc + 1) * 128], ident[:],
                        start=True, stop=True),
                        ws, inc=(dc == ND - 1))
                tr_free_xq = []
                xnT_dst = xnT[:, :, gi * 128:(gi + 1) * 128]
                tevn = P.op("act", lambda e, d=xnT_dst: e.activation(
                    d, p_tr[:, 0:1024].rearrange("p (c j) -> p c j", j=128),
                    AF.Copy), [prev_ttr] + xnT_free[gi])
                xnT_free[gi] = []
                tr_free_xn.append(tevn)
                tevq = P.op("act", lambda e: e.activation(
                    xqT[:], p_tr[:, 1024:2048], AF.Copy),
                    [prev_ttq] + xqT_free)
                xqT_free = []
                tr_free_xq.append(tevq)
                tso = P.op("sp", lambda e, i=i: e.dma_start(
                    out=xqt_s[i].rearrange("c p j -> p c j"),
                    in_=xqT[:].rearrange("p (c j) -> p c j", j=128)),
                    [tevq], dma=True)
                xqT_free.append(tso)

                if gi == G - 1:
                    for gg in range(G):
                        tsn = P.op("sp", lambda e, b=b, gg=gg: e.dma_start(
                            out=xnt_s[b * G + gg].rearrange("c p j -> p c j"),
                            in_=xnT[:, :, gg * 128:(gg + 1) * 128]),
                            [tevn], dma=True)
                        xnT_free[gg].append(tsn)
                    for proj in range(2):
                        wmat = wq if proj == 0 else wk
                        twm = t_wq if proj == 0 else t_wk
                        pp = p_q if proj == 0 else p_k
                        for do in range(ND):
                            par = do % 2
                            pps = pp[:, par * 512:(par + 1) * 512]
                            tmm = None
                            for di in range(ND):
                                tmm = P.op("pe", lambda e, di=di, do=do,
                                           w=wmat, pp2=pps: e.matmul(
                                    pp2, w[:, di, do * 128:(do + 1) * 128],
                                    xnT[:, di, :],
                                    start=(di == 0), stop=(di == ND - 1)),
                                    ([tevn, twm] + kq_psum_free[par])
                                    if di == 0 else [], inc=(di == ND - 1))
                            kq_psum_free[par] = []
                            if proj == 0:
                                dst = qTb[:, do, :]
                                tev = P.op("act", lambda e, d=dst, pp2=pps,
                                           bc=do: e.activation(
                                    d, pp2, AF.Relu, bias=bqk32[:, bc:bc + 1],
                                    scale=1.0 / 32.0),
                                    [tmm, t_b32] + qTb_free)
                            else:
                                dst = kT[:, do, b * 512:(b + 1) * 512]
                                tev = P.op("act", lambda e, d=dst, pp2=pps,
                                           bc=8 + do: e.activation(
                                    d, pp2, AF.Relu, bias=bqk[:, bc:bc + 1]),
                                    [tmm, t_bqk])
                            if proj == 0:
                                qTb_free = []
                            kq_psum_free[par].append(tev)
                    tact_b = P.token("act")
                    for gg in range(G):
                        tqs_st = P.op("sp", lambda e, b=b, gg=gg: e.dma_start(
                            out=qt_s[b * G + gg].rearrange("c p j -> p c j"),
                            in_=qTb[:, :, gg * 128:(gg + 1) * 128]),
                            [tact_b], dma=True)
                        qTb_free.append(tqs_st)
                    tpe_batch = P.token("pe")
                    for gg in range(G):
                        xnT_free[gg].append(tpe_batch)

            if "kT" in taps:
                tp = tap("kT", [128, ND * S], BF16)
                P.op("sp", lambda e, tp=tp: e.dma_start(
                    out=tp, in_=kT[:].rearrange("p c s -> p (c s)")),
                    [P.token("act")], dma=True)

        P.barrier()

        s12b = ExitStack()
        vS = s12b.enter_context(nc.sbuf_tensor("vS", [128, NT * D], BF16))

        with ExitStack() as s1b:
            wv = s1b.enter_context(nc.sbuf_tensor("wv_t", [128, ND, D], BF16))
            xnT_in = s1b.enter_context(
                nc.sbuf_tensor("xnT_in", [128, 2, ND, 128], BF16))
            p_v = s1b.enter_context(nc.psum_tensor("p_v", [128, 2048], F32))

            t_wv = P.op("gp", lambda e: e.dma_start(
                out=wv[:], in_=wv_d.rearrange("(k p) n -> p k n", p=128)),
                dma=True)
            xnTin_free = [[], []]
            pv_free = [[], []]
            xnb_tok = {}

            def p1b_load(i):
                sl = i % 2
                t = P.op("sp", lambda e, i=i, sl=sl: e.dma_start(
                    out=xnT_in[:, sl],
                    in_=xnt_s[i].rearrange("c p j -> p c j")),
                    xnTin_free[sl], dma=True)
                xnTin_free[sl] = []
                xnb_tok[i] = t

            p1b_load(0)
            for i in range(NT):
                sl = i % 2
                tld = xnb_tok.pop(i)
                if i + 1 < NT:
                    p1b_load(i + 1)
                pvh = p_v[:, sl * 1024:(sl + 1) * 1024]
                tmm = None
                for half in range(2):
                    ph = pvh[:, half * 512:(half + 1) * 512]
                    for di in range(ND):
                        P.op("pe", lambda e, di=di, sl=sl, half=half, ph=ph:
                             e.matmul(
                            ph, xnT_in[:, sl, di, :],
                            wv[:, di, half * 512:(half + 1) * 512],
                            start=(di == 0), stop=False),
                            ([tld, t_wv, t_ones, t_bv] + pv_free[sl])
                            if (di == 0 and half == 0) else [], inc=False)
                    tmm = P.op("pe", lambda e, half=half, ph=ph: e.matmul(
                        ph, ones1[:], bvr[:, half * 512:(half + 1) * 512],
                        start=False, stop=True), [])
                pv_free[sl] = []
                tev = P.op("act", lambda e, i=i, pv2=pvh: e.activation(
                    vS[:, i * D:(i + 1) * D], pv2, AF.Relu), [tmm])
                pv_free[sl].append(tev)
                xnTin_free[sl].append(tev)

            if "V" in taps:
                tp = tap("V", [128, NT * D], BF16)
                P.op("sp", lambda e, tp=tp: e.dma_start(out=tp, in_=vS[:]),
                     [P.token("act")], dma=True)

        P.barrier()

        with ExitStack() as s2:
            wo = s2.enter_context(nc.sbuf_tensor("wo_t", [128, ND, D], BF16))
            qT = s2.enter_context(
                nc.sbuf_tensor("qT_t", [128, 2, ND, 128], BF16))
            Psb = s2.enter_context(nc.sbuf_tensor("Psb", [128, 2 * S], BF16))
            PTsb = s2.enter_context(nc.sbuf_tensor("PTsb", [128, 2, 512], BF16))
            yb = s2.enter_context(nc.sbuf_tensor("yb", [128, D], BF16))
            yT = s2.enter_context(nc.sbuf_tensor("yT", [128, D], BF16))
            aosb = s2.enter_context(nc.sbuf_tensor("aosb", [128, D], BF16))
            wqnat = s2.enter_context(nc.sbuf_tensor("wqnat", [128, 2, D], BF16))
            wbounce = s2.enter_context(nc.sbuf_tensor("wbounce", [128, D], BF16))
            p_s = s2.enter_context(nc.psum_tensor("p_s", [128, 1024], F32))
            p_y = s2.enter_context(nc.psum_tensor("p_y", [128, 1024], F32))
            p_pt = s2.enter_context(nc.psum_tensor("p_pt", [128, 1024], F32))
            p_ao = s2.enter_context(nc.psum_tensor("p_ao", [128, 512], F32))
            p_wp = s2.enter_context(nc.psum_tensor("p_wp", [128, 512], F32))

            t_wo = P.op("gp", lambda e: e.dma_start(
                out=wo[:], in_=wo_d.rearrange("(k p) n -> p k n", p=128)),
                dma=True)

            unit_idx = [0]
            prep_xt_free = [[], []]
            prep_psum_free = []
            prep_bounce_free = []
            prep_wq_war = [[] for _ in range(2)]
            prep_done = []
            prep_q = []

            def emit_prep_quant():
                k = unit_idx[0]
                if k >= len(UNITS):
                    return
                unit_idx[0] += 1
                unit = UNITS[k]
                if unit[0] == "g":
                    src_ap = wg_d[unit[1] * 128:(unit[1] + 1) * 128, :]
                    wsrc = D
                elif unit[0] == "u":
                    src_ap = wu_d[unit[1] * 128:(unit[1] + 1) * 128, :]
                    wsrc = D
                else:
                    _, dcn, ssn = unit
                    wsrc = 768 if ssn == 2 else 1024
                    src_ap = wd_d[dcn * 128:(dcn + 1) * 128,
                                  ssn * 1024:ssn * 1024 + wsrc]
                sl = k % 2
                buf = k % 2
                stg = xts(sl)
                tld = P.op("sp", lambda e, d=stg[:, 0:wsrc], s=src_ap:
                           e.dma_start(out=d, in_=s),
                           prep_xt_free[sl], dma=True)
                prep_xt_free[sl] = []
                tq_l, tq_sf = emit_quant_multi(stg[:, 0:wsrc],
                                               wqnat[:, buf, 0:wsrc], wsrc,
                                               [tld],
                                               dst_war=list(prep_wq_war[buf]))
                prep_wq_war[buf] = []
                prep_xt_free[sl].extend(tq_sf)
                prep_q.append(dict(k=k, buf=buf, tq=tq_l, wsrc=wsrc))

            def emit_prep_store():
                if not prep_q:
                    return
                d = prep_q.pop(0)
                k, buf, wsrc = d["k"], d["buf"], d["wsrc"]
                ntr = wsrc // 128
                tev_last = None
                for grp in range((ntr + 3) // 4):
                    j0 = grp * 4
                    j1 = min(ntr, j0 + 4)
                    tt = None
                    for j in range(j0, j1):
                        tt = P.op("pe", lambda e, j=j, j0=j0, buf=buf:
                                  e.matmul(
                            p_wp[:, (j - j0) * 128:(j - j0 + 1) * 128],
                            wqnat[:, buf, j * 128:(j + 1) * 128], ident[:],
                            start=True, stop=True),
                            (d["tq"] + prep_psum_free) if j == j0 else [],
                            inc=(j == j1 - 1))
                    prep_psum_free.clear()
                    tev_last = P.op("act", lambda e, j0=j0, j1=j1:
                                    e.activation(
                        wbounce[:, j0 * 128:j1 * 128],
                        p_wp[:, 0:(j1 - j0) * 128], AF.Copy),
                        [tt] + prep_bounce_free)
                    prep_bounce_free.clear()
                    prep_psum_free.append(tev_last)
                prep_wq_war[buf].append(P.token("pe"))
                tst = P.op("sp", lambda e, k=k, ntr=ntr: e.dma_start(
                    out=wqall[k].rearrange("(c p j) -> p c j", p=128,
                                           j=128)[:, 0:ntr, :],
                    in_=wbounce[:, 0:wsrc].rearrange("p (c j) -> p c j",
                                                     j=128)),
                    [tev_last], dma=True)
                prep_bounce_free.append(tst)
                prep_done.append(tst)

            qT_free = [[], []]
            ps_free = [[], []]
            psb_war = {}
            PT_free = [[], []]
            ppt_free = [[], []]
            py_free = []
            yb_free = []
            yT_free = []
            ao_free = []
            trs_prev = None
            saved = {}
            qld_tok = {}

            def p2_qload(i):
                si = i % 2
                t = P.op("sp", lambda e, i=i, si=si: e.dma_start(
                    out=qT[:, si],
                    in_=qt_s[i].rearrange("c p j -> p c j")),
                    qT_free[si], dma=True)
                qT_free[si] = []
                qld_tok[i] = t

            p2_qload(0)
            for it in range(NT + 1):
                if it < NT:
                    i = it
                    L = 128 * (i + 1)
                    nkc = (L + 511) // 512
                    si = i % 2
                    off = si * S
                    rscol = 9 + 3 * si
                    tqld = qld_tok.pop(i)
                    if i + 1 < NT:
                        p2_qload(i + 1)
                    texp = {}
                    tmm = None
                    for kc in range(nkc):
                        w = min(512, L - kc * 512)
                        par = kc % 2
                        psl = p_s[:, par * 512:par * 512 + w]
                        for dc in range(ND):
                            tmm = P.op("pe", lambda e, dc=dc, kc=kc, w=w,
                                       psl=psl, si=si: e.matmul(
                                psl, qT[:, si, dc, :],
                                kT[:, dc, kc * 512:kc * 512 + w],
                                start=(dc == 0), stop=(dc == ND - 1)),
                                ([tqld] + ps_free[par]) if dc == 0 else [],
                                inc=(dc == ND - 1))
                        ps_free[par] = []
                        tmx = P.op("dve", lambda e, kc=kc, psl=psl:
                                   e.reduce_max(
                            negmc[:, kc:kc + 1], psl, AX.X, negate=True),
                            [tmm], drain=True)
                        tex = P.op("act", lambda e, kc=kc, w=w, psl=psl,
                                   off=off: e.activation(
                            Psb[:, off + kc * 512:off + kc * 512 + w], psl,
                            AF.Exp, bias=negmc[:, kc:kc + 1], scale=1.0,
                            accum_out=sums[:, kc:kc + 1]),
                            [tmx, trs_prev] + psb_war.pop((si, kc), []))
                        if kc == nkc - 1:
                            tz = P.op("dve", lambda e, kc=kc, w=w, off=off:
                                      e.tensor_tensor(
                                Psb[:, off + kc * 512 + w - 128:
                                    off + kc * 512 + w],
                                Psb[:, off + kc * 512 + w - 128:
                                    off + kc * 512 + w],
                                mask[:], ALU.mult), [tex, t_mask])
                            tex = P.op("dve", lambda e, kc=kc, w=w, off=off:
                                       e.reduce_sum(
                                sums[:, kc:kc + 1],
                                Psb[:, off + kc * 512:off + kc * 512 + w],
                                AX.X), [tz], strict=True)
                        texp[kc] = tex
                        ps_free[par].append(tex)
                    qT_free[si] = [tmm]
                    tm1 = P.op("dve", lambda e, nkc=nkc: e.tensor_reduce(
                        sm[:, 7:8], negmc[:, 0:nkc], AX.X, ALU.min),
                        [texp[nkc - 1]], strict=True)
                    tfe = P.op("dve", lambda e, nkc=nkc: e.tensor_scalar(
                        fe[:, 0:nkc], negmc[:, 0:nkc], sm[:, 7:8], -1.0,
                        ALU.subtract, ALU.mult), [tm1], strict=True)
                    tfee = P.op("act", lambda e, nkc=nkc: e.activation(
                        fe[:, 0:nkc], fe[:, 0:nkc], AF.Exp), [tfe],
                        drain=True)
                    tsm2 = P.op("dve", lambda e, nkc=nkc: e.tensor_tensor(
                        sums2[:, 0:nkc], sums[:, 0:nkc], fe[:, 0:nkc],
                        ALU.mult), [tfee], strict=True)
                    tss = P.op("dve", lambda e, nkc=nkc: e.reduce_sum(
                        sm[:, 8:9], sums2[:, 0:nkc], AX.X), [tsm2],
                        strict=True)
                    trs = P.op("dve", lambda e, rscol=rscol: e.reciprocal(
                        sm[:, rscol:rscol + 1], sm[:, 8:9]), [tss],
                        strict=True)
                    trs_prev = trs
                    tresc = {}
                    for kc in range(nkc):
                        w = min(512, L - kc * 512)
                        tresc[kc] = P.op("dve", lambda e, kc=kc, w=w, off=off:
                                         e.tensor_scalar(
                            Psb[:, off + kc * 512:off + kc * 512 + w],
                            Psb[:, off + kc * 512:off + kc * 512 + w],
                            fe[:, kc:kc + 1], None, ALU.mult),
                            [tfee, texp[kc]])
                    if "P1" in taps and i == 1:
                        tp = tap("P1", [128, S], BF16)
                        P.op("sp", lambda e, tp=tp, off=off: e.dma_start(
                            out=tp, in_=Psb[:, off:off + S]),
                            [tresc[kc] for kc in range(nkc)], dma=True)
                    saved[i] = dict(nkc=nkc, nj=L // 128, si=si, off=off,
                                    rscol=rscol, trs=trs, tresc=tresc)

                if it >= 1:
                    j = it - 1
                    d = saved.pop(j)
                    sj, offj, nj = d["si"], d["off"], d["nj"]
                    ngr = (nj + 3) // 4
                    ptev = {}
                    pv_last = [None]

                    def emit_T2(jg):
                        j0 = jg * 4
                        j1 = min(nj, j0 + 4)
                        par = jg % 2
                        tt = None
                        for jj in range(j0, j1):
                            tt = P.op("pe", lambda e, jj=jj, j0=j0, par=par,
                                      offj=offj: e.matmul(
                                p_pt[:, par * 512 + (jj - j0) * 128:
                                     par * 512 + (jj - j0 + 1) * 128],
                                Psb[:, offj + jj * 128:offj + (jj + 1) * 128],
                                ident[:], start=True, stop=True),
                                ([d["tresc"][jg]] + ppt_free[par])
                                if jj == j0 else [], inc=(jj == j1 - 1))
                        ppt_free[par] = []
                        if jg % 2 == 0:
                            tptev = P.op("dve", lambda e, j0=j0, j1=j1,
                                         par=par: e.tensor_copy(
                                PTsb[:, par, 0:(j1 - j0) * 128],
                                p_pt[:, par * 512:
                                     par * 512 + (j1 - j0) * 128]),
                                [tt] + PT_free[par])
                        else:
                            tptev = P.op("act", lambda e, j0=j0, j1=j1,
                                         par=par: e.activation(
                                PTsb[:, par, 0:(j1 - j0) * 128],
                                p_pt[:, par * 512:
                                     par * 512 + (j1 - j0) * 128],
                                AF.Copy), [tt] + PT_free[par])
                        PT_free[par] = []
                        ppt_free[par].append(tptev)
                        ptev[jg] = tptev

                    def emit_PV2(jg):
                        j0 = jg * 4
                        j1 = min(nj, j0 + 4)
                        par = jg % 2
                        tptev = ptev.pop(jg)
                        tpl = None
                        for jj in range(j0, j1):
                            for half in range(2):
                                tpl = P.op("pe", lambda e, jj=jj, j0=j0,
                                           par=par, half=half, nj=nj:
                                           e.matmul(
                                    p_y[:, half * 512:(half + 1) * 512],
                                    PTsb[:, par,
                                         (jj - j0) * 128:(jj - j0 + 1) * 128],
                                    vS[:, jj * D + half * 512:
                                       jj * D + (half + 1) * 512],
                                    start=(jj == 0), stop=(jj == nj - 1)),
                                    [tptev] + (py_free if (jj == 0 and
                                                           half == 0) else []),
                                    inc=(jj == j1 - 1 and half == 1))
                        if jg == 0:
                            py_free.clear()
                        PT_free[par].append(tpl)
                        pv_last[0] = tpl

                    emit_T2(0)
                    for jg in range(1, ngr):
                        emit_T2(jg)
                        emit_PV2(jg - 1)
                    emit_PV2(ngr - 1)
                    tplast = pv_last[0]
                    for kc in range(d["nkc"]):
                        psb_war[(sj, kc)] = [tplast]
                    tyb = P.op("dve", lambda e, rc=d["rscol"]: e.tensor_scalar(
                        yb[:], p_y[:], sm[:, rc:rc + 1], None, ALU.mult),
                        [tplast, d["trs"]] + yb_free, strict=True)
                    yb_free = []
                    py_free = [tyb]
                    if "y1" in taps and j == 1:
                        tp = tap("y1", [128, D], BF16)
                        P.op("sp", lambda e, tp=tp: e.dma_start(
                            out=tp, in_=yb[:]), [tyb], dma=True)
                    tytev = []
                    for grp in range(2):
                        par = grp
                        tt = None
                        for jj in range(4):
                            dc = grp * 4 + jj
                            tt = P.op("pe", lambda e, dc=dc, jj=jj, par=par:
                                      e.matmul(
                                p_pt[:, par * 512 + jj * 128:
                                     par * 512 + (jj + 1) * 128],
                                yb[:, dc * 128:(dc + 1) * 128], ident[:],
                                start=True, stop=True),
                                ([tyb] + ppt_free[par]) if jj == 0 else [],
                                inc=(jj == 3))
                        ppt_free[par] = []
                        tev = P.op("act", lambda e, grp=grp, par=par:
                                   e.activation(
                            yT[:, grp * 512:(grp + 1) * 512],
                            p_pt[:, par * 512:(par + 1) * 512], AF.Copy),
                            [tt] + yT_free)
                        yT_free = []
                        ppt_free[par].append(tev)
                        tytev.append(tev)
                    yb_free = [P.token("pe")]
                    tao_ev = []
                    for half in range(2):
                        for dc in range(ND):
                            P.op("pe", lambda e, dc=dc, half=half: e.matmul(
                                p_ao[:], yT[:, dc * 128:(dc + 1) * 128],
                                wo[:, dc, half * 512:(half + 1) * 512],
                                start=(dc == 0), stop=False),
                                ([t_wo, t_bo] + tytev + ao_free)
                                if dc == 0 else [], inc=False)
                        tmm2 = P.op("pe", lambda e, half=half: e.matmul(
                            p_ao[:], ones1[:],
                            bor[:, half * 512:(half + 1) * 512],
                            start=False, stop=True), [])
                        ao_free = []
                        tev = P.op("act", lambda e, half=half: e.activation(
                            aosb[:, half * 512:(half + 1) * 512], p_ao[:],
                            AF.Relu), [tmm2])
                        ao_free.append(tev)
                        tao_ev.append(tev)
                    yT_free = [P.token("pe")]
                    tao_st = P.op("sp", lambda e, j=j: e.dma_start(
                        out=ao_s[j], in_=aosb[:]), tao_ev, dma=True)
                    ao_free.append(tao_st)
                    if "ao1" in taps and j == 1:
                        tp = tap("ao1", [128, D], BF16)
                        P.op("sp", lambda e, tp=tp: e.dma_start(
                            out=tp, in_=aosb[:]), tao_ev, dma=True)
                if it >= 2:
                    emit_prep_store()
                    emit_prep_store()
                if it >= 1:
                    emit_prep_quant()
                    emit_prep_quant()
            while unit_idx[0] < len(UNITS):
                emit_prep_store()
                emit_prep_quant()
            while prep_q:
                emit_prep_store()
            tcoll = P.token("sp.d")

        P.barrier()
        s12b.close()
        s12a.close()
        P.barrier()

        with ExitStack() as s3:
            wgT = s3.enter_context(nc.sbuf_tensor("wgT", [128, ND, DI], BF16))
            wuT = s3.enter_context(nc.sbuf_tensor("wuT", [128, ND, DI], BF16))
            wdT = s3.enter_context(nc.sbuf_tensor("wdT", [128, NI, D], BF16))
            xqT3 = s3.enter_context(
                nc.sbuf_tensor("xqT3", [128, 2, ND, 128], BF16))
            aoin = s3.enter_context(nc.sbuf_tensor("aoin", [128, 2, D], BF16))
            hs2 = s3.enter_context(nc.sbuf_tensor("hs2", [128, 2, 512], F32))
            hqb = s3.enter_context(nc.sbuf_tensor("hqb", [128, 2, DI], BF16))
            hqT = s3.enter_context(nc.sbuf_tensor("hqT", [128, DI], BF16))
            outb = s3.enter_context(nc.sbuf_tensor("outb", [128, D], F32))
            p_g = s3.enter_context(nc.psum_tensor("p_g", [128, 1024], F32))
            p_u = s3.enter_context(nc.psum_tensor("p_u", [128, 1024], F32))
            p_d = s3.enter_context(nc.psum_tensor("p_d", [128, 1024], F32))
            p_ht = s3.enter_context(nc.psum_tensor("p_ht", [128, 1024], F32))

            tload_units = []
            for u, unit in enumerate(UNITS):
                slot = wqall[u].rearrange("(c p j) -> p c j", p=128, j=128)
                if unit[0] == "g":
                    dstu = wgT[:, :, unit[1] * 128:(unit[1] + 1) * 128]
                    ntr = 8
                elif unit[0] == "u":
                    dstu = wuT[:, :, unit[1] * 128:(unit[1] + 1) * 128]
                    ntr = 8
                else:
                    _, dcn, ssn = unit
                    ntr = 6 if ssn == 2 else 8
                    dstu = wdT[:, ssn * 8:ssn * 8 + ntr,
                               dcn * 128:(dcn + 1) * 128]
                tload_units.append(P.op(
                    ("sp", "gp", "act")[u % 3],
                    lambda e, d=dstu, s2=slot, ntr=ntr: e.dma_start(
                        out=d, in_=s2[:, 0:ntr, :]), [prep_done[u]],
                    dma=True))
            twg_l = tload_units[19:22]
            twu_l = tload_units[41:44]
            twd_l = tload_units[65:68]

            nchunks = [(jx * 512, min(512, DI - jx * 512))
                       for jx in range((DI + 511) // 512)]
            xq3_free = [[], []]
            xt_free3 = [[], []]
            ao_in_free = [[], []]
            pg_free = [[], []]
            pu_free = [[], []]
            hslot_free = [[], []]
            hqb_war = [[], []]
            hqT_free = []
            pht_free = [[], []]
            pd_free = []
            out_free = []
            saved3 = {}

            ld3 = {}

            def emit_p3_loads(i):
                sl = i % 2
                t1_ = P.op("sp", lambda e, i=i, sl=sl: e.dma_start(
                    out=xqT3[:, sl],
                    in_=xqt_s[i].rearrange("c p j -> p c j")),
                    xq3_free[sl], dma=True)
                xq3_free[sl] = []
                t2_ = P.op("sp", lambda e, i=i, sl=sl: e.dma_start(
                    out=aoin[:, sl], in_=ao_s[i]), ao_in_free[sl],
                    dma=True)
                ao_in_free[sl] = []
                t3_ = P.op("sp", lambda e, i=i, sl=sl: e.dma_start(
                    out=xts(sl), in_=x_d[i * 128:(i + 1) * 128, :]),
                    xt_free3[sl], dma=True)
                xt_free3[sl] = []
                ld3[i] = (t1_, t2_, t3_)

            emit_p3_loads(0)
            for it in range(NT + 1):
                if it < NT:
                    i = it
                    sl = i % 2
                    tqld, taold, txld = ld3.pop(i)
                    tqh = []
                    tu = None
                    for ci, (n0, w) in enumerate(nchunks):
                        par = ci % 2
                        hs = hs2[:, par, 0:w]
                        pg = p_g[:, par * 512:par * 512 + w]
                        pu = p_u[:, par * 512:par * 512 + w]
                        tg = None
                        for di in range(ND):
                            tg = P.op("pe", lambda e, di=di, n0=n0, w=w,
                                      sl=sl, pg=pg: e.matmul(
                                pg, xqT3[:, sl, di, :],
                                wgT[:, di, n0:n0 + w],
                                start=(di == 0), stop=(di == ND - 1)),
                                ([tqld] + twg_l + pg_free[par])
                                if di == 0 else [], inc=(di == ND - 1))
                        pg_free[par] = []
                        for di in range(ND):
                            tu = P.op("pe", lambda e, di=di, n0=n0, w=w,
                                      sl=sl, pu=pu: e.matmul(
                                pu, xqT3[:, sl, di, :],
                                wuT[:, di, n0:n0 + w],
                                start=(di == 0), stop=(di == ND - 1)),
                                (twu_l + pu_free[par]) if di == 0 else [],
                                inc=(di == ND - 1))
                        pu_free[par] = []
                        tsl_ = P.op("act", lambda e, hs=hs, pg=pg:
                                    e.activation(hs, pg, AF.Silu),
                                    [tg] + hslot_free[par])
                        hslot_free[par] = []
                        tmul = P.op("dve", lambda e, hs=hs, pu=pu:
                                    e.tensor_tensor(hs, hs, pu, ALU.mult),
                                    [tsl_, tu])
                        pg_free[par].append(tsl_)
                        pu_free[par].append(tmul)
                        tq, sfree = emit_quant(
                            hs, hqb[:, sl, n0:n0 + w], w, [tmul],
                            dst_war=list(hqb_war[sl]))
                        hslot_free[par] = sfree
                        tqh.append(tq)
                    hqb_war[sl] = []
                    xq3_free[sl].append(tu)
                    if "hq0" in taps and i == 0:
                        tp = tap("hq0", [128, DI], BF16)
                        P.op("sp", lambda e, tp=tp, sl=sl: e.dma_start(
                            out=tp, in_=hqb[:, sl]), tqh, dma=True)
                    saved3[i] = dict(tqh=tqh, taold=taold, txld=txld, sl=sl)

                if it >= 1:
                    j = it - 1
                    d = saved3.pop(j)
                    sj = d["sl"]
                    ngr3 = (NI + 3) // 4
                    htev = {}
                    dn_last = [None]

                    def emit_HT(jg):
                        j0 = jg * 4
                        j1 = min(NI, j0 + 4)
                        par = jg % 2
                        tt = None
                        for jj in range(j0, j1):
                            tt = P.op("pe", lambda e, jj=jj, j0=j0, par=par,
                                      sj=sj: e.matmul(
                                p_ht[:, par * 512 + (jj - j0) * 128:
                                     par * 512 + (jj - j0 + 1) * 128],
                                hqb[:, sj, jj * 128:(jj + 1) * 128],
                                ident[:], start=True, stop=True),
                                ([d["tqh"][jg]] + pht_free[par])
                                if jj == j0 else [], inc=(jj == j1 - 1))
                        pht_free[par] = []
                        if jg % 2 == 0:
                            tev = P.op("dve", lambda e, j0=j0, j1=j1, par=par:
                                       e.tensor_copy(
                                hqT[:, j0 * 128:j1 * 128],
                                p_ht[:, par * 512:
                                     par * 512 + (j1 - j0) * 128]),
                                [tt] + hqT_free)
                        else:
                            tev = P.op("act", lambda e, j0=j0, j1=j1, par=par:
                                       e.activation(
                                hqT[:, j0 * 128:j1 * 128],
                                p_ht[:, par * 512:
                                     par * 512 + (j1 - j0) * 128],
                                AF.Copy), [tt] + hqT_free)
                        pht_free[par].append(tev)
                        htev[jg] = tev

                    def emit_DN(jg):
                        j0 = jg * 4
                        j1 = min(NI, j0 + 4)
                        tev = htev.pop(jg)
                        tl = None
                        for half in range(2):
                            for ic in range(j0, j1):
                                tl = P.op("pe", lambda e, ic=ic, half=half:
                                          e.matmul(
                                    p_d[:, half * 512:(half + 1) * 512],
                                    hqT[:, ic * 128:(ic + 1) * 128],
                                    wdT[:, ic, half * 512:(half + 1) * 512],
                                    start=(ic == 0), stop=(ic == NI - 1)),
                                    ([tev] + twd_l + pd_free)
                                    if (ic == j0 and half == 0) else [],
                                    inc=(ic == j1 - 1 and half == 1))
                        if jg == 0:
                            pd_free.clear()
                        dn_last[0] = tl

                    emit_HT(0)
                    for jg in range(1, ngr3):
                        emit_HT(jg)
                        emit_DN(jg - 1)
                    emit_DN(ngr3 - 1)
                    tdm = dn_last[0]
                    hqb_war[sj] = [P.token("pe")]
                    hqT_free[:] = [tdm]
                    tc1 = P.op("dve", lambda e, sj=sj: e.scalar_tensor_tensor(
                        outb[:], p_d[:], 1.0, aoin[:, sj], ALU.mult, ALU.add),
                        [tdm, d["taold"]] + out_free)
                    out_free = []
                    tc2 = P.op("dve", lambda e, sj=sj: e.tensor_tensor(
                        outb[:], outb[:], xts(sj), ALU.add),
                        [d["txld"], tc1])
                    pd_free.append(tc1)
                    ao_in_free[sj].append(tc1)
                    xt_free3[sj].append(tc2)
                    tst = P.op("sp", lambda e, j=j: e.dma_start(
                        out=out_d[j * 128:(j + 1) * 128, :], in_=outb[:]),
                        [tc2], dma=True)
                    out_free.append(tst)
                if it + 1 < NT:
                    emit_p3_loads(it + 1)

            tfinal = P.token("sp.d")
            P.op("sp", lambda e: e.nop(),
                 [("pe", P.cnt["pe"]), ("dve", P.cnt["dve"]), tfinal],
                 inc=False)

        with nc.Block() as block:
            block.sync(P.replay("sp"))
            block.tensor(P.replay("pe"))
            block.vector(P.replay("dve"))
            block.scalar(P.replay("act"))
            block.gpsimd(P.replay("gp"))

    return nc, tapout


def _layout_inputs(L0_attn_x, L0_attn_Wq, L0_attn_bq, L0_attn_Wk, L0_attn_bk,
                   L0_attn_Wv, L0_attn_bv, L0_attn_Wo, L0_attn_bo,
                   L0_ffn_w_gate, L0_ffn_w_up, L0_ffn_w_down):
    x = np.asarray(L0_attn_x, dtype=np.float32)
    bq = np.asarray(L0_attn_bq, np.float32).reshape(8, 128).T
    bk = np.asarray(L0_attn_bk, np.float32).reshape(8, 128).T
    bqk = np.ascontiguousarray(np.concatenate([bq, bk], axis=1))
    common = {
        "wq": np.ascontiguousarray(np.asarray(L0_attn_Wq, np.float32)),
        "wk": np.ascontiguousarray(np.asarray(L0_attn_Wk, np.float32)),
        "wv": np.ascontiguousarray(np.asarray(L0_attn_Wv, np.float32)),
        "wo": np.ascontiguousarray(np.asarray(L0_attn_Wo, np.float32)),
        "bqk": bqk,
        "bv": np.ascontiguousarray(
            np.asarray(L0_attn_bv, np.float32).reshape(1, D)),
        "bo": np.ascontiguousarray(
            np.asarray(L0_attn_bo, np.float32).reshape(1, D)),
        "wg": np.ascontiguousarray(np.asarray(L0_ffn_w_gate, np.float32)),
        "wu": np.ascontiguousarray(np.asarray(L0_ffn_w_up, np.float32)),
        "wd": np.ascontiguousarray(np.asarray(L0_ffn_w_down, np.float32)),
    }
    in_maps = []
    for c in range(8):
        m = dict(common)
        m["x"] = np.ascontiguousarray(x[c])
        in_maps.append(m)
    return in_maps


def kernel(L0_attn_x, L0_attn_Wq, L0_attn_bq, L0_attn_Wk, L0_attn_bk,
           L0_attn_Wv, L0_attn_bv, L0_attn_Wo, L0_attn_bo,
           L0_ffn_w_gate, L0_ffn_w_up, L0_ffn_w_down):
    in_maps = _layout_inputs(L0_attn_x, L0_attn_Wq, L0_attn_bq, L0_attn_Wk,
                             L0_attn_bk, L0_attn_Wv, L0_attn_bv, L0_attn_Wo,
                             L0_attn_bo, L0_ffn_w_gate, L0_ffn_w_up,
                             L0_ffn_w_down)
    nc, _ = build_program()
    res = run_bass_kernel_spmd(nc, in_maps, core_ids=list(range(8)))
    return np.stack([np.asarray(r["out"], dtype=np.float32)
                     for r in res.results], axis=0)
